# version 02
# baseline (speedup 1.0000x reference)
import numpy as np
import ml_dtypes
import concourse.bass as bass
import concourse.mybir as mybir
from concourse.bass_utils import run_bass_kernel_spmd
from concourse.bass_core_rust import IndirectOffsetOnAxis
from contextlib import ExitStack

F32 = mybir.dt.float32
BF16 = mybir.dt.bfloat16
U32 = mybir.dt.uint32
ALU = mybir.AluOpType
ACT = mybir.ActivationFunctionType
AXX = mybir.AxisListType.X

NCORE = 8
T = 1024
D = 2048
NC = 16
NH = 16
S = 4096
SCALE = float(1.0 / np.sqrt(128.0))
NE = 8
DI = 1024

N_BLOCKS = 4

RACE_CHECK = False


class Sem:
    def __init__(self, nc, name):
        self.h = nc.alloc_semaphore(name)
        self.n = 0

    def inc(self, k):
        self.n += k
        return self.h


class Prog:
    ENGINES = ("sync", "gpsimd", "vector", "scalar", "tensor")

    def __init__(self, nc):
        self.nc = nc
        self.streams = {k: [] for k in self.ENGINES}
        self.last_done = {k: None for k in self.ENGINES}

    def emit(self, eng, fn, kind="op"):
        self.streams[eng].append(fn)
        if kind == "op":
            self.last_done[eng] = None

    def wait(self, eng, sem, val=None):
        v = sem.n if val is None else val
        h = sem.h
        if v > 0:
            self.emit(eng, lambda e, h=h, v=v: e.wait_ge(h, v), kind="wait")

    def mark(self, eng, sem, k=1):
        h = sem.inc(k)
        v = k
        if self.last_done[eng] is None:
            self.emit(eng, lambda e, h=h, v=v: e.last.then_inc(h, v), kind="mark")
            self.last_done[eng] = (h, sem.n)
        else:
            h0, v0 = self.last_done[eng]
            def f(e, h=h, v=v, h0=h0, v0=v0):
                e.wait_ge(h0, v0)
                e.nop().then_inc(h, v)
            self.emit(eng, f, kind="mark")
        return sem.n

    def ssync(self, eng):
        if self.last_done[eng] is not None:
            h, v = self.last_done[eng]
            self.emit(eng, lambda e, h=h, v=v: e.wait_ge(h, v), kind="wait")
        else:
            sem = self.chain[eng]
            self.mark(eng, sem)
            self.wait(eng, sem)

    def seq(self, eng, *fns):
        for i, f in enumerate(fns):
            if i:
                self.ssync(eng)
            self.emit(eng, f)


class EngRec:
    def __init__(self, e):
        object.__setattr__(self, "_e", e)
        object.__setattr__(self, "last", None)

    _NO_RECORD = ("wait_ge", "wait_op", "sem_inc", "nop", "maybe_wait_ge")

    def __getattr__(self, name):
        f = getattr(object.__getattribute__(self, "_e"), name)
        if not callable(f):
            return f
        rec = name not in EngRec._NO_RECORD
        def call(*a, **kw):
            r = f(*a, **kw)
            if rec and hasattr(r, "then_inc"):
                object.__setattr__(self, "last", r)
            return r
        return call


def build_program():
    nc = bass.Bass("TRN2", target_bir_lowering=False, debug=False,
                   detect_race_conditions=RACE_CHECK)

    def dram_in(name, shape, dt=F32):
        return nc.dram_tensor(name, list(shape), dt, kind="ExternalInput").ap()

    x0 = dram_in("x0", [128, NC, T])
    w_l0q = dram_in("w_l0q", [D, D]); w_l0k = dram_in("w_l0k", [D, D])
    w_l0v = dram_in("w_l0v", [D, D]); w_l0o = dram_in("w_l0o", [D, D])
    w_l0g = dram_in("w_l0g", [D, 16]); w_l0qkv = dram_in("w_l0qkv", [D, 3 * D])
    w_l1q = dram_in("w_l1q", [D, D]); w_l1k = dram_in("w_l1k", [D, D])
    w_l1v = dram_in("w_l1v", [D, D]); w_l1g = dram_in("w_l1g", [D, D])
    b_l1 = dram_in("b_l1", [128, 4, NC])
    w_rtr = dram_in("w_rtr", [D, NE])
    w_gu = dram_in("w_gu", [NE, D, 2 * DI])
    w_dn = dram_in("w_dn", [NE, DI, D])
    ident_bf_in = dram_in("ident_bf", [128, 128], BF16)
    ident_f_in = dram_in("ident_f", [128, 128])
    hotcols_in = dram_in("hotcols", [128, NH, NH], BF16)
    gsel_in = dram_in("gsel", [128, 16], U32)
    ysel_in = dram_in("ysel", [128, 16], U32)
    pmask_in = dram_in("pmask", [128, 4])
    iota16_in = dram_in("iota16", [128, 16])
    hotrows8_in = dram_in("hotrows8", [8, 8, 128], BF16)

    y_out = nc.dram_tensor("y", [128, NC, T], F32, kind="ExternalOutput").ap()

    k_loc = nc.dram_tensor("k_loc", [D, T], BF16, kind="Internal").ap()
    v_loc = nc.dram_tensor("v_loc", [T, D], BF16, kind="Internal").ap()
    k_gth = nc.dram_tensor("k_gth", [4, D, T], BF16, kind="Internal").ap()
    v_gth = nc.dram_tensor("v_gth", [4, T, D], BF16, kind="Internal").ap()
    q_loc = nc.dram_tensor("q_loc", [D, T], BF16, kind="Internal").ap()
    q_gth = nc.dram_tensor("q_gth", [4 * D, T], BF16, kind="Internal").ap()
    kv_loc = nc.dram_tensor("kv_loc", [T, 256], BF16, kind="Internal").ap()
    kv_gth = nc.dram_tensor("kv_gth", [4 * T, 256], BF16, kind="Internal").ap()
    yh_loc = nc.dram_tensor("yh_loc", [4, 512, T], BF16, kind="Internal").ap()
    yh_gth = nc.dram_tensor("yh_gth", [4, 4, 512, T], BF16, kind="Internal").ap()
    pfx_loc = nc.dram_tensor("pfx_loc", [128, NC], F32, kind="Internal").ap()
    pfx_gth = nc.dram_tensor("pfx_gth", [4, 128, NC], F32, kind="Internal").ap()
    RG = [[0, 1, 2, 3], [4, 5, 6, 7]]

    ctx = ExitStack()
    sb = lambda n, s, d: ctx.enter_context(nc.sbuf_tensor(n, s, d))
    psf = lambda n, s: ctx.enter_context(nc.psum_tensor(n, s, F32))

    x_sb = sb("x_sb", [128, NC, T], F32)
    xn_ar = sb("xn_ar", [128, NC * T], BF16)
    qa_ar = sb("qa_ar", [128, NC * T], BF16)
    at_ar = sb("at_ar", [128, NC * T], BF16)
    wsl = sb("wsl", [128, 2, 16, 128], BF16)
    sq_sb = sb("sq_sb", [128, T], F32)
    sqb_sb = sb("sqb_sb", [128, T], BF16)
    vec_sb = sb("vec_sb", [128, 2, T], F32)
    ones_b = sb("ones_b", [128, 128], BF16)
    ones_f = sb("ones_f", [128, 128], F32)
    ident_b = sb("ident_b", [128, 128], BF16)
    ident_f = sb("identf_sb", [128, 128], F32)
    hot_sb = sb("hot_sb", [128, NH, NH], BF16)
    nrm_sb = sb("nrm_sb", [16, 16], F32)
    cvec_sb = sb("cvec_sb", [128, NH], F32)
    gsel_sb = sb("gsel_sb", [128, 16], U32)
    ysel_sb = sb("ysel_sb", [128, 16], U32)
    pmask_sb = sb("pmask_sb", [128, 4], F32)
    bl1_sb = sb("bl1_sb", [128, 4, NC], F32)
    stg_sb = sb("stg_sb", [128, 16, 128], BF16)
    ev_sb = sb("ev_sb", [128, 2, 512], F32)
    gt_sb = sb("gt_sb", [128, 16], F32)
    mx8_sb = sb("mx8_sb", [128, 8], F32)
    mi8_sb = sb("mi8_sb", [128, 8], U32)
    sel_sb = sb("sel_sb", [128, 8, 4], F32)
    msk_sb = sb("msk_sb", [128, 8, 16], F32)
    iota16_sb = sb("iota16_sb", [128, 16], F32)
    cb4_sb = sb("cb4_sb", [128, 4], F32)
    hr8_sb = sb("hr8_sb", [8, 8, 128], BF16)
    l1b_sb = sb("l1b_sb", [128, 16], F32)
    wb1_sb = sb("wb1_sb", [128, T], BF16)
    sg1_sb = sb("sg1_sb", [128, 2, 512], BF16)
    tm1_sb = sb("tm1_sb", [128, 2, 512], BF16)

    ps_a = psf("ps_a", [128, 2, 512])
    ps_b = psf("ps_b", [128, 2, 512])
    ps_c = psf("ps_c", [128, 4, 512])

    def arview(ar, c=NC):
        return ar.ap().rearrange("p (c t) -> p c t", c=c)
    xn_v = arview(xn_ar)
    q_v = arview(qa_ar)
    at_v = arview(at_ar)
    kh_v = xn_ar.ap()[:, 0:4096].rearrange("p (j q) -> p j q", j=32)
    vh_v = xn_ar.ap()[:, 4096:8192].rearrange("p (j h) -> p j h", j=32)
    ptA = at_ar.ap()[:, 0:1024]
    ptB = at_ar.ap()[:, 1024:2048]
    def attn_chunk(c):
        if c < 14:
            return at_ar.ap()[:, 2048 + c * 1024: 2048 + (c + 1) * 1024]
        return xn_ar.ap()[:, 8192 + (c - 14) * 1024: 8192 + (c - 13) * 1024]
    nrm_psv = ps_b[0:16, :, :].rearrange("p a t -> p (a t)")
    ksh_v = at_ar.ap()[:, 0:4096].rearrange("p (j q) -> p j q", j=32)
    vtm_v = at_ar.ap()[:, 4096:8192].rearrange("p (j q) -> p j q", j=32)
    ptfA = at_ar.ap()[:, 8192:9216]
    ptfB = at_ar.ap()[:, 9216:10240]
    ystg_v = at_ar.ap()[:, 10240:10752].rearrange("p (a t) -> p a t", a=2)
    ksel_v = at_ar.ap()[:, 10752:11776].rearrange("p (tt e) -> p tt e", tt=8)
    vsel_v = at_ar.ap()[:, 11776:12800].rearrange("p (tt e) -> p tt e", tt=8)
    ytmp_v = at_ar.ap()[:, 12800:14848].rearrange("p (a t) -> p a t", a=2)
    qmine_v = qa_ar.ap().rearrange("p (s b t) -> p s b t", s=4, b=4)
    m_psv = ps_b[0:1, 0, :]
    m2_psv = ps_b[0:1, 1, :]
    vecA = vec_sb[0:1, 0, :]
    vecB = vec_sb[0:1, 1, :]
    A_bc = ps_c.ap().rearrange("p c t -> p (c t)")[:, 0:1024]
    B_bc = ps_c.ap().rearrange("p c t -> p (c t)")[:, 1024:2048]

    s_wdma = Sem(nc, "s_wdma")
    s_mm = Sem(nc, "s_mm")
    s_ve = Sem(nc, "s_ve")
    s_ac = Sem(nc, "s_ac")
    s_ge = Sem(nc, "s_ge")
    s_io = Sem(nc, "s_io")
    s_cc = Sem(nc, "s_cc")
    s_st = Sem(nc, "s_st")
    s_ind = Sem(nc, "s_ind")

    P = Prog(nc)
    P.chain = {eng: Sem(nc, "s_chain_" + eng) for eng in Prog.ENGINES}

    def dma(eng, out, in_, sem=None, k=16):
        if sem is None:
            sem = s_io
        h = sem.inc(k)
        def f(e, out=out, in_=in_, h=h, k=k):
            e.dma_start(out=out, in_=in_).then_inc(h, k)
        P.emit(eng, f)

    def load_consts():
        dma("sync", ident_b[:], ident_bf_in[:, :])
        dma("sync", ident_f[:], ident_f_in[:, :])
        dma("sync", hot_sb[:], hotcols_in[:, :, :])
        dma("sync", gsel_sb[:], gsel_in[:, :])
        dma("sync", ysel_sb[:], ysel_in[:, :])
        dma("sync", pmask_sb[:], pmask_in[:, :])
        dma("sync", bl1_sb[:], b_l1[:, :, :])
        dma("sync", iota16_sb[:], iota16_in[:, :])
        dma("sync", hr8_sb[:], hotrows8_in[:, :, :])
        dma("sync", x_sb[:], x0[:, :, :])
        def g(e):
            e.memset(ones_b[:], 1.0)
            e.memset(ones_f[:], 1.0)
        P.emit("gpsimd", g)
        P.mark("gpsimd", s_ge)

    def layernorm():
        x_ready = s_io.n
        ve_ready = s_ve.n
        P.wait("scalar", s_io, x_ready)
        P.wait("scalar", s_ve, ve_ready)
        P.wait("tensor", s_io, x_ready)
        P.wait("tensor", s_ve, ve_ready)
        P.wait("tensor", s_ge, 1)
        sqm = []
        for c in range(NC):
            if c >= 1:
                P.wait("scalar", s_mm, sqm[c - 1])
            def fsq(e, c=c):
                e.activation(out=sq_sb[:, :], in_=x_sb[:, c, :], func=ACT.Square)
            P.emit("scalar", fsq)
            P.mark("scalar", s_ac)
            P.wait("tensor", s_ac)
            def fmm(e, c=c):
                for hf in range(2):
                    e.matmul(ps_c[0:1, hf, :], ones_f[:, 0:1],
                             x_sb[:, c, hf * 512:(hf + 1) * 512],
                             start=(c == 0), stop=(c == NC - 1))
                    e.matmul(ps_c[0:1, 2 + hf, :], ones_f[:, 0:1],
                             sq_sb[:, hf * 512:(hf + 1) * 512],
                             start=(c == 0), stop=(c == NC - 1))
            P.emit("tensor", fmm)
            sqm.append(P.mark("tensor", s_mm))
        P.wait("vector", s_mm)
        mfv = ps_c[0:1, :, :].rearrange("p a t -> p (a t)")
        P.seq("vector",
              lambda e: e.tensor_scalar_mul(out=vecA[:], in0=mfv[:, 0:1024], scalar1=1.0 / D),
              lambda e: e.tensor_mul(out=vecB[:], in0=vecA[:], in1=vecA[:]),
              lambda e: e.scalar_tensor_tensor(out=vecB[:], in0=mfv[:, 1024:2048],
                                               scalar=1.0 / float(D), in1=vecB[:],
                                               op0=ALU.mult, op1=ALU.subtract),
              lambda e: e.tensor_scalar_add(out=vecB[:], in0=vecB[:], scalar1=1e-5))
        P.mark("vector", s_ve)
        P.wait("scalar", s_ve)
        def fsr(e):
            e.activation(out=sq_sb[0:1, 0:1024], in_=vecB[:], func=ACT.Sqrt)
        P.emit("scalar", fsr)
        P.mark("scalar", s_ac)
        P.wait("vector", s_ac)
        P.seq("vector",
              lambda e: e.reciprocal(out=vecB[:], in_=sq_sb[0:1, 0:1024]),
              lambda e: e.tensor_mul(out=vecA[:], in0=vecA[:], in1=vecB[:]))
        P.mark("vector", s_ve)
        P.wait("tensor", s_ve)
        def fbc(e):
            for hf in range(2):
                e.matmul(ps_c[:, hf, :], ones_f[0:1, :], vecB[0:1, hf * 512:(hf + 1) * 512],
                         start=True, stop=True)
                e.matmul(ps_c[:, 2 + hf, :], ones_f[0:1, :], vecA[0:1, hf * 512:(hf + 1) * 512],
                         start=True, stop=True)
        P.emit("tensor", fbc)
        P.mark("tensor", s_mm)
        P.wait("vector", s_mm)
        for c in range(NC):
            P.seq("vector",
                  lambda e, c=c: e.tensor_mul(out=sq_sb[:, :], in0=x_sb[:, c, :], in1=A_bc),
                  lambda e, c=c: e.tensor_sub(out=xn_v[:, c, :], in0=sq_sb[:, :], in1=B_bc))
            P.ssync("vector")
        P.mark("vector", s_ve)

    wctr = [0]
    slab_mm_done = []

    def projection(w_dram, col0, n_out, rhs_fn, evac_fn, nkc=NC, xn_ready=None,
                   hook=None):
        if xn_ready is not None:
            P.wait("tensor", s_ve, xn_ready)
        for c in range(n_out):
            i = wctr[0]
            buf = i % 2
            wctr[0] += 1
            if i >= 2:
                P.wait("gpsimd", s_mm, slab_mm_done[i - 2])
            h = s_wdma.inc(16)
            def fw(e, c=c, buf=buf, h=h, w_dram=w_dram, col0=col0, nkc=nkc):
                src = w_dram[:, col0 + c * 128: col0 + (c + 1) * 128].rearrange(
                    "(k p) e -> p k e", p=128)
                e.dma_start(out=wsl[:, buf, 0:nkc, :], in_=src).then_inc(h, 16)
            P.emit("gpsimd", fw)
            P.wait("tensor", s_wdma)
            for hf in range(2):
                def fmm(e, c=c, buf=buf, hf=hf, rhs_fn=rhs_fn, nkc=nkc):
                    for kc in range(nkc):
                        e.matmul(ps_a[:, hf, :], wsl[:, buf, kc, :], rhs_fn(kc, hf),
                                 start=(kc == 0), stop=(kc == nkc - 1))
                P.emit("tensor", fmm)
                P.mark("tensor", s_mm)
                P.wait("vector", s_mm)
                evac_fn(c, hf)
                P.mark("vector", s_ve)
                P.wait("tensor", s_ve)
            if hook is not None:
                hook(c)
            slab_mm_done.append(s_mm.n)

    def xn_rhs(kc, hf):
        return xn_v[:, kc, hf * 512:(hf + 1) * 512]

    def l0_qkv(xn_ready):
        def evq(c, hf):
            def f(e, c=c, hf=hf):
                e.tensor_copy(out=q_v[:, c, hf * 512:(hf + 1) * 512], in_=ps_a[:, hf, :])
            P.emit("vector", f)
        projection(w_l0q, 0, NC, xn_rhs, evq, xn_ready=xn_ready)
        def evk(c, hf):
            def f(e, c=c, hf=hf):
                e.tensor_copy(out=at_v[:, c, hf * 512:(hf + 1) * 512], in_=ps_a[:, hf, :])
            P.emit("vector", f)
        projection(w_l0k, 0, NC, xn_rhs, evk)
        k_done = s_ve.n
        P.wait("sync", s_ve, k_done)
        dma("sync", k_loc.rearrange("(c p) t -> p c t", p=128), at_v[:, :, :])
        k_dma = s_io.n
        sqm = []
        for c in range(NC):
            P.wait("scalar", s_ve, k_done)
            if c >= 1:
                P.wait("scalar", s_mm, sqm[c - 1])
            def fsq(e, c=c):
                e.activation(out=sqb_sb[:, :], in_=at_v[:, c, :], func=ACT.Square)
            P.emit("scalar", fsq)
            P.mark("scalar", s_ac)
            P.wait("tensor", s_ac)
            def fnm(e, c=c):
                for hf in range(2):
                    e.matmul(nrm_psv[:, hf * 512:(hf + 1) * 512], hot_sb[:, c, :],
                             sqb_sb[:, hf * 512:(hf + 1) * 512],
                             start=(c == 0), stop=(c == NC - 1))
            P.emit("tensor", fnm)
            sqm.append(P.mark("tensor", s_mm))
        P.wait("vector", s_mm)
        def fkn(e):
            e.reduce_max(out=nrm_sb[:, 0:1], in_=nrm_psv[:, :], axis=AXX)
        P.emit("vector", fkn)
        kn_done = P.mark("vector", s_ve)
        P.wait("vector", s_io, k_dma)
        def evv(c, hf):
            def f(e, c=c, hf=hf):
                e.tensor_copy(out=at_v[:, c, hf * 512:(hf + 1) * 512], in_=ps_a[:, hf, :])
            P.emit("vector", f)
        projection(w_l0v, 0, NC, xn_rhs, evv)
        tp = 0
        for c in range(NC):
            for tt in range(8):
                par = tp % 2
                tp += 1
                def ftr(e, c=c, tt=tt, par=par):
                    e.matmul(ps_c[:, par, 0:128], at_v[:, c, tt * 128:(tt + 1) * 128],
                             ident_b[:], start=True, stop=True)
                P.emit("tensor", ftr)
                P.mark("tensor", s_mm)
                P.wait("vector", s_mm)
                def fev(e, tt=tt, par=par):
                    e.tensor_copy(out=stg_sb[:, tt, :], in_=ps_c[:, par, 0:128])
                P.emit("vector", fev)
                P.mark("vector", s_ve)
                if tp >= 2:
                    P.wait("tensor", s_ve)
            P.wait("sync", s_ve)
            dma("sync", v_loc.rearrange("(tt p) e -> p tt e", p=128)[:, :, c * 128:(c + 1) * 128],
                stg_sb[:, 0:8, :])
            P.wait("vector", s_io)
        v_dma = s_io.n
        sqm2 = []
        for c in range(NC):
            if c == 0:
                P.wait("tensor", s_ve, kn_done)
            if c >= 1:
                P.wait("scalar", s_mm, sqm2[c - 1])
            def fsq(e, c=c):
                e.activation(out=sqb_sb[:, :], in_=q_v[:, c, :], func=ACT.Square)
            P.emit("scalar", fsq)
            P.mark("scalar", s_ac)
            P.wait("tensor", s_ac)
            def fnm(e, c=c):
                for hf in range(2):
                    e.matmul(nrm_psv[:, hf * 512:(hf + 1) * 512], hot_sb[:, c, :],
                             sqb_sb[:, hf * 512:(hf + 1) * 512],
                             start=(c == 0), stop=(c == NC - 1))
            P.emit("tensor", fnm)
            sqm2.append(P.mark("tensor", s_mm))
        P.wait("vector", s_mm)
        P.seq("vector",
              lambda e: e.reduce_max(out=nrm_sb[:, 1:2], in_=nrm_psv[:, :], axis=AXX),
              lambda e: e.tensor_mul(out=nrm_sb[:, 2:3], in0=nrm_sb[:, 0:1], in1=nrm_sb[:, 1:2]))
        P.mark("vector", s_ve)
        P.wait("scalar", s_ve)
        def fsr(e):
            e.activation(out=nrm_sb[:, 3:4], in_=nrm_sb[:, 2:3], func=ACT.Sqrt)
        P.emit("scalar", fsr)
        P.mark("scalar", s_ac)
        P.wait("vector", s_ac)
        P.seq("vector",
              lambda e: e.tensor_scalar_mul(out=nrm_sb[:, 4:5], in0=nrm_sb[:, 3:4], scalar1=-SCALE),
              lambda e: e.tensor_scalar(out=ev_sb[0:16, 0, 0:16], in0=ident_f[0:16, 0:16],
                                        scalar1=nrm_sb[:, 4:5], scalar2=None, op0=ALU.mult))
        P.mark("vector", s_ve)
        P.wait("tensor", s_ve)
        def fcb(e):
            e.matmul(ps_b[:, 1, 0:16], ones_f[0:16, :], ev_sb[0:16, 0, 0:16],
                     start=True, stop=True)
        P.emit("tensor", fcb)
        P.mark("tensor", s_mm)
        P.wait("vector", s_mm)
        def fcv(e):
            e.tensor_copy(out=cvec_sb[:, :], in_=ps_b[:, 1, 0:16])
        P.emit("vector", fcv)
        cvec_done = P.mark("vector", s_ve)
        P.wait("gpsimd", s_io, max(k_dma, v_dma))
        h1 = s_cc.inc(1)
        def fcc1(e, h1=h1):
            e.collective_compute("AllGather", ALU.bypass, replica_groups=RG,
                                 ins=[k_loc[:, :]], outs=[k_gth[:, :, :]]).then_inc(h1, 1)
        P.emit("gpsimd", fcc1)
        h2 = s_cc.inc(1)
        def fcc2(e, h2=h2):
            e.collective_compute("AllGather", ALU.bypass, replica_groups=RG,
                                 ins=[v_loc[:, :]], outs=[v_gth[:, :, :]]).then_inc(h2, 1)
        P.emit("gpsimd", fcc2)
        return cvec_done

    def l0_attn(cvec_done):
        cc_done = s_cc.n
        P.wait("sync", s_cc, cc_done)
        P.wait("scalar", s_ve, cvec_done)
        for h in range(NH):
            for sl in range(4):
                dma("sync", kh_v.rearrange("p j q -> p (j q)")[:, sl * 1024:(sl + 1) * 1024],
                    k_gth[sl, h * 128:(h + 1) * 128, :])
            for sl in range(4):
                dma("sync",
                    vh_v.rearrange("p j h2 -> p (j h2)")[:, sl * 1024:(sl + 1) * 1024]
                    .rearrange("p (tt e) -> p tt e", tt=8),
                    v_gth[sl].rearrange("(tt p) e -> p tt e", p=128)[:, :, h * 128:(h + 1) * 128])
            ld_done = s_io.n
            P.wait("tensor", s_io, ld_done)
            P.wait("scalar", s_io, ld_done)
            for hf in range(2):
                for jc in range(16):
                    par = jc % 2
                    def fst(e, h=h, hf=hf, jc=jc, par=par):
                        for jj in range(2):
                            j = jc * 2 + jj
                            e.matmul(ps_c[:, par * 2 + jj, :], kh_v[:, j, :],
                                     q_v[:, h, hf * 512:(hf + 1) * 512],
                                     start=True, stop=True)
                    P.emit("tensor", fst)
                    P.mark("tensor", s_mm)
                    P.wait("scalar", s_mm)
                    def fex(e, h=h, par=par):
                        src = ps_c[:, par * 2:par * 2 + 2, :].rearrange("p a t -> p (a t)")
                        dst = ptA if par == 0 else ptB
                        e.activation(out=dst, in_=src, func=ACT.Exp,
                                     bias=cvec_sb[:, h:h + 1], scale=SCALE)
                    P.emit("scalar", fex)
                    P.mark("scalar", s_ac)
                    P.wait("tensor", s_ac)
                    def fpv(e, hf=hf, jc=jc, par=par):
                        for jj in range(2):
                            j = jc * 2 + jj
                            pt = (ptA if par == 0 else ptB)[:, jj * 512:(jj + 1) * 512]
                            e.matmul(ps_a[:, 0, :], vh_v[:, j, :], pt,
                                     start=(j == 0), stop=(j == 31))
                            e.matmul(ps_a[:, 1, :], ones_b[:], pt,
                                     start=(j == 0), stop=(j == 31))
                    P.emit("tensor", fpv)
                    P.mark("tensor", s_mm)
                P.wait("vector", s_mm)
                P.seq("vector",
                      lambda e: e.reciprocal(out=ev_sb[:, 0, :], in_=ps_a[:, 1, :]),
                      lambda e, h=h, hf=hf: e.tensor_mul(
                          out=attn_chunk(h)[:, hf * 512:(hf + 1) * 512],
                          in0=ps_a[:, 0, :], in1=ev_sb[:, 0, :]))
                P.mark("vector", s_ve)
                P.wait("tensor", s_ve)
            P.wait("sync", s_mm)

    def l0_ffn():
        xn_ok = s_ve.n
        h = s_wdma.inc(16)
        def fwg(e, h=h):
            src = w_l0g[:, :].rearrange("(k p) e -> p k e", p=128)
            e.dma_start(out=wsl[:, 0, :, 0:16], in_=src).then_inc(h, 16)
        P.emit("gpsimd", fwg)
        P.wait("tensor", s_wdma)
        P.wait("tensor", s_ve, xn_ok)
        def fgm(e):
            for hf in range(2):
                for kc in range(NC):
                    e.matmul(nrm_psv[:, hf * 512:(hf + 1) * 512], wsl[:, 0, kc, 0:16],
                             xn_v[:, kc, hf * 512:(hf + 1) * 512],
                             start=(kc == 0), stop=(kc == NC - 1))
        P.emit("tensor", fgm)
        P.mark("tensor", s_mm)
        wctr[0] += 2
        slab_mm_done.extend([s_mm.n, s_mm.n])
        P.wait("vector", s_mm)
        def fge(e):
            e.tensor_copy(out=stg_sb[0:16, :, :].rearrange("p a b -> p (a b)")[:, 0:1024],
                          in_=nrm_psv[:, :])
        P.emit("vector", fge)
        P.mark("vector", s_ve)
        glog = stg_sb[0:16, :, :].rearrange("p a b -> p (a b)")
        P.wait("tensor", s_ve)
        for tt in range(8):
            par = tt % 2
            def ftr(e, tt=tt, par=par):
                e.matmul(ps_c[:, par, 0:16], glog[:, tt * 128:(tt + 1) * 128],
                         ident_b[0:16, 0:16], start=True, stop=True)
            P.emit("tensor", ftr)
            P.mark("tensor", s_mm)
            P.wait("vector", s_mm)
            P.seq("vector",
                  lambda e, par=par: e.tensor_copy(out=gt_sb[:, :], in_=ps_c[:, par, 0:16]),
                  lambda e: e.max(mx8_sb[:, 0:8], gt_sb[:, :]),
                  lambda e: e.max_index(mi8_sb[:, 0:8], mx8_sb[:, 0:8], gt_sb[:, :]))
            P.ssync("vector")
            def frt(e, tt=tt):
                e.tensor_copy(out=sel_sb[:, tt, 0:1], in_=mi8_sb[:, 0:1])
                e.reduce_max(out=sel_sb[:, tt, 1:2], in_=gt_sb[:, :], axis=AXX, negate=True)
            P.emit("vector", frt)
            P.mark("vector", s_ve)
            P.wait("scalar", s_ve)
            def fex(e, tt=tt):
                e.activation(out=gt_sb[:, :], in_=gt_sb[:, :], func=ACT.Exp,
                             bias=sel_sb[:, tt, 1:2], scale=1.0)
            P.emit("scalar", fex)
            P.mark("scalar", s_ac)
            P.wait("vector", s_ac)
            P.seq("vector",
                  lambda e, tt=tt: e.reduce_sum(out=sel_sb[:, tt, 2:3], in_=gt_sb[:, :], axis=AXX),
                  lambda e, tt=tt: e.reciprocal(out=sel_sb[:, tt, 3:4], in_=sel_sb[:, tt, 2:3]),
                  lambda e, tt=tt: e.tensor_single_scalar(out=msk_sb[:, tt, :], in_=iota16_sb[:, :],
                                                          scalar=sel_sb[:, tt, 0:1], op=ALU.is_equal))
            P.mark("vector", s_ve)
            P.wait("tensor", s_ve)
        msk_done = s_ve.n
        qstg = at_ar.ap()[:, 0:2048]
        qdma = []
        def evqf(c, hf):
            slot = c % 2
            def f(e, c=c, hf=hf, slot=slot):
                e.tensor_copy(out=qstg[:, slot * 1024 + hf * 512: slot * 1024 + (hf + 1) * 512],
                              in_=ps_a[:, hf, :])
            P.emit("vector", f)
            if hf == 1:
                P.wait("sync", s_ve, s_ve.n + 1)
                dma("sync", q_loc[c * 128:(c + 1) * 128, :],
                    qstg[:, slot * 1024:(slot + 1) * 1024])
                qdma.append(s_io.n)
        def evqf2(c, hf):
            if c >= 2 and hf == 0:
                P.wait("vector", s_io, qdma[c - 2])
            evqf(c, hf)
        projection(w_l0qkv, 0, NC, xn_rhs, evqf2)
        kvt = at_ar.ap()[:, 0:1024]
        tsel = at_ar.ap()[:, 2048:3072].rearrange("p (tt e) -> p tt e", tt=8)
        for kv_i, (colbase, dst) in enumerate(((2048, ksel_v), (4096, vsel_v))):
            if kv_i == 0:
                P.wait("vector", s_io)
                P.wait("vector", s_ve, msk_done)
            def evkv(c, hf, kvt=kvt):
                def f(e, c=c, hf=hf):
                    e.tensor_copy(out=kvt[:, hf * 512:(hf + 1) * 512], in_=ps_a[:, hf, :])
                P.emit("vector", f)
            def hook(c):
                for tt in range(8):
                    par = tt % 2
                    P.wait("tensor", s_ve)
                    def ftr(e, tt=tt, par=par, kvt=kvt):
                        e.matmul(ps_c[:, 2 + par, 0:128], kvt[:, tt * 128:(tt + 1) * 128],
                                 ident_b[:], start=True, stop=True)
                    P.emit("tensor", ftr)
                    P.mark("tensor", s_mm)
                    P.wait("vector", s_mm)
                    def fev(e, tt=tt, par=par):
                        e.tensor_copy(out=tsel[:, tt, :], in_=ps_c[:, 2 + par, 0:128])
                    P.emit("vector", fev)
                    P.mark("vector", s_ve)
                P.ssync("vector")
                def facc(e, c=c, dst=dst):
                    for tt in range(8):
                        if c == 0:
                            e.tensor_scalar(out=dst[:, tt, :], in0=tsel[:, tt, :],
                                            scalar1=msk_sb[:, tt, c:c + 1], scalar2=None,
                                            op0=ALU.mult)
                        else:
                            e.scalar_tensor_tensor(out=dst[:, tt, :], in0=tsel[:, tt, :],
                                                   scalar=msk_sb[:, tt, c:c + 1],
                                                   in1=dst[:, tt, :],
                                                   op0=ALU.mult, op1=ALU.add)
                P.emit("vector", facc)
                P.mark("vector", s_ve)
            projection(w_l0qkv, colbase, NC, xn_rhs, evkv, hook=hook)
        P.ssync("vector")
        def fvt1(e):
            for tt in range(8):
                e.tensor_scalar(out=vsel_v[:, tt, :], in0=vsel_v[:, tt, :],
                                scalar1=sel_sb[:, tt, 3:4], scalar2=None, op0=ALU.mult)
        P.emit("vector", fvt1)
        P.mark("vector", s_ve)
        P.wait("sync", s_ve)
        dma("sync", kv_loc.rearrange("(tt p) e -> p tt e", p=128)[:, :, 0:128], ksel_v[:, :, :])
        dma("sync", kv_loc.rearrange("(tt p) e -> p tt e", p=128)[:, :, 128:256], vsel_v[:, :, :])
        kv_dma = s_io.n
        P.wait("gpsimd", s_io, kv_dma)
        hq = s_cc.inc(1)
        def fccq(e, hq=hq):
            e.collective_compute("AllGather", ALU.bypass, replica_groups=RG,
                                 ins=[q_loc[:, :]], outs=[q_gth[:, :]]).then_inc(hq, 1)
        P.emit("gpsimd", fccq)
        hkv = s_cc.inc(1)
        def fcckv(e, hkv=hkv):
            e.collective_compute("AllGather", ALU.bypass, replica_groups=RG,
                                 ins=[kv_loc[:, :]], outs=[kv_gth[:, :]]).then_inc(hkv, 1)
        P.emit("gpsimd", fcckv)
        cc_done = s_cc.n
        P.wait("sync", s_cc, cc_done)
        for j in range(32):
            hdt = s_io.inc(16)
            def ftd(e, j=j, hdt=hdt):
                e.dma_start(out=ksh_v[:, j, :], in_=kv_gth[j * 128:(j + 1) * 128, 0:128],
                            transpose=True).then_inc(hdt, 16)
            P.emit("sync", ftd)
        for j in range(32):
            dma("sync", vtm_v[:, j, :], kv_gth[j * 128:(j + 1) * 128, 128:256])
        P.wait("gpsimd", s_cc, cc_done)
        for col in range(16):
            hgi = s_ind.inc(16)
            def fgi(e, col=col, hgi=hgi):
                e.indirect_dma_start(out=qmine_v[:, col // 4, col % 4, :], out_offset=None,
                                     in_=q_gth[:, :],
                                     in_offset=IndirectOffsetOnAxis(gsel_sb[:, col:col + 1], 0)
                                     ).then_inc(hgi, 16)
            P.emit("gpsimd", fgi)
        ld_done = s_io.n
        qm_done = s_ind.n
        P.wait("scalar", s_io, ld_done)
        P.wait("scalar", s_ind, qm_done)
        P.wait("tensor", s_ind, qm_done)
        segs = []
        for seg in range(4):
            if seg >= 1:
                P.wait("scalar", s_mm)
            def fsq(e, seg=seg):
                e.activation(out=sqb_sb[:, :],
                             in_=ksh_v.rearrange("p j q -> p (j q)")[:, seg * 1024:(seg + 1) * 1024],
                             func=ACT.Square)
            P.emit("scalar", fsq)
            P.mark("scalar", s_ac)
            P.wait("tensor", s_ac)
            def fnm(e):
                for hf in range(2):
                    e.matmul(ps_b[0:1, hf, :], ones_b[:, 0:1], sqb_sb[:, hf * 512:(hf + 1) * 512],
                             start=True, stop=True)
            P.emit("tensor", fnm)
            P.mark("tensor", s_mm)
            P.wait("vector", s_mm)
            if seg == 0:
                def frd(e):
                    e.reduce_max(out=nrm_sb[0:1, 0:1],
                                 in_=ps_b[0:1, :, :].rearrange("p a t -> p (a t)"), axis=AXX)
                P.emit("vector", frd)
            else:
                P.seq("vector",
                      lambda e: e.reduce_max(out=nrm_sb[0:1, 1:2],
                                             in_=ps_b[0:1, :, :].rearrange("p a t -> p (a t)"),
                                             axis=AXX),
                      lambda e: e.tensor_max(out=nrm_sb[0:1, 0:1], in0=nrm_sb[0:1, 0:1],
                                             in1=nrm_sb[0:1, 1:2]))
            P.mark("vector", s_ve)
            P.wait("tensor", s_ve)
            segs.append(s_ve.n)
        for sl in range(4):
            for seg in range(4):
                P.wait("scalar", s_mm)
                def fsq(e, sl=sl, seg=seg):
                    e.activation(out=sqb_sb[:, :], in_=qmine_v[:, sl, seg, :], func=ACT.Square)
                P.emit("scalar", fsq)
                P.mark("scalar", s_ac)
                P.wait("tensor", s_ac)
                def fnm(e):
                    for hf in range(2):
                        e.matmul(ps_b[0:1, hf, :], ones_b[:, 0:1],
                                 sqb_sb[:, hf * 512:(hf + 1) * 512], start=True, stop=True)
                P.emit("tensor", fnm)
                P.mark("tensor", s_mm)
                P.wait("vector", s_mm)
                if seg == 0:
                    def frd(e, sl=sl):
                        e.reduce_max(out=nrm_sb[0:1, 2 + sl:3 + sl],
                                     in_=ps_b[0:1, :, :].rearrange("p a t -> p (a t)"), axis=AXX)
                    P.emit("vector", frd)
                else:
                    P.seq("vector",
                          lambda e: e.reduce_max(out=nrm_sb[0:1, 6:7],
                                                 in_=ps_b[0:1, :, :].rearrange("p a t -> p (a t)"),
                                                 axis=AXX),
                          lambda e, sl=sl: e.tensor_max(out=nrm_sb[0:1, 2 + sl:3 + sl],
                                                        in0=nrm_sb[0:1, 2 + sl:3 + sl],
                                                        in1=nrm_sb[0:1, 6:7]))
                P.mark("vector", s_ve)
                P.wait("tensor", s_ve)
        P.ssync("vector")
        def fcs(e):
            for sl in range(4):
                e.tensor_mul(out=nrm_sb[0:1, 8 + sl:9 + sl], in0=nrm_sb[0:1, 0:1],
                             in1=nrm_sb[0:1, 2 + sl:3 + sl])
        P.emit("vector", fcs)
        P.mark("vector", s_ve)
        P.wait("scalar", s_ve)
        def fsr(e):
            e.activation(out=nrm_sb[0:1, 12:16], in_=nrm_sb[0:1, 8:12], func=ACT.Sqrt)
        P.emit("scalar", fsr)
        P.mark("scalar", s_ac)
        P.wait("vector", s_ac)
        def fng(e):
            e.tensor_scalar_mul(out=nrm_sb[0:1, 12:16], in0=nrm_sb[0:1, 12:16], scalar1=-SCALE)
        P.emit("vector", fng)
        P.mark("vector", s_ve)
        P.wait("tensor", s_ve)
        def fbb(e):
            e.matmul(ps_b[:, 1, 0:4], ones_f[0:1, :], nrm_sb[0:1, 12:16], start=True, stop=True)
        P.emit("tensor", fbb)
        P.mark("tensor", s_mm)
        P.wait("vector", s_mm)
        def fcv(e):
            e.tensor_copy(out=cb4_sb[:, :], in_=ps_b[:, 1, 0:4])
        P.emit("vector", fcv)
        cb_done = P.mark("vector", s_ve)
        P.wait("scalar", s_ve, cb_done)
        ydma = []
        for sl in range(4):
            for pi in range(16):
                slab = (2 * pi) // 8
                off = ((2 * pi) % 8) * 128
                njs = 2 * pi + 2
                nch = (njs + 3) // 4
                for ch in range(nch):
                    par = ch % 2
                    j0 = ch * 4
                    jn = min(4, njs - j0)
                    def fst(e, sl=sl, pi=pi, ch=ch, par=par, j0=j0, jn=jn, slab=slab, off=off):
                        for jj in range(jn):
                            j = j0 + jj
                            e.matmul(ps_c[:, par * 2 + jj // 2, (jj % 2) * 256:(jj % 2 + 1) * 256],
                                     ksh_v[:, j, :], qmine_v[:, sl, slab, off:off + 256],
                                     start=True, stop=True)
                    P.emit("tensor", fst)
                    P.mark("tensor", s_mm)
                    P.wait("scalar", s_mm)
                    def fex(e, sl=sl, par=par, jn=jn):
                        src = ps_c[:, par * 2:par * 2 + 2, :].rearrange("p a t -> p (a t)")[:, 0:jn * 256]
                        dst = (ptfA if par == 0 else ptfB)[:, 0:jn * 256]
                        e.activation(out=dst, in_=src, func=ACT.Exp,
                                     bias=cb4_sb[:, sl:sl + 1], scale=SCALE)
                    P.emit("scalar", fex)
                    P.mark("scalar", s_ac)
                    has_mask = (j0 + jn == njs)
                    if has_mask:
                        P.wait("gpsimd", s_ac)
                        def fmk(e, par=par, jn=jn):
                            buf = ptfA if par == 0 else ptfB
                            e.affine_select(out=buf[:, (jn - 2) * 256:(jn - 1) * 256],
                                            in_=buf[:, (jn - 2) * 256:(jn - 1) * 256],
                                            pattern=[[1, 256]], compare_op=ALU.is_ge,
                                            fill=0.0, base=0, channel_multiplier=-1)
                            e.affine_select(out=buf[:, (jn - 1) * 256:jn * 256],
                                            in_=buf[:, (jn - 1) * 256:jn * 256],
                                            pattern=[[1, 256]], compare_op=ALU.is_ge,
                                            fill=0.0, base=-128, channel_multiplier=-1)
                        P.emit("gpsimd", fmk)
                        P.mark("gpsimd", s_ge)
                        P.wait("tensor", s_ge)
                    else:
                        P.wait("tensor", s_ac)
                    def fpv(e, par=par, j0=j0, jn=jn, njs=njs):
                        buf = ptfA if par == 0 else ptfB
                        for jj in range(jn):
                            j = j0 + jj
                            pt = buf[:, jj * 256:(jj + 1) * 256]
                            e.matmul(ps_a[:, 0, 0:256], vtm_v[:, j, :], pt,
                                     start=(j == 0), stop=(j == njs - 1))
                            e.matmul(ps_a[:, 1, 0:256], ones_b[:], pt,
                                     start=(j == 0), stop=(j == njs - 1))
                    P.emit("tensor", fpv)
                    P.mark("tensor", s_mm)
                yslot = pi % 2
                if len(ydma) >= 2:
                    P.wait("vector", s_io, ydma[-2])
                P.wait("vector", s_mm)
                P.seq("vector",
                      lambda e: e.reciprocal(out=ev_sb[:, 0, 0:256], in_=ps_a[:, 1, 0:256]),
                      lambda e, yslot=yslot: e.tensor_mul(out=ystg_v[:, yslot, :],
                                                          in0=ps_a[:, 0, 0:256],
                                                          in1=ev_sb[:, 0, 0:256]))
                P.mark("vector", s_ve)
                P.wait("tensor", s_ve)
                P.wait("sync", s_ve)
                dma("sync", yh_loc[slab, sl * 128:(sl + 1) * 128, off:off + 256],
                    ystg_v[:, yslot, :])
                ydma.append(s_io.n)
        yh_done = s_io.n
        P.wait("gpsimd", s_io, yh_done)
        hy = s_cc.inc(1)
        def fccy(e, hy=hy):
            e.collective_compute("AllGather", ALU.bypass, replica_groups=RG,
                                 ins=[yh_loc[:, :, :]], outs=[yh_gth[:, :, :, :]]).then_inc(hy, 1)
        P.emit("gpsimd", fccy)
        ycc = s_cc.n
        P.wait("gpsimd", s_cc, ycc)
        yflat = yh_gth.rearrange("r s h t -> (r s h) t")
        P.ssync("vector")
        add_marks = []
        for c in range(NC):
            if c >= 2:
                P.wait("gpsimd", s_ve, add_marks[c - 2])
            hgi = s_ind.inc(16)
            def fgy(e, c=c, hgi=hgi):
                e.indirect_dma_start(out=ytmp_v[:, c % 2, :], out_offset=None,
                                     in_=yflat,
                                     in_offset=IndirectOffsetOnAxis(ysel_sb[:, c:c + 1], 0)
                                     ).then_inc(hgi, 16)
            P.emit("gpsimd", fgy)
            P.wait("vector", s_ind)
            def fad(e, c=c):
                e.tensor_add(out=x_sb[:, c, :], in0=x_sb[:, c, :], in1=ytmp_v[:, c % 2, :])
            P.emit("vector", fad)
            add_marks.append(P.mark("vector", s_ve))

    def l0_out():
        P.ssync("vector")
        def attn_rhs(kc, hf):
            return attn_chunk(kc)[:, hf * 512:(hf + 1) * 512]
        def evo(c, hf):
            def f(e, c=c, hf=hf):
                e.scalar_tensor_tensor(out=x_sb[:, c, hf * 512:(hf + 1) * 512],
                                       in0=ps_a[:, hf, :], scalar=1.0,
                                       in1=x_sb[:, c, hf * 512:(hf + 1) * 512],
                                       op0=ALU.mult, op1=ALU.add)
            P.emit("vector", f)
        projection(w_l0o, 0, NC, attn_rhs, evo)


    def l1_attn():
        P.ssync("vector")
        def evq1(c, hf):
            def f(e, c=c, hf=hf):
                e.tensor_scalar(out=q_v[:, c, hf * 512:(hf + 1) * 512], in0=ps_a[:, hf, :],
                                scalar1=bl1_sb[:, 0, c:c + 1], scalar2=None, op0=ALU.add)
            P.emit("vector", f)
        projection(w_l1q, 0, NC, xn_rhs, evq1, xn_ready=s_ve.n)
        def evk1(c, hf):
            def f(e, c=c, hf=hf):
                e.tensor_scalar(out=at_v[:, c, hf * 512:(hf + 1) * 512], in0=ps_a[:, hf, :],
                                scalar1=bl1_sb[:, 1, c:c + 1], scalar2=None, op0=ALU.add)
            P.emit("vector", f)
        projection(w_l1k, 0, NC, xn_rhs, evk1)
        def evv1(c, hf):
            def f(e, c=c, hf=hf):
                e.scalar_tensor_tensor(out=at_v[:, c, hf * 512:(hf + 1) * 512],
                                       in0=ps_a[:, hf, :], scalar=bl1_sb[:, 2, c:c + 1],
                                       in1=at_v[:, c, hf * 512:(hf + 1) * 512],
                                       op0=ALU.add, op1=ALU.mult)
            P.emit("vector", f)
        projection(w_l1v, 0, NC, xn_rhs, evv1)
        def evg1(c, hf):
            P.wait("scalar", s_mm)
            def fg(e, c=c, hf=hf):
                e.activation(out=sg1_sb[:, hf, :], in_=ps_a[:, hf, :], func=ACT.Sigmoid,
                             bias=bl1_sb[:, 3, c:c + 1], scale=1.0)
            P.emit("scalar", fg)
            P.mark("scalar", s_ac)
            P.wait("vector", s_ac)
            def fq(e, c=c, hf=hf):
                e.tensor_mul(out=q_v[:, c, hf * 512:(hf + 1) * 512],
                             in0=q_v[:, c, hf * 512:(hf + 1) * 512], in1=sg1_sb[:, hf, :])
            P.emit("vector", fq)
        projection(w_l1g, 0, NC, xn_rhs, evg1)
        P.ssync("vector")
        for c in range(NC):
            def fs(e, c=c):
                e.reduce_sum(out=cvec_sb[:, c:c + 1], in_=at_v[:, c, :], axis=AXX)
            P.emit("vector", fs)
        P.mark("vector", s_ve)
        P.wait("sync", s_ve)
        dma("sync", pfx_loc[:, :], cvec_sb[:, :])
        P.wait("gpsimd", s_io)
        hpf = s_cc.inc(1)
        def fccp(e, hpf=hpf):
            e.collective_compute("AllGather", ALU.bypass, replica_groups=RG,
                                 ins=[pfx_loc[:, :]], outs=[pfx_gth[:, :, :]]).then_inc(hpf, 1)
        P.emit("gpsimd", fccp)
        P.wait("sync", s_cc)
        dma("sync", ev_sb[:, 0, 0:64].rearrange("p (r c) -> p r c", r=4), pfx_gth[:, :, :].rearrange("r p c -> p r c"))
        P.wait("vector", s_io)
        gv = ev_sb[:, 0, 0:64].rearrange("p (r c) -> p r c", r=4)
        P.seq("vector",
              lambda e: e.tensor_scalar(out=l1b_sb[:, :], in0=gv[:, 0, :],
                                        scalar1=pmask_sb[:, 0:1], scalar2=None, op0=ALU.mult),
              lambda e: e.scalar_tensor_tensor(out=l1b_sb[:, :], in0=gv[:, 1, :],
                                               scalar=pmask_sb[:, 1:2], in1=l1b_sb[:, :],
                                               op0=ALU.mult, op1=ALU.add),
              lambda e: e.scalar_tensor_tensor(out=l1b_sb[:, :], in0=gv[:, 2, :],
                                               scalar=pmask_sb[:, 2:3], in1=l1b_sb[:, :],
                                               op0=ALU.mult, op1=ALU.add),
              lambda e: e.scalar_tensor_tensor(out=l1b_sb[:, :], in0=gv[:, 3, :],
                                               scalar=pmask_sb[:, 3:4], in1=l1b_sb[:, :],
                                               op0=ALU.mult, op1=ALU.add))
        P.ssync("vector")
        def fo1(e):
            e.memset(vec_sb[:, 0, :], 1.0)
        P.emit("gpsimd", fo1)
        P.mark("gpsimd", s_ge)
        P.wait("vector", s_ge)
        for c in range(NC):
            P.seq("vector",
                  lambda e, c=c: e.tensor_tensor_scan(out=sq_sb[:, :], data0=vec_sb[:, 0, :],
                                                      data1=at_v[:, c, :],
                                                      initial=l1b_sb[:, c:c + 1],
                                                      op0=ALU.mult, op1=ALU.add),
                  lambda e, c=c: e.tensor_mul(out=vec_sb[:, 1, :], in0=q_v[:, c, :], in1=sq_sb[:, :]),
                  lambda e, c=c: e.tensor_add(out=x_sb[:, c, :], in0=x_sb[:, c, :],
                                              in1=vec_sb[:, 1, :]))
            P.ssync("vector")
        P.mark("vector", s_ve)

    def l1_ffn():
        xn_ok = s_ve.n
        i = wctr[0]; buf = i % 2; wctr[0] += 1
        if i >= 2:
            P.wait("gpsimd", s_mm, slab_mm_done[i - 2])
        h = s_wdma.inc(16)
        def fwr(e, buf=buf, h=h):
            src = w_rtr[:, :].rearrange("(k p) e -> p k e", p=128)
            e.dma_start(out=wsl[:, buf, :, 0:8], in_=src).then_inc(h, 16)
        P.emit("gpsimd", fwr)
        P.wait("tensor", s_wdma)
        P.wait("tensor", s_ve, xn_ok)
        def frm(e, buf=buf):
            for hf in range(2):
                for kc in range(NC):
                    e.matmul(nrm_psv[0:8, hf * 512:(hf + 1) * 512], wsl[:, buf, kc, 0:8],
                             xn_v[:, kc, hf * 512:(hf + 1) * 512],
                             start=(kc == 0), stop=(kc == NC - 1))
        P.emit("tensor", frm)
        P.mark("tensor", s_mm)
        slab_mm_done.append(s_mm.n)
        P.wait("vector", s_mm)
        def fre(e):
            e.tensor_copy(out=stg_sb[0:8, :, :].rearrange("p a b -> p (a b)")[:, 0:1024],
                          in_=nrm_psv[0:8, :])
        P.emit("vector", fre)
        P.mark("vector", s_ve)
        rlog = stg_sb[0:8, :, :].rearrange("p a b -> p (a b)")
        P.wait("tensor", s_ve)
        for tt in range(8):
            par = tt % 2
            def ftr(e, tt=tt, par=par):
                e.matmul(ps_c[:, par, 0:8], rlog[:, tt * 128:(tt + 1) * 128],
                         ident_b[0:8, 0:8], start=True, stop=True)
            P.emit("tensor", ftr)
            P.mark("tensor", s_mm)
            P.wait("vector", s_mm)
            P.seq("vector",
                  lambda e, par=par: e.tensor_copy(out=gt_sb[:, 0:8], in_=ps_c[:, par, 0:8]),
                  lambda e: e.reduce_max(out=gt_sb[:, 8:9], in_=gt_sb[:, 0:8], axis=AXX,
                                         negate=True))
            P.ssync("vector")
            P.wait("scalar", s_ve, s_ve.n + 1)
            P.mark("vector", s_ve)
            def fex(e):
                e.activation(out=gt_sb[:, 0:8], in_=gt_sb[:, 0:8], func=ACT.Exp,
                             bias=gt_sb[:, 8:9], scale=1.0)
            P.emit("scalar", fex)
            P.mark("scalar", s_ac)
            P.wait("vector", s_ac)
            P.seq("vector",
                  lambda e: e.reduce_sum(out=gt_sb[:, 9:10], in_=gt_sb[:, 0:8], axis=AXX),
                  lambda e: e.reciprocal(out=gt_sb[:, 10:11], in_=gt_sb[:, 9:10]),
                  lambda e: e.tensor_scalar(out=gt_sb[:, 0:8], in0=gt_sb[:, 0:8],
                                            scalar1=gt_sb[:, 10:11], scalar2=None, op0=ALU.mult),
                  lambda e: e.max(mx8_sb[:, 0:8], gt_sb[:, 0:8]),
                  lambda e: e.tensor_add(out=gt_sb[:, 11:12], in0=mx8_sb[:, 0:1],
                                         in1=mx8_sb[:, 1:2]),
                  lambda e: e.reciprocal(out=gt_sb[:, 12:13], in_=gt_sb[:, 11:12]),
                  lambda e: e.tensor_single_scalar(out=gt_sb[:, 13:14], in_=mx8_sb[:, 1:2],
                                                   scalar=0.0, op=ALU.bypass),
                  lambda e, tt=tt: e.tensor_single_scalar(out=msk_sb[:, tt, 0:8],
                                                          in_=gt_sb[:, 0:8],
                                                          scalar=mx8_sb[:, 1:2], op=ALU.is_ge),
                  lambda e, tt=tt: e.tensor_mul(out=msk_sb[:, tt, 0:8], in0=msk_sb[:, tt, 0:8],
                                                in1=gt_sb[:, 0:8]),
                  lambda e, tt=tt: e.tensor_scalar(out=msk_sb[:, tt, 0:8],
                                                   in0=msk_sb[:, tt, 0:8],
                                                   scalar1=gt_sb[:, 12:13], scalar2=None,
                                                   op0=ALU.mult))
            P.ssync("vector")
            P.mark("vector", s_ve)
            P.wait("tensor", s_ve)
        w8 = stg_sb[0:8, :, :].rearrange("p a b -> p (a b)")[:, 1024:2048]
        for tt in range(8):
            par = tt % 2
            def ftw(e, tt=tt, par=par):
                e.matmul(ps_c[:, 2 + par, 0:8].rearrange("p a -> p a") if False else
                         ps_c[0:8, 2 + par, 0:128], msk_sb[:, tt, 0:8], ident_f[:, :],
                         start=True, stop=True)
            P.emit("tensor", ftw)
            P.mark("tensor", s_mm)
            P.wait("vector", s_mm)
            def fwe(e, tt=tt, par=par):
                e.tensor_copy(out=w8[0:8, tt * 128:(tt + 1) * 128], in_=ps_c[0:8, 2 + par, 0:128])
            P.emit("vector", fwe)
            P.mark("vector", s_ve)
            P.wait("tensor", s_ve)
        w8_done = s_ve.n
        def hp_v(j):
            return qa_ar.ap()[:, j * 1024:(j + 1) * 1024]
        for ee in range(NE):
            P.ssync("vector")
            P.wait("tensor", s_ve)
            def fwb(e, ee=ee):
                for hf in range(2):
                    e.matmul(ps_c[:, 2 + hf, :], hr8_sb[:, ee, :], w8[:, hf * 512:(hf + 1) * 512],
                             start=True, stop=True)
            P.emit("tensor", fwb)
            P.mark("tensor", s_mm)
            P.wait("vector", s_mm)
            def fwbe(e):
                e.tensor_copy(out=wb1_sb[:, 0:512], in_=ps_c[:, 2, :])
                e.tensor_copy(out=wb1_sb[:, 512:1024], in_=ps_c[:, 3, :])
            P.emit("vector", fwbe)
            P.mark("vector", s_ve)
            for j in range(8):
                for which, col0, pst in ((0, j * 128, ps_a), (1, DI + j * 128, ps_b)):
                    i = wctr[0]; buf = i % 2; wctr[0] += 1
                    if i >= 2:
                        P.wait("gpsimd", s_mm, slab_mm_done[i - 2])
                    h = s_wdma.inc(16)
                    def fw(e, ee=ee, col0=col0, buf=buf, h=h):
                        src = w_gu[ee, :, col0:col0 + 128].rearrange("(k p) e -> p k e", p=128)
                        e.dma_start(out=wsl[:, buf, :, :], in_=src).then_inc(h, 16)
                    P.emit("gpsimd", fw)
                    P.wait("tensor", s_wdma)
                    def fmm(e, buf=buf, pst=pst):
                        for hf in range(2):
                            for kc in range(NC):
                                e.matmul(pst[:, hf, :], wsl[:, buf, kc, :],
                                         xn_v[:, kc, hf * 512:(hf + 1) * 512],
                                         start=(kc == 0), stop=(kc == NC - 1))
                    P.emit("tensor", fmm)
                    P.mark("tensor", s_mm)
                    slab_mm_done.append(s_mm.n)
                P.wait("scalar", s_mm)
                for hf in range(2):
                    def fsl(e, hf=hf):
                        e.activation(out=sg1_sb[:, hf, :], in_=ps_a[:, hf, :], func=ACT.Silu)
                    P.emit("scalar", fsl)
                P.mark("scalar", s_ac)
                P.wait("vector", s_ac)
                for hf in range(2):
                    P.seq("vector",
                          lambda e, hf=hf: e.tensor_mul(out=tm1_sb[:, hf, :], in0=ps_b[:, hf, :],
                                                        in1=wb1_sb[:, hf * 512:(hf + 1) * 512]),
                          lambda e, j=j, hf=hf: e.tensor_mul(
                              out=hp_v(j)[:, hf * 512:(hf + 1) * 512],
                              in0=tm1_sb[:, hf, :], in1=sg1_sb[:, hf, :]))
                P.mark("vector", s_ve)
                P.wait("tensor", s_ve)
            for oc in range(NC):
                i = wctr[0]; buf = i % 2; wctr[0] += 1
                if i >= 2:
                    P.wait("gpsimd", s_mm, slab_mm_done[i - 2])
                h = s_wdma.inc(16)
                def fw(e, ee=ee, oc=oc, buf=buf, h=h):
                    src = w_dn[ee, :, oc * 128:(oc + 1) * 128].rearrange("(k p) e -> p k e", p=128)
                    e.dma_start(out=wsl[:, buf, 0:8, :], in_=src).then_inc(h, 16)
                P.emit("gpsimd", fw)
                P.wait("tensor", s_wdma)
                def fmm(e, buf=buf):
                    for hf in range(2):
                        for kc in range(8):
                            e.matmul(ps_a[:, hf, :], wsl[:, buf, kc, :],
                                     hp_v(kc)[:, hf * 512:(hf + 1) * 512],
                                     start=(kc == 0), stop=(kc == 7))
                P.emit("tensor", fmm)
                P.mark("tensor", s_mm)
                slab_mm_done.append(s_mm.n)
                P.wait("vector", s_mm)
                for hf in range(2):
                    def fev(e, oc=oc, hf=hf):
                        e.scalar_tensor_tensor(out=x_sb[:, oc, hf * 512:(hf + 1) * 512],
                                               in0=ps_a[:, hf, :], scalar=1.0,
                                               in1=x_sb[:, oc, hf * 512:(hf + 1) * 512],
                                               op0=ALU.mult, op1=ALU.add)
                    P.emit("vector", fev)
                P.mark("vector", s_ve)
                P.wait("tensor", s_ve)

    def store_out():
        P.wait("sync", s_ve)
        dma("sync", y_out[:, :, :], x_sb[:, :, :], sem=s_st)
        P.wait("sync", s_st)

    load_consts()
    if N_BLOCKS >= 1:
        layernorm()
        xn_ready = s_ve.n
        cv = l0_qkv(xn_ready)
        l0_attn(cv)
        l0_out()
    if N_BLOCKS >= 2:
        layernorm()
        l0_ffn()
    if N_BLOCKS >= 3:
        layernorm()
        l1_attn()
    if N_BLOCKS >= 4:
        layernorm()
        l1_ffn()
    store_out()

    with nc.Block() as block:
        @block.sync
        def _(e):
            er = EngRec(e)
            for f in P.streams["sync"]:
                f(er)
        @block.gpsimd
        def _(e):
            er = EngRec(e)
            for f in P.streams["gpsimd"]:
                f(er)
        @block.vector
        def _(e):
            er = EngRec(e)
            for f in P.streams["vector"]:
                f(er)
        @block.scalar
        def _(e):
            er = EngRec(e)
            for f in P.streams["scalar"]:
                f(er)
        @block.tensor
        def _(e):
            er = EngRec(e)
            for f in P.streams["tensor"]:
                f(er)
    return nc


def _host_inputs(inputs):
    bf = ml_dtypes.bfloat16
    x = np.asarray(inputs["L0_attn_x"], dtype=np.float32)
    shared = {
        "w_l0q": np.ascontiguousarray(np.asarray(inputs["L0_attn_w_q"], np.float32).T),
        "w_l0k": np.ascontiguousarray(np.asarray(inputs["L0_attn_w_k"], np.float32).T),
        "w_l0v": np.ascontiguousarray(np.asarray(inputs["L0_attn_w_v"], np.float32).T),
        "w_l0o": np.ascontiguousarray(np.asarray(inputs["L0_attn_w_o"], np.float32).T),
        "w_l0g": np.ascontiguousarray(np.asarray(inputs["L0_ffn_W_gate"], np.float32).T),
        "w_l0qkv": np.ascontiguousarray(np.asarray(inputs["L0_ffn_W_qkv"], np.float32).T),
        "w_l1q": np.ascontiguousarray(np.asarray(inputs["L1_attn_Wq"], np.float32).T),
        "w_l1k": np.ascontiguousarray(np.asarray(inputs["L1_attn_Wk"], np.float32).T),
        "w_l1v": np.ascontiguousarray(np.asarray(inputs["L1_attn_Wv"], np.float32).T),
        "w_l1g": np.ascontiguousarray(np.asarray(inputs["L1_attn_Wg"], np.float32).T),
        "w_rtr": np.ascontiguousarray(np.asarray(inputs["L1_ffn_router_gate"], np.float32).T),
        "w_dn": np.ascontiguousarray(np.asarray(inputs["L1_ffn_expert_down"], np.float32)),
    }
    wgu = np.asarray(inputs["L1_ffn_expert_gate_up"], np.float32)
    shared["w_gu"] = np.ascontiguousarray(
        np.concatenate([wgu[:, :, 0::2], wgu[:, :, 1::2]], axis=2))
    bl1 = np.stack([np.asarray(inputs["L1_attn_b" + k], np.float32).reshape(16, 128).T
                    for k in ("q", "k", "v", "g")], axis=1)
    shared["b_l1"] = np.ascontiguousarray(bl1)
    shared["ident_bf"] = np.eye(128, dtype=bf)
    shared["ident_f"] = np.eye(128, dtype=np.float32)
    hc = np.zeros((128, NH, NH), dtype=bf)
    for h in range(NH):
        hc[:, h, h] = 1.0
    shared["hotcols"] = hc
    shared["iota16"] = np.broadcast_to(np.arange(16, dtype=np.float32)[None, :], (128, 16)).copy()
    hr8 = np.zeros((8, 8, 128), dtype=bf)
    for e8 in range(8):
        hr8[e8, e8, :] = 1.0
    shared["hotrows8"] = hr8

    p = np.arange(128, dtype=np.uint32)
    in_maps = []
    for c in range(NCORE):
        g, r = c // 4, c % 4
        xs = np.ascontiguousarray(
            x[g, r * T:(r + 1) * T, :].T.reshape(NC, 128, T).transpose(1, 0, 2))
        gsel = np.zeros((128, 16), dtype=np.uint32)
        for s_ in range(4):
            for b_ in range(4):
                gsel[:, s_ * 4 + b_] = b_ * 2048 + (4 * r + s_) * 128 + p
        ysel = np.zeros((128, 16), dtype=np.uint32)
        for cc in range(16):
            ysel[:, cc] = (cc // 4) * 2048 + r * 512 + (cc % 4) * 128 + p
        pmask = np.zeros((128, 4), dtype=np.float32)
        pmask[:, :r] = 1.0
        m = dict(shared)
        m.update({"x0": xs, "gsel": gsel, "ysel": ysel, "pmask": pmask})
        in_maps.append(m)
    return in_maps


def kernel(**inputs):
    in_maps = _host_inputs(inputs)
    nc = build_program()
    res = run_bass_kernel_spmd(nc, in_maps, core_ids=list(range(NCORE)))
    out = np.empty((2, S, D), dtype=np.float32)
    for c in range(NCORE):
        g, r = c // 4, c % 4
        yc = np.asarray(res.results[c]["y"])
        out[g, r * T:(r + 1) * T, :] = yc.transpose(2, 1, 0).reshape(T, D)
    return out


# version 03
# speedup vs baseline: 1.1772x; 1.1772x over previous
import numpy as np
import ml_dtypes
import concourse.bass as bass
import concourse.mybir as mybir
from concourse.bass_utils import run_bass_kernel_spmd
from concourse.bass_core_rust import IndirectOffsetOnAxis
from contextlib import ExitStack

F32 = mybir.dt.float32
BF16 = mybir.dt.bfloat16
U32 = mybir.dt.uint32
ALU = mybir.AluOpType
ACT = mybir.ActivationFunctionType
AXX = mybir.AxisListType.X

NCORE = 8
T = 1024
D = 2048
NC = 16
NH = 16
S = 4096
SCALE = float(1.0 / np.sqrt(128.0))
NE = 8
DI = 1024

N_BLOCKS = 4

RACE_CHECK = False


class Sem:
    def __init__(self, nc, name):
        self.h = nc.alloc_semaphore(name)
        self.n = 0

    def inc(self, k):
        self.n += k
        return self.h


class Prog:
    ENGINES = ("sync", "gpsimd", "vector", "scalar", "tensor")

    def __init__(self, nc):
        self.nc = nc
        self.streams = {k: [] for k in self.ENGINES}
        self.last_done = {k: None for k in self.ENGINES}

    def emit(self, eng, fn, kind="op"):
        self.streams[eng].append(fn)
        if kind == "op":
            self.last_done[eng] = None

    def wait(self, eng, sem, val=None):
        v = sem.n if val is None else val
        h = sem.h
        if v > 0:
            self.emit(eng, lambda e, h=h, v=v: e.wait_ge(h, v), kind="wait")

    def mark(self, eng, sem, k=1):
        h = sem.inc(k)
        v = k
        if self.last_done[eng] is None:
            self.emit(eng, lambda e, h=h, v=v: e.last.then_inc(h, v), kind="mark")
            self.last_done[eng] = (h, sem.n)
        else:
            h0, v0 = self.last_done[eng]
            def f(e, h=h, v=v, h0=h0, v0=v0):
                e.wait_ge(h0, v0)
                e.nop().then_inc(h, v)
            self.emit(eng, f, kind="mark")
        return sem.n

    def ssync(self, eng):
        if self.last_done[eng] is not None:
            h, v = self.last_done[eng]
            self.emit(eng, lambda e, h=h, v=v: e.wait_ge(h, v), kind="wait")
        else:
            sem = self.chain[eng]
            self.mark(eng, sem)
            self.wait(eng, sem)

    def seq(self, eng, *fns):
        for i, f in enumerate(fns):
            if i:
                self.ssync(eng)
            self.emit(eng, f)


class EngRec:
    def __init__(self, e):
        object.__setattr__(self, "_e", e)
        object.__setattr__(self, "last", None)

    _NO_RECORD = ("wait_ge", "wait_op", "sem_inc", "nop", "maybe_wait_ge")

    def __getattr__(self, name):
        f = getattr(object.__getattribute__(self, "_e"), name)
        if not callable(f):
            return f
        rec = name not in EngRec._NO_RECORD
        def call(*a, **kw):
            r = f(*a, **kw)
            if rec and hasattr(r, "then_inc"):
                object.__setattr__(self, "last", r)
            return r
        return call


def build_program():
    nc = bass.Bass("TRN2", target_bir_lowering=False, debug=False,
                   detect_race_conditions=RACE_CHECK)

    def dram_in(name, shape, dt=F32):
        return nc.dram_tensor(name, list(shape), dt, kind="ExternalInput").ap()

    x0 = dram_in("x0", [128, NC, T])
    w_l0q = dram_in("w_l0q", [16, 128, D]); w_l0k = dram_in("w_l0k", [16, 128, D])
    w_l0v = dram_in("w_l0v", [16, 128, D]); w_l0o = dram_in("w_l0o", [16, 128, D])
    w_l0g = dram_in("w_l0g", [1, 128, 16 * 16]); w_l0qkv = dram_in("w_l0qkv", [48, 128, D])
    w_l1q = dram_in("w_l1q", [16, 128, D]); w_l1k = dram_in("w_l1k", [16, 128, D])
    w_l1v = dram_in("w_l1v", [16, 128, D]); w_l1g = dram_in("w_l1g", [16, 128, D])
    b_l1 = dram_in("b_l1", [128, 4, NC])
    w_rtr = dram_in("w_rtr", [1, 128, 16 * 8])
    w_gu = dram_in("w_gu", [NE * 16, 128, D])
    w_dn = dram_in("w_dn", [NE * 16, 128, 8 * 128])
    ident_bf_in = dram_in("ident_bf", [128, 128], BF16)
    ident_f_in = dram_in("ident_f", [128, 128])
    hotcols_in = dram_in("hotcols", [128, NH, NH], BF16)
    gsel_in = dram_in("gsel", [128, 16], U32)
    ysel_in = dram_in("ysel", [128, 16], U32)
    pmask_in = dram_in("pmask", [128, 4])
    iota16_in = dram_in("iota16", [128, 16])
    hotrows8_in = dram_in("hotrows8", [8, 8, 128], BF16)

    y_out = nc.dram_tensor("y", [128, NC, T], F32, kind="ExternalOutput").ap()

    k_loc = nc.dram_tensor("k_loc", [D, T], BF16, kind="Internal").ap()
    v_loc = nc.dram_tensor("v_loc", [T, D], BF16, kind="Internal").ap()
    k_gth = nc.dram_tensor("k_gth", [4, D, T], BF16, kind="Internal").ap()
    v_gth = nc.dram_tensor("v_gth", [4, T, D], BF16, kind="Internal").ap()
    q_loc = nc.dram_tensor("q_loc", [D, T], BF16, kind="Internal").ap()
    q_gth = nc.dram_tensor("q_gth", [4 * D, T], BF16, kind="Internal").ap()
    kv_loc = nc.dram_tensor("kv_loc", [T, 256], BF16, kind="Internal").ap()
    kv_gth = nc.dram_tensor("kv_gth", [4 * T, 256], BF16, kind="Internal").ap()
    yh_loc = nc.dram_tensor("yh_loc", [4, 512, T], BF16, kind="Internal").ap()
    yh_gth = nc.dram_tensor("yh_gth", [4, 4, 512, T], BF16, kind="Internal").ap()
    pfx_loc = nc.dram_tensor("pfx_loc", [128, NC], F32, kind="Internal").ap()
    pfx_gth = nc.dram_tensor("pfx_gth", [4, 128, NC], F32, kind="Internal").ap()
    RG = [[0, 1, 2, 3], [4, 5, 6, 7]]

    ctx = ExitStack()
    sb = lambda n, s, d: ctx.enter_context(nc.sbuf_tensor(n, s, d))
    psf = lambda n, s: ctx.enter_context(nc.psum_tensor(n, s, F32))

    x_sb = sb("x_sb", [128, NC, T], F32)
    xn_ar = sb("xn_ar", [128, NC * T], BF16)
    qa_ar = sb("qa_ar", [128, NC * T], BF16)
    at_ar = sb("at_ar", [128, NC * T], BF16)
    wsl = sb("wsl", [128, 3, 16, 128], BF16)
    sq_sb = sb("sq_sb", [128, T], F32)
    sqb_sb = sb("sqb_sb", [128, T], BF16)
    vec_sb = sb("vec_sb", [128, 2, T], F32)
    ones_b = sb("ones_b", [128, 128], BF16)
    ones_f = sb("ones_f", [128, 128], F32)
    ident_b = sb("ident_b", [128, 128], BF16)
    ident_f = sb("identf_sb", [128, 128], F32)
    hot_sb = sb("hot_sb", [128, NH, NH], BF16)
    nrm_sb = sb("nrm_sb", [16, 16], F32)
    cvec_sb = sb("cvec_sb", [128, NH], F32)
    gsel_sb = sb("gsel_sb", [128, 16], U32)
    ysel_sb = sb("ysel_sb", [128, 16], U32)
    pmask_sb = sb("pmask_sb", [128, 4], F32)
    bl1_sb = sb("bl1_sb", [128, 4, NC], F32)
    stg_sb = sb("stg_sb", [128, 16, 128], BF16)
    ev_sb = sb("ev_sb", [128, 2, 512], F32)
    gt_sb = sb("gt_sb", [128, 16], F32)
    mx8_sb = sb("mx8_sb", [128, 8], F32)
    mi8_sb = sb("mi8_sb", [128, 8], U32)
    sel_sb = sb("sel_sb", [128, 8, 4], F32)
    msk_sb = sb("msk_sb", [128, 8, 16], F32)
    iota16_sb = sb("iota16_sb", [128, 16], F32)
    cb4_sb = sb("cb4_sb", [128, 4], F32)
    hr8_sb = sb("hr8_sb", [8, 8, 128], BF16)
    l1b_sb = sb("l1b_sb", [128, 16], F32)
    wb1_sb = sb("wb1_sb", [128, T], BF16)
    sg1_sb = sb("sg1_sb", [128, 2, 512], BF16)
    tm1_sb = sb("tm1_sb", [128, 2, 512], BF16)

    ps_a = psf("ps_a", [128, 2, 512])
    ps_b = psf("ps_b", [128, 2, 512])
    ps_c = psf("ps_c", [128, 4, 512])

    def arview(ar, c=NC):
        return ar.ap().rearrange("p (c t) -> p c t", c=c)
    xn_v = arview(xn_ar)
    q_v = arview(qa_ar)
    at_v = arview(at_ar)
    kh_v = xn_ar.ap()[:, 0:4096].rearrange("p (j q) -> p j q", j=32)
    vh_v = xn_ar.ap()[:, 4096:8192].rearrange("p (j h) -> p j h", j=32)
    ptA = at_ar.ap()[:, 0:1024]
    ptB = at_ar.ap()[:, 1024:2048]
    def attn_chunk(c):
        if c < 14:
            return at_ar.ap()[:, 2048 + c * 1024: 2048 + (c + 1) * 1024]
        return xn_ar.ap()[:, 8192 + (c - 14) * 1024: 8192 + (c - 13) * 1024]
    nrm_psv = ps_b[0:16, :, :].rearrange("p a t -> p (a t)")
    ksh_v = at_ar.ap()[:, 0:4096].rearrange("p (j q) -> p j q", j=32)
    vtm_v = at_ar.ap()[:, 4096:8192].rearrange("p (j q) -> p j q", j=32)
    ptfA = at_ar.ap()[:, 8192:9216]
    ptfB = at_ar.ap()[:, 9216:10240]
    ystg_v = at_ar.ap()[:, 10240:10752].rearrange("p (a t) -> p a t", a=2)
    ksel_v = at_ar.ap()[:, 10752:11776].rearrange("p (tt e) -> p tt e", tt=8)
    vsel_v = at_ar.ap()[:, 11776:12800].rearrange("p (tt e) -> p tt e", tt=8)
    ytmp_v = at_ar.ap()[:, 12800:14848].rearrange("p (a t) -> p a t", a=2)
    qmine_v = qa_ar.ap().rearrange("p (s b t) -> p s b t", s=4, b=4)
    m_psv = ps_b[0:1, 0, :]
    m2_psv = ps_b[0:1, 1, :]
    vecA = vec_sb[0:1, 0, :]
    vecB = vec_sb[0:1, 1, :]
    A_bc = ps_c.ap().rearrange("p c t -> p (c t)")[:, 0:1024]
    B_bc = ps_c.ap().rearrange("p c t -> p (c t)")[:, 1024:2048]

    s_wdma = Sem(nc, "s_wdma")
    s_mm = Sem(nc, "s_mm")
    s_ve = Sem(nc, "s_ve")
    s_ac = Sem(nc, "s_ac")
    s_ge = Sem(nc, "s_ge")
    s_io = Sem(nc, "s_io")
    s_cc = Sem(nc, "s_cc")
    s_st = Sem(nc, "s_st")
    s_ind = Sem(nc, "s_ind")

    P = Prog(nc)
    P.chain = {eng: Sem(nc, "s_chain_" + eng) for eng in Prog.ENGINES}

    def dma(eng, out, in_, sem=None, k=16):
        if sem is None:
            sem = s_io
        h = sem.inc(k)
        def f(e, out=out, in_=in_, h=h, k=k):
            e.dma_start(out=out, in_=in_).then_inc(h, k)
        P.emit(eng, f)

    def load_consts():
        dma("sync", ident_b[:], ident_bf_in[:, :])
        dma("sync", ident_f[:], ident_f_in[:, :])
        dma("sync", hot_sb[:], hotcols_in[:, :, :])
        dma("sync", gsel_sb[:], gsel_in[:, :])
        dma("sync", ysel_sb[:], ysel_in[:, :])
        dma("sync", pmask_sb[:], pmask_in[:, :])
        dma("sync", bl1_sb[:], b_l1[:, :, :])
        dma("sync", iota16_sb[:], iota16_in[:, :])
        dma("sync", hr8_sb[:], hotrows8_in[:, :, :])
        dma("sync", x_sb[:], x0[:, :, :])
        def g(e):
            e.memset(ones_b[:], 1.0)
            e.memset(ones_f[:], 1.0)
        P.emit("gpsimd", g)
        P.mark("gpsimd", s_ge)

    def layernorm():
        x_ready = s_io.n
        ve_ready = s_ve.n
        P.wait("scalar", s_io, x_ready)
        P.wait("scalar", s_ve, ve_ready)
        P.wait("tensor", s_io, x_ready)
        P.wait("tensor", s_ve, ve_ready)
        P.wait("tensor", s_ge, 1)
        sqm = []
        for c in range(NC):
            if c >= 1:
                P.wait("scalar", s_mm, sqm[c - 1])
            def fsq(e, c=c):
                e.activation(out=sq_sb[:, :], in_=x_sb[:, c, :], func=ACT.Square)
            P.emit("scalar", fsq)
            P.mark("scalar", s_ac)
            P.wait("tensor", s_ac)
            def fmm(e, c=c):
                for hf in range(2):
                    e.matmul(ps_c[0:1, hf, :], ones_f[:, 0:1],
                             x_sb[:, c, hf * 512:(hf + 1) * 512],
                             start=(c == 0), stop=(c == NC - 1))
                    e.matmul(ps_c[0:1, 2 + hf, :], ones_f[:, 0:1],
                             sq_sb[:, hf * 512:(hf + 1) * 512],
                             start=(c == 0), stop=(c == NC - 1))
            P.emit("tensor", fmm)
            sqm.append(P.mark("tensor", s_mm))
        P.wait("vector", s_mm)
        mfv = ps_c[0:1, :, :].rearrange("p a t -> p (a t)")
        P.seq("vector",
              lambda e: e.tensor_scalar_mul(out=vecA[:], in0=mfv[:, 0:1024], scalar1=1.0 / D),
              lambda e: e.tensor_mul(out=vecB[:], in0=vecA[:], in1=vecA[:]),
              lambda e: e.scalar_tensor_tensor(out=vecB[:], in0=mfv[:, 1024:2048],
                                               scalar=1.0 / float(D), in1=vecB[:],
                                               op0=ALU.mult, op1=ALU.subtract),
              lambda e: e.tensor_scalar_add(out=vecB[:], in0=vecB[:], scalar1=1e-5))
        P.mark("vector", s_ve)
        P.wait("scalar", s_ve)
        def fsr(e):
            e.activation(out=sq_sb[0:1, 0:1024], in_=vecB[:], func=ACT.Sqrt)
        P.emit("scalar", fsr)
        P.mark("scalar", s_ac)
        P.wait("vector", s_ac)
        P.seq("vector",
              lambda e: e.reciprocal(out=vecB[:], in_=sq_sb[0:1, 0:1024]),
              lambda e: e.tensor_mul(out=vecA[:], in0=vecA[:], in1=vecB[:]))
        P.mark("vector", s_ve)
        P.wait("tensor", s_ve)
        def fbc(e):
            for hf in range(2):
                e.matmul(ps_c[:, hf, :], ones_f[0:1, :], vecB[0:1, hf * 512:(hf + 1) * 512],
                         start=True, stop=True)
                e.matmul(ps_c[:, 2 + hf, :], ones_f[0:1, :], vecA[0:1, hf * 512:(hf + 1) * 512],
                         start=True, stop=True)
        P.emit("tensor", fbc)
        P.mark("tensor", s_mm)
        P.wait("vector", s_mm)
        for c in range(NC):
            P.seq("vector",
                  lambda e, c=c: e.tensor_mul(out=sq_sb[:, :], in0=x_sb[:, c, :], in1=A_bc),
                  lambda e, c=c: e.tensor_sub(out=xn_v[:, c, :], in0=sq_sb[:, :], in1=B_bc))
            P.ssync("vector")
        P.mark("vector", s_ve)

    wctr = [0]
    slab_mm_done = []

    def projection(w_dram, col0, n_out, rhs_fn, evac_fn, nkc=NC, xn_ready=None,
                   hook=None):
        if xn_ready is not None:
            P.wait("tensor", s_ve, xn_ready)
        for c in range(n_out):
            i = wctr[0]
            buf = i % 3
            wctr[0] += 1
            if i >= 3:
                P.wait("gpsimd", s_mm, slab_mm_done[i - 3])
            h = s_wdma.inc(16)
            def fw(e, c=c, buf=buf, h=h, w_dram=w_dram, col0=col0, nkc=nkc):
                sl = col0 // 128 + c
                e.dma_start(out=wsl[:, buf, 0:nkc, :],
                            in_=w_dram[sl, :, :].rearrange("p (k e) -> p k e", k=nkc)
                            ).then_inc(h, 16)
            P.emit("gpsimd", fw)
            P.wait("tensor", s_wdma)
            for hf in range(2):
                def fmm(e, c=c, buf=buf, hf=hf, rhs_fn=rhs_fn, nkc=nkc):
                    for kc in range(nkc):
                        e.matmul(ps_a[:, hf, :], wsl[:, buf, kc, :], rhs_fn(kc, hf),
                                 start=(kc == 0), stop=(kc == nkc - 1))
                P.emit("tensor", fmm)
                P.mark("tensor", s_mm)
                P.wait("vector", s_mm)
                evac_fn(c, hf)
                P.mark("vector", s_ve)
                P.wait("tensor", s_ve)
            if hook is not None:
                hook(c)
            slab_mm_done.append(s_mm.n)

    def xn_rhs(kc, hf):
        return xn_v[:, kc, hf * 512:(hf + 1) * 512]

    def l0_qkv(xn_ready):
        def evq(c, hf):
            def f(e, c=c, hf=hf):
                e.tensor_copy(out=q_v[:, c, hf * 512:(hf + 1) * 512], in_=ps_a[:, hf, :])
            P.emit("vector", f)
        projection(w_l0q, 0, NC, xn_rhs, evq, xn_ready=xn_ready)
        def evk(c, hf):
            def f(e, c=c, hf=hf):
                e.tensor_copy(out=at_v[:, c, hf * 512:(hf + 1) * 512], in_=ps_a[:, hf, :])
            P.emit("vector", f)
        projection(w_l0k, 0, NC, xn_rhs, evk)
        k_done = s_ve.n
        P.wait("sync", s_ve, k_done)
        dma("sync", k_loc.rearrange("(c p) t -> p c t", p=128), at_v[:, :, :])
        k_dma = s_io.n
        sqm = []
        for c in range(NC):
            P.wait("scalar", s_ve, k_done)
            if c >= 1:
                P.wait("scalar", s_mm, sqm[c - 1])
            def fsq(e, c=c):
                e.activation(out=sqb_sb[:, :], in_=at_v[:, c, :], func=ACT.Square)
            P.emit("scalar", fsq)
            P.mark("scalar", s_ac)
            P.wait("tensor", s_ac)
            def fnm(e, c=c):
                for hf in range(2):
                    e.matmul(nrm_psv[:, hf * 512:(hf + 1) * 512], hot_sb[:, c, :],
                             sqb_sb[:, hf * 512:(hf + 1) * 512],
                             start=(c == 0), stop=(c == NC - 1))
            P.emit("tensor", fnm)
            sqm.append(P.mark("tensor", s_mm))
        P.wait("vector", s_mm)
        def fkn(e):
            e.reduce_max(out=nrm_sb[:, 0:1], in_=nrm_psv[:, :], axis=AXX)
        P.emit("vector", fkn)
        kn_done = P.mark("vector", s_ve)
        P.wait("vector", s_io, k_dma)
        def evv(c, hf):
            def f(e, c=c, hf=hf):
                e.tensor_copy(out=at_v[:, c, hf * 512:(hf + 1) * 512], in_=ps_a[:, hf, :])
            P.emit("vector", f)
        projection(w_l0v, 0, NC, xn_rhs, evv)
        tp = 0
        for c in range(NC):
            for tt in range(8):
                par = tp % 2
                tp += 1
                def ftr(e, c=c, tt=tt, par=par):
                    e.matmul(ps_c[:, par, 0:128], at_v[:, c, tt * 128:(tt + 1) * 128],
                             ident_b[:], start=True, stop=True)
                P.emit("tensor", ftr)
                P.mark("tensor", s_mm)
                P.wait("vector", s_mm)
                def fev(e, tt=tt, par=par):
                    e.tensor_copy(out=stg_sb[:, tt, :], in_=ps_c[:, par, 0:128])
                P.emit("vector", fev)
                P.mark("vector", s_ve)
                if tp >= 2:
                    P.wait("tensor", s_ve)
            P.wait("sync", s_ve)
            dma("sync", v_loc.rearrange("(tt p) e -> p tt e", p=128)[:, :, c * 128:(c + 1) * 128],
                stg_sb[:, 0:8, :])
            P.wait("vector", s_io)
        v_dma = s_io.n
        sqm2 = []
        for c in range(NC):
            if c == 0:
                P.wait("tensor", s_ve, kn_done)
            if c >= 1:
                P.wait("scalar", s_mm, sqm2[c - 1])
            def fsq(e, c=c):
                e.activation(out=sqb_sb[:, :], in_=q_v[:, c, :], func=ACT.Square)
            P.emit("scalar", fsq)
            P.mark("scalar", s_ac)
            P.wait("tensor", s_ac)
            def fnm(e, c=c):
                for hf in range(2):
                    e.matmul(nrm_psv[:, hf * 512:(hf + 1) * 512], hot_sb[:, c, :],
                             sqb_sb[:, hf * 512:(hf + 1) * 512],
                             start=(c == 0), stop=(c == NC - 1))
            P.emit("tensor", fnm)
            sqm2.append(P.mark("tensor", s_mm))
        P.wait("vector", s_mm)
        P.seq("vector",
              lambda e: e.reduce_max(out=nrm_sb[:, 1:2], in_=nrm_psv[:, :], axis=AXX),
              lambda e: e.tensor_mul(out=nrm_sb[:, 2:3], in0=nrm_sb[:, 0:1], in1=nrm_sb[:, 1:2]))
        P.mark("vector", s_ve)
        P.wait("scalar", s_ve)
        def fsr(e):
            e.activation(out=nrm_sb[:, 3:4], in_=nrm_sb[:, 2:3], func=ACT.Sqrt)
        P.emit("scalar", fsr)
        P.mark("scalar", s_ac)
        P.wait("vector", s_ac)
        P.seq("vector",
              lambda e: e.tensor_scalar_mul(out=nrm_sb[:, 4:5], in0=nrm_sb[:, 3:4], scalar1=-SCALE),
              lambda e: e.tensor_scalar(out=ev_sb[0:16, 0, 0:16], in0=ident_f[0:16, 0:16],
                                        scalar1=nrm_sb[:, 4:5], scalar2=None, op0=ALU.mult))
        P.mark("vector", s_ve)
        P.wait("tensor", s_ve)
        def fcb(e):
            e.matmul(ps_b[:, 1, 0:16], ones_f[0:16, :], ev_sb[0:16, 0, 0:16],
                     start=True, stop=True)
        P.emit("tensor", fcb)
        P.mark("tensor", s_mm)
        P.wait("vector", s_mm)
        def fcv(e):
            e.tensor_copy(out=cvec_sb[:, :], in_=ps_b[:, 1, 0:16])
        P.emit("vector", fcv)
        cvec_done = P.mark("vector", s_ve)
        P.wait("gpsimd", s_io, max(k_dma, v_dma))
        h1 = s_cc.inc(1)
        def fcc1(e, h1=h1):
            e.collective_compute("AllGather", ALU.bypass, replica_groups=RG,
                                 ins=[k_loc[:, :]], outs=[k_gth[:, :, :]]).then_inc(h1, 1)
        P.emit("gpsimd", fcc1)
        h2 = s_cc.inc(1)
        def fcc2(e, h2=h2):
            e.collective_compute("AllGather", ALU.bypass, replica_groups=RG,
                                 ins=[v_loc[:, :]], outs=[v_gth[:, :, :]]).then_inc(h2, 1)
        P.emit("gpsimd", fcc2)
        return cvec_done

    def l0_attn(cvec_done):
        cc_done = s_cc.n
        P.wait("sync", s_cc, cc_done)
        P.wait("scalar", s_ve, cvec_done)
        for h in range(NH):
            for sl in range(4):
                dma("sync", kh_v.rearrange("p j q -> p (j q)")[:, sl * 1024:(sl + 1) * 1024],
                    k_gth[sl, h * 128:(h + 1) * 128, :])
            for sl in range(4):
                dma("sync",
                    vh_v.rearrange("p j h2 -> p (j h2)")[:, sl * 1024:(sl + 1) * 1024]
                    .rearrange("p (tt e) -> p tt e", tt=8),
                    v_gth[sl].rearrange("(tt p) e -> p tt e", p=128)[:, :, h * 128:(h + 1) * 128])
            ld_done = s_io.n
            P.wait("tensor", s_io, ld_done)
            P.wait("scalar", s_io, ld_done)
            for hf in range(2):
                def fst(e, h=h, hf=hf, jc=None, par=None):
                    for jj in range(2):
                        j = jc * 2 + jj
                        e.matmul(ps_c[:, par * 2 + jj, :], kh_v[:, j, :],
                                 q_v[:, h, hf * 512:(hf + 1) * 512],
                                 start=True, stop=True)
                import functools
                P.emit("tensor", functools.partial(fst, jc=0, par=0))
                st_marks = [P.mark("tensor", s_mm)]
                for jc in range(16):
                    par = jc % 2
                    if jc + 1 < 16:
                        P.emit("tensor", functools.partial(fst, jc=jc + 1, par=(jc + 1) % 2))
                        st_marks.append(P.mark("tensor", s_mm))
                    P.wait("scalar", s_mm, st_marks[jc])
                    def fex(e, h=h, par=par):
                        src = ps_c[:, par * 2:par * 2 + 2, :].rearrange("p a t -> p (a t)")
                        dst = ptA if par == 0 else ptB
                        e.activation(out=dst, in_=src, func=ACT.Exp,
                                     bias=cvec_sb[:, h:h + 1], scale=SCALE)
                    P.emit("scalar", fex)
                    P.mark("scalar", s_ac)
                    P.wait("tensor", s_ac)
                    def fpv(e, hf=hf, jc=jc, par=par):
                        for jj in range(2):
                            j = jc * 2 + jj
                            pt = (ptA if par == 0 else ptB)[:, jj * 512:(jj + 1) * 512]
                            e.matmul(ps_a[:, 0, :], vh_v[:, j, :], pt,
                                     start=(j == 0), stop=(j == 31))
                            e.matmul(ps_a[:, 1, :], ones_b[:], pt,
                                     start=(j == 0), stop=(j == 31))
                    P.emit("tensor", fpv)
                    P.mark("tensor", s_mm)
                P.wait("vector", s_mm)
                P.seq("vector",
                      lambda e: e.reciprocal(out=ev_sb[:, 0, :], in_=ps_a[:, 1, :]),
                      lambda e, h=h, hf=hf: e.tensor_mul(
                          out=attn_chunk(h)[:, hf * 512:(hf + 1) * 512],
                          in0=ps_a[:, 0, :], in1=ev_sb[:, 0, :]))
                P.mark("vector", s_ve)
                P.wait("tensor", s_ve)
            P.wait("sync", s_mm)

    def l0_ffn():
        xn_ok = s_ve.n
        gbuf = wctr[0] % 3
        if wctr[0] >= 3:
            P.wait("gpsimd", s_mm, slab_mm_done[wctr[0] - 3])
        h = s_wdma.inc(16)
        def fwg(e, h=h, gbuf=gbuf):
            e.dma_start(out=wsl[:, gbuf, :, 0:16],
                        in_=w_l0g[0, :, :].rearrange("p (k e) -> p k e", k=16)).then_inc(h, 16)
        P.emit("gpsimd", fwg)
        P.wait("tensor", s_wdma)
        P.wait("tensor", s_ve, xn_ok)
        def fgm(e, gbuf=gbuf):
            for hf in range(2):
                for kc in range(NC):
                    e.matmul(nrm_psv[:, hf * 512:(hf + 1) * 512], wsl[:, gbuf, kc, 0:16],
                             xn_v[:, kc, hf * 512:(hf + 1) * 512],
                             start=(kc == 0), stop=(kc == NC - 1))
        P.emit("tensor", fgm)
        P.mark("tensor", s_mm)
        wctr[0] += 3
        slab_mm_done.extend([s_mm.n, s_mm.n, s_mm.n])
        P.wait("vector", s_mm)
        def fge(e):
            e.tensor_copy(out=stg_sb[0:16, :, :].rearrange("p a b -> p (a b)")[:, 0:1024],
                          in_=nrm_psv[:, :])
        P.emit("vector", fge)
        P.mark("vector", s_ve)
        glog = stg_sb[0:16, :, :].rearrange("p a b -> p (a b)")
        P.wait("tensor", s_ve)
        for tt in range(8):
            par = tt % 2
            def ftr(e, tt=tt, par=par):
                e.matmul(ps_c[:, par, 0:16], glog[:, tt * 128:(tt + 1) * 128],
                         ident_b[0:16, 0:16], start=True, stop=True)
            P.emit("tensor", ftr)
            P.mark("tensor", s_mm)
            P.wait("vector", s_mm)
            P.seq("vector",
                  lambda e, par=par: e.tensor_copy(out=gt_sb[:, :], in_=ps_c[:, par, 0:16]),
                  lambda e: e.max(mx8_sb[:, 0:8], gt_sb[:, :]),
                  lambda e: e.max_index(mi8_sb[:, 0:8], mx8_sb[:, 0:8], gt_sb[:, :]))
            P.ssync("vector")
            def frt(e, tt=tt):
                e.tensor_copy(out=sel_sb[:, tt, 0:1], in_=mi8_sb[:, 0:1])
                e.reduce_max(out=sel_sb[:, tt, 1:2], in_=gt_sb[:, :], axis=AXX, negate=True)
            P.emit("vector", frt)
            P.mark("vector", s_ve)
            P.wait("scalar", s_ve)
            def fex(e, tt=tt):
                e.activation(out=gt_sb[:, :], in_=gt_sb[:, :], func=ACT.Exp,
                             bias=sel_sb[:, tt, 1:2], scale=1.0)
            P.emit("scalar", fex)
            P.mark("scalar", s_ac)
            P.wait("vector", s_ac)
            P.seq("vector",
                  lambda e, tt=tt: e.reduce_sum(out=sel_sb[:, tt, 2:3], in_=gt_sb[:, :], axis=AXX),
                  lambda e, tt=tt: e.reciprocal(out=sel_sb[:, tt, 3:4], in_=sel_sb[:, tt, 2:3]),
                  lambda e, tt=tt: e.tensor_single_scalar(out=msk_sb[:, tt, :], in_=iota16_sb[:, :],
                                                          scalar=sel_sb[:, tt, 0:1], op=ALU.is_equal))
            P.mark("vector", s_ve)
            P.wait("tensor", s_ve)
        msk_done = s_ve.n
        qstg = at_ar.ap()[:, 0:2048]
        qdma = []
        def evqf(c, hf):
            slot = c % 2
            def f(e, c=c, hf=hf, slot=slot):
                e.tensor_copy(out=qstg[:, slot * 1024 + hf * 512: slot * 1024 + (hf + 1) * 512],
                              in_=ps_a[:, hf, :])
            P.emit("vector", f)
            if hf == 1:
                P.wait("sync", s_ve, s_ve.n + 1)
                dma("sync", q_loc[c * 128:(c + 1) * 128, :],
                    qstg[:, slot * 1024:(slot + 1) * 1024])
                qdma.append(s_io.n)
        def evqf2(c, hf):
            if c >= 2 and hf == 0:
                P.wait("vector", s_io, qdma[c - 2])
            evqf(c, hf)
        projection(w_l0qkv, 0, NC, xn_rhs, evqf2)
        kvt = at_ar.ap()[:, 0:1024]
        tsel = at_ar.ap()[:, 2048:3072].rearrange("p (tt e) -> p tt e", tt=8)
        for kv_i, (colbase, dst) in enumerate(((2048, ksel_v), (4096, vsel_v))):
            if kv_i == 0:
                P.wait("vector", s_io)
                P.wait("vector", s_ve, msk_done)
            def evkv(c, hf, kvt=kvt):
                def f(e, c=c, hf=hf):
                    e.tensor_copy(out=kvt[:, hf * 512:(hf + 1) * 512], in_=ps_a[:, hf, :])
                P.emit("vector", f)
            def hook(c):
                for tt in range(8):
                    par = tt % 2
                    P.wait("tensor", s_ve)
                    def ftr(e, tt=tt, par=par, kvt=kvt):
                        e.matmul(ps_c[:, 2 + par, 0:128], kvt[:, tt * 128:(tt + 1) * 128],
                                 ident_b[:], start=True, stop=True)
                    P.emit("tensor", ftr)
                    P.mark("tensor", s_mm)
                    P.wait("vector", s_mm)
                    def fev(e, tt=tt, par=par):
                        e.tensor_copy(out=tsel[:, tt, :], in_=ps_c[:, 2 + par, 0:128])
                    P.emit("vector", fev)
                    P.mark("vector", s_ve)
                P.ssync("vector")
                def facc(e, c=c, dst=dst):
                    for tt in range(8):
                        if c == 0:
                            e.tensor_scalar(out=dst[:, tt, :], in0=tsel[:, tt, :],
                                            scalar1=msk_sb[:, tt, c:c + 1], scalar2=None,
                                            op0=ALU.mult)
                        else:
                            e.scalar_tensor_tensor(out=dst[:, tt, :], in0=tsel[:, tt, :],
                                                   scalar=msk_sb[:, tt, c:c + 1],
                                                   in1=dst[:, tt, :],
                                                   op0=ALU.mult, op1=ALU.add)
                P.emit("vector", facc)
                P.mark("vector", s_ve)
            projection(w_l0qkv, colbase, NC, xn_rhs, evkv, hook=hook)
        P.ssync("vector")
        def fvt1(e):
            for tt in range(8):
                e.tensor_scalar(out=vsel_v[:, tt, :], in0=vsel_v[:, tt, :],
                                scalar1=sel_sb[:, tt, 3:4], scalar2=None, op0=ALU.mult)
        P.emit("vector", fvt1)
        P.mark("vector", s_ve)
        P.wait("sync", s_ve)
        dma("sync", kv_loc.rearrange("(tt p) e -> p tt e", p=128)[:, :, 0:128], ksel_v[:, :, :])
        dma("sync", kv_loc.rearrange("(tt p) e -> p tt e", p=128)[:, :, 128:256], vsel_v[:, :, :])
        kv_dma = s_io.n
        P.wait("gpsimd", s_io, kv_dma)
        hq = s_cc.inc(1)
        def fccq(e, hq=hq):
            e.collective_compute("AllGather", ALU.bypass, replica_groups=RG,
                                 ins=[q_loc[:, :]], outs=[q_gth[:, :]]).then_inc(hq, 1)
        P.emit("gpsimd", fccq)
        hkv = s_cc.inc(1)
        def fcckv(e, hkv=hkv):
            e.collective_compute("AllGather", ALU.bypass, replica_groups=RG,
                                 ins=[kv_loc[:, :]], outs=[kv_gth[:, :]]).then_inc(hkv, 1)
        P.emit("gpsimd", fcckv)
        cc_done = s_cc.n
        P.wait("sync", s_cc, cc_done)
        for j in range(32):
            hdt = s_io.inc(16)
            def ftd(e, j=j, hdt=hdt):
                e.dma_start(out=ksh_v[:, j, :], in_=kv_gth[j * 128:(j + 1) * 128, 0:128],
                            transpose=True).then_inc(hdt, 16)
            P.emit("sync", ftd)
        for j in range(32):
            dma("sync", vtm_v[:, j, :], kv_gth[j * 128:(j + 1) * 128, 128:256])
        P.wait("gpsimd", s_cc, cc_done)
        for col in range(16):
            hgi = s_ind.inc(16)
            def fgi(e, col=col, hgi=hgi):
                e.indirect_dma_start(out=qmine_v[:, col // 4, col % 4, :], out_offset=None,
                                     in_=q_gth[:, :],
                                     in_offset=IndirectOffsetOnAxis(gsel_sb[:, col:col + 1], 0)
                                     ).then_inc(hgi, 16)
            P.emit("gpsimd", fgi)
        ld_done = s_io.n
        qm_done = s_ind.n
        P.wait("scalar", s_io, ld_done)
        P.wait("scalar", s_ind, qm_done)
        P.wait("tensor", s_ind, qm_done)
        segs = []
        for seg in range(4):
            if seg >= 1:
                P.wait("scalar", s_mm)
            def fsq(e, seg=seg):
                e.activation(out=sqb_sb[:, :],
                             in_=ksh_v.rearrange("p j q -> p (j q)")[:, seg * 1024:(seg + 1) * 1024],
                             func=ACT.Square)
            P.emit("scalar", fsq)
            P.mark("scalar", s_ac)
            P.wait("tensor", s_ac)
            def fnm(e):
                for hf in range(2):
                    e.matmul(ps_b[0:1, hf, :], ones_b[:, 0:1], sqb_sb[:, hf * 512:(hf + 1) * 512],
                             start=True, stop=True)
            P.emit("tensor", fnm)
            P.mark("tensor", s_mm)
            P.wait("vector", s_mm)
            if seg == 0:
                def frd(e):
                    e.reduce_max(out=nrm_sb[0:1, 0:1],
                                 in_=ps_b[0:1, :, :].rearrange("p a t -> p (a t)"), axis=AXX)
                P.emit("vector", frd)
            else:
                P.seq("vector",
                      lambda e: e.reduce_max(out=nrm_sb[0:1, 1:2],
                                             in_=ps_b[0:1, :, :].rearrange("p a t -> p (a t)"),
                                             axis=AXX),
                      lambda e: e.tensor_max(out=nrm_sb[0:1, 0:1], in0=nrm_sb[0:1, 0:1],
                                             in1=nrm_sb[0:1, 1:2]))
            P.mark("vector", s_ve)
            P.wait("tensor", s_ve)
            segs.append(s_ve.n)
        for sl in range(4):
            for seg in range(4):
                P.wait("scalar", s_mm)
                def fsq(e, sl=sl, seg=seg):
                    e.activation(out=sqb_sb[:, :], in_=qmine_v[:, sl, seg, :], func=ACT.Square)
                P.emit("scalar", fsq)
                P.mark("scalar", s_ac)
                P.wait("tensor", s_ac)
                def fnm(e):
                    for hf in range(2):
                        e.matmul(ps_b[0:1, hf, :], ones_b[:, 0:1],
                                 sqb_sb[:, hf * 512:(hf + 1) * 512], start=True, stop=True)
                P.emit("tensor", fnm)
                P.mark("tensor", s_mm)
                P.wait("vector", s_mm)
                if seg == 0:
                    def frd(e, sl=sl):
                        e.reduce_max(out=nrm_sb[0:1, 2 + sl:3 + sl],
                                     in_=ps_b[0:1, :, :].rearrange("p a t -> p (a t)"), axis=AXX)
                    P.emit("vector", frd)
                else:
                    P.seq("vector",
                          lambda e: e.reduce_max(out=nrm_sb[0:1, 6:7],
                                                 in_=ps_b[0:1, :, :].rearrange("p a t -> p (a t)"),
                                                 axis=AXX),
                          lambda e, sl=sl: e.tensor_max(out=nrm_sb[0:1, 2 + sl:3 + sl],
                                                        in0=nrm_sb[0:1, 2 + sl:3 + sl],
                                                        in1=nrm_sb[0:1, 6:7]))
                P.mark("vector", s_ve)
                P.wait("tensor", s_ve)
        P.ssync("vector")
        def fcs(e):
            for sl in range(4):
                e.tensor_mul(out=nrm_sb[0:1, 8 + sl:9 + sl], in0=nrm_sb[0:1, 0:1],
                             in1=nrm_sb[0:1, 2 + sl:3 + sl])
        P.emit("vector", fcs)
        P.mark("vector", s_ve)
        P.wait("scalar", s_ve)
        def fsr(e):
            e.activation(out=nrm_sb[0:1, 12:16], in_=nrm_sb[0:1, 8:12], func=ACT.Sqrt)
        P.emit("scalar", fsr)
        P.mark("scalar", s_ac)
        P.wait("vector", s_ac)
        def fng(e):
            e.tensor_scalar_mul(out=nrm_sb[0:1, 12:16], in0=nrm_sb[0:1, 12:16], scalar1=-SCALE)
        P.emit("vector", fng)
        P.mark("vector", s_ve)
        P.wait("tensor", s_ve)
        def fbb(e):
            e.matmul(ps_b[:, 1, 0:4], ones_f[0:1, :], nrm_sb[0:1, 12:16], start=True, stop=True)
        P.emit("tensor", fbb)
        P.mark("tensor", s_mm)
        P.wait("vector", s_mm)
        def fcv(e):
            e.tensor_copy(out=cb4_sb[:, :], in_=ps_b[:, 1, 0:4])
        P.emit("vector", fcv)
        cb_done = P.mark("vector", s_ve)
        P.wait("scalar", s_ve, cb_done)
        ydma = []
        for sl in range(4):
            for pi in range(16):
                slab = (2 * pi) // 8
                off = ((2 * pi) % 8) * 128
                njs = 2 * pi + 2
                nch = (njs + 3) // 4
                import functools
                def fstf(e, sl=sl, slab=slab, off=off, par=None, j0=None, jn=None):
                    for jj in range(jn):
                        j = j0 + jj
                        e.matmul(ps_c[:, par * 2 + jj // 2, (jj % 2) * 256:(jj % 2 + 1) * 256],
                                 ksh_v[:, j, :], qmine_v[:, sl, slab, off:off + 256],
                                 start=True, stop=True)
                P.emit("tensor", functools.partial(fstf, par=0, j0=0, jn=min(4, njs)))
                stf_marks = [P.mark("tensor", s_mm)]
                for ch in range(nch):
                    par = ch % 2
                    j0 = ch * 4
                    jn = min(4, njs - j0)
                    if ch + 1 < nch:
                        P.emit("tensor", functools.partial(
                            fstf, par=(ch + 1) % 2, j0=(ch + 1) * 4,
                            jn=min(4, njs - (ch + 1) * 4)))
                        stf_marks.append(P.mark("tensor", s_mm))
                    P.wait("scalar", s_mm, stf_marks[ch])
                    def fex(e, sl=sl, par=par, jn=jn):
                        src = ps_c[:, par * 2:par * 2 + 2, :].rearrange("p a t -> p (a t)")[:, 0:jn * 256]
                        dst = (ptfA if par == 0 else ptfB)[:, 0:jn * 256]
                        e.activation(out=dst, in_=src, func=ACT.Exp,
                                     bias=cb4_sb[:, sl:sl + 1], scale=SCALE)
                    P.emit("scalar", fex)
                    P.mark("scalar", s_ac)
                    has_mask = (j0 + jn == njs)
                    if has_mask:
                        P.wait("gpsimd", s_ac)
                        def fmk(e, par=par, jn=jn):
                            buf = ptfA if par == 0 else ptfB
                            e.affine_select(out=buf[:, (jn - 2) * 256:(jn - 1) * 256],
                                            in_=buf[:, (jn - 2) * 256:(jn - 1) * 256],
                                            pattern=[[1, 256]], compare_op=ALU.is_ge,
                                            fill=0.0, base=0, channel_multiplier=-1)
                            e.affine_select(out=buf[:, (jn - 1) * 256:jn * 256],
                                            in_=buf[:, (jn - 1) * 256:jn * 256],
                                            pattern=[[1, 256]], compare_op=ALU.is_ge,
                                            fill=0.0, base=-128, channel_multiplier=-1)
                        P.emit("gpsimd", fmk)
                        P.mark("gpsimd", s_ge)
                        P.wait("tensor", s_ge)
                    else:
                        P.wait("tensor", s_ac)
                    def fpv(e, par=par, j0=j0, jn=jn, njs=njs):
                        buf = ptfA if par == 0 else ptfB
                        for jj in range(jn):
                            j = j0 + jj
                            pt = buf[:, jj * 256:(jj + 1) * 256]
                            e.matmul(ps_a[:, 0, 0:256], vtm_v[:, j, :], pt,
                                     start=(j == 0), stop=(j == njs - 1))
                            e.matmul(ps_a[:, 1, 0:256], ones_b[:], pt,
                                     start=(j == 0), stop=(j == njs - 1))
                    P.emit("tensor", fpv)
                    P.mark("tensor", s_mm)
                yslot = pi % 2
                if len(ydma) >= 2:
                    P.wait("vector", s_io, ydma[-2])
                P.wait("vector", s_mm)
                P.seq("vector",
                      lambda e: e.reciprocal(out=ev_sb[:, 0, 0:256], in_=ps_a[:, 1, 0:256]),
                      lambda e, yslot=yslot: e.tensor_mul(out=ystg_v[:, yslot, :],
                                                          in0=ps_a[:, 0, 0:256],
                                                          in1=ev_sb[:, 0, 0:256]))
                P.mark("vector", s_ve)
                P.wait("tensor", s_ve)
                P.wait("sync", s_ve)
                dma("sync", yh_loc[slab, sl * 128:(sl + 1) * 128, off:off + 256],
                    ystg_v[:, yslot, :])
                ydma.append(s_io.n)
        yh_done = s_io.n
        P.wait("gpsimd", s_io, yh_done)
        hy = s_cc.inc(1)
        def fccy(e, hy=hy):
            e.collective_compute("AllGather", ALU.bypass, replica_groups=RG,
                                 ins=[yh_loc[:, :, :]], outs=[yh_gth[:, :, :, :]]).then_inc(hy, 1)
        P.emit("gpsimd", fccy)
        ycc = s_cc.n
        P.wait("gpsimd", s_cc, ycc)
        yflat = yh_gth.rearrange("r s h t -> (r s h) t")
        P.ssync("vector")
        add_marks = []
        for c in range(NC):
            if c >= 2:
                P.wait("gpsimd", s_ve, add_marks[c - 2])
            hgi = s_ind.inc(16)
            def fgy(e, c=c, hgi=hgi):
                e.indirect_dma_start(out=ytmp_v[:, c % 2, :], out_offset=None,
                                     in_=yflat,
                                     in_offset=IndirectOffsetOnAxis(ysel_sb[:, c:c + 1], 0)
                                     ).then_inc(hgi, 16)
            P.emit("gpsimd", fgy)
            P.wait("vector", s_ind)
            def fad(e, c=c):
                e.tensor_add(out=x_sb[:, c, :], in0=x_sb[:, c, :], in1=ytmp_v[:, c % 2, :])
            P.emit("vector", fad)
            add_marks.append(P.mark("vector", s_ve))

    def l0_out():
        P.ssync("vector")
        def attn_rhs(kc, hf):
            return attn_chunk(kc)[:, hf * 512:(hf + 1) * 512]
        def evo(c, hf):
            def f(e, c=c, hf=hf):
                e.scalar_tensor_tensor(out=x_sb[:, c, hf * 512:(hf + 1) * 512],
                                       in0=ps_a[:, hf, :], scalar=1.0,
                                       in1=x_sb[:, c, hf * 512:(hf + 1) * 512],
                                       op0=ALU.mult, op1=ALU.add)
            P.emit("vector", f)
        projection(w_l0o, 0, NC, attn_rhs, evo)


    def l1_attn():
        P.ssync("vector")
        def evq1(c, hf):
            def f(e, c=c, hf=hf):
                e.tensor_scalar(out=q_v[:, c, hf * 512:(hf + 1) * 512], in0=ps_a[:, hf, :],
                                scalar1=bl1_sb[:, 0, c:c + 1], scalar2=None, op0=ALU.add)
            P.emit("vector", f)
        projection(w_l1q, 0, NC, xn_rhs, evq1, xn_ready=s_ve.n)
        def evk1(c, hf):
            def f(e, c=c, hf=hf):
                e.tensor_scalar(out=at_v[:, c, hf * 512:(hf + 1) * 512], in0=ps_a[:, hf, :],
                                scalar1=bl1_sb[:, 1, c:c + 1], scalar2=None, op0=ALU.add)
            P.emit("vector", f)
        projection(w_l1k, 0, NC, xn_rhs, evk1)
        def evv1(c, hf):
            def f(e, c=c, hf=hf):
                e.scalar_tensor_tensor(out=at_v[:, c, hf * 512:(hf + 1) * 512],
                                       in0=ps_a[:, hf, :], scalar=bl1_sb[:, 2, c:c + 1],
                                       in1=at_v[:, c, hf * 512:(hf + 1) * 512],
                                       op0=ALU.add, op1=ALU.mult)
            P.emit("vector", f)
        projection(w_l1v, 0, NC, xn_rhs, evv1)
        def evg1(c, hf):
            P.wait("scalar", s_mm)
            def fg(e, c=c, hf=hf):
                e.activation(out=sg1_sb[:, hf, :], in_=ps_a[:, hf, :], func=ACT.Sigmoid,
                             bias=bl1_sb[:, 3, c:c + 1], scale=1.0)
            P.emit("scalar", fg)
            P.mark("scalar", s_ac)
            P.wait("vector", s_ac)
            def fq(e, c=c, hf=hf):
                e.tensor_mul(out=q_v[:, c, hf * 512:(hf + 1) * 512],
                             in0=q_v[:, c, hf * 512:(hf + 1) * 512], in1=sg1_sb[:, hf, :])
            P.emit("vector", fq)
        projection(w_l1g, 0, NC, xn_rhs, evg1)
        P.ssync("vector")
        for c in range(NC):
            def fs(e, c=c):
                e.reduce_sum(out=cvec_sb[:, c:c + 1], in_=at_v[:, c, :], axis=AXX)
            P.emit("vector", fs)
        P.mark("vector", s_ve)
        P.wait("sync", s_ve)
        dma("sync", pfx_loc[:, :], cvec_sb[:, :])
        P.wait("gpsimd", s_io)
        hpf = s_cc.inc(1)
        def fccp(e, hpf=hpf):
            e.collective_compute("AllGather", ALU.bypass, replica_groups=RG,
                                 ins=[pfx_loc[:, :]], outs=[pfx_gth[:, :, :]]).then_inc(hpf, 1)
        P.emit("gpsimd", fccp)
        P.wait("sync", s_cc)
        dma("sync", ev_sb[:, 0, 0:64].rearrange("p (r c) -> p r c", r=4), pfx_gth[:, :, :].rearrange("r p c -> p r c"))
        P.wait("vector", s_io)
        gv = ev_sb[:, 0, 0:64].rearrange("p (r c) -> p r c", r=4)
        P.seq("vector",
              lambda e: e.tensor_scalar(out=l1b_sb[:, :], in0=gv[:, 0, :],
                                        scalar1=pmask_sb[:, 0:1], scalar2=None, op0=ALU.mult),
              lambda e: e.scalar_tensor_tensor(out=l1b_sb[:, :], in0=gv[:, 1, :],
                                               scalar=pmask_sb[:, 1:2], in1=l1b_sb[:, :],
                                               op0=ALU.mult, op1=ALU.add),
              lambda e: e.scalar_tensor_tensor(out=l1b_sb[:, :], in0=gv[:, 2, :],
                                               scalar=pmask_sb[:, 2:3], in1=l1b_sb[:, :],
                                               op0=ALU.mult, op1=ALU.add),
              lambda e: e.scalar_tensor_tensor(out=l1b_sb[:, :], in0=gv[:, 3, :],
                                               scalar=pmask_sb[:, 3:4], in1=l1b_sb[:, :],
                                               op0=ALU.mult, op1=ALU.add))
        P.ssync("vector")
        def fo1(e):
            e.memset(vec_sb[:, 0, :], 1.0)
        P.emit("gpsimd", fo1)
        P.mark("gpsimd", s_ge)
        P.wait("vector", s_ge)
        for c in range(NC):
            P.seq("vector",
                  lambda e, c=c: e.tensor_tensor_scan(out=sq_sb[:, :], data0=vec_sb[:, 0, :],
                                                      data1=at_v[:, c, :],
                                                      initial=l1b_sb[:, c:c + 1],
                                                      op0=ALU.mult, op1=ALU.add),
                  lambda e, c=c: e.tensor_mul(out=vec_sb[:, 1, :], in0=q_v[:, c, :], in1=sq_sb[:, :]),
                  lambda e, c=c: e.tensor_add(out=x_sb[:, c, :], in0=x_sb[:, c, :],
                                              in1=vec_sb[:, 1, :]))
            P.ssync("vector")
        P.mark("vector", s_ve)

    def l1_ffn():
        xn_ok = s_ve.n
        i = wctr[0]; buf = i % 3; wctr[0] += 1
        if i >= 3:
            P.wait("gpsimd", s_mm, slab_mm_done[i - 3])
        h = s_wdma.inc(16)
        def fwr(e, buf=buf, h=h):
            e.dma_start(out=wsl[:, buf, :, 0:8],
                        in_=w_rtr[0, :, :].rearrange("p (k e) -> p k e", k=16)).then_inc(h, 16)
        P.emit("gpsimd", fwr)
        P.wait("tensor", s_wdma)
        P.wait("tensor", s_ve, xn_ok)
        def frm(e, buf=buf):
            for hf in range(2):
                for kc in range(NC):
                    e.matmul(nrm_psv[0:8, hf * 512:(hf + 1) * 512], wsl[:, buf, kc, 0:8],
                             xn_v[:, kc, hf * 512:(hf + 1) * 512],
                             start=(kc == 0), stop=(kc == NC - 1))
        P.emit("tensor", frm)
        P.mark("tensor", s_mm)
        slab_mm_done.append(s_mm.n)
        P.wait("vector", s_mm)
        def fre(e):
            e.tensor_copy(out=stg_sb[0:8, :, :].rearrange("p a b -> p (a b)")[:, 0:1024],
                          in_=nrm_psv[0:8, :])
        P.emit("vector", fre)
        P.mark("vector", s_ve)
        rlog = stg_sb[0:8, :, :].rearrange("p a b -> p (a b)")
        P.wait("tensor", s_ve)
        for tt in range(8):
            par = tt % 2
            def ftr(e, tt=tt, par=par):
                e.matmul(ps_c[:, par, 0:8], rlog[:, tt * 128:(tt + 1) * 128],
                         ident_b[0:8, 0:8], start=True, stop=True)
            P.emit("tensor", ftr)
            P.mark("tensor", s_mm)
            P.wait("vector", s_mm)
            P.seq("vector",
                  lambda e, par=par: e.tensor_copy(out=gt_sb[:, 0:8], in_=ps_c[:, par, 0:8]),
                  lambda e: e.reduce_max(out=gt_sb[:, 8:9], in_=gt_sb[:, 0:8], axis=AXX,
                                         negate=True))
            P.ssync("vector")
            P.wait("scalar", s_ve, s_ve.n + 1)
            P.mark("vector", s_ve)
            def fex(e):
                e.activation(out=gt_sb[:, 0:8], in_=gt_sb[:, 0:8], func=ACT.Exp,
                             bias=gt_sb[:, 8:9], scale=1.0)
            P.emit("scalar", fex)
            P.mark("scalar", s_ac)
            P.wait("vector", s_ac)
            P.seq("vector",
                  lambda e: e.reduce_sum(out=gt_sb[:, 9:10], in_=gt_sb[:, 0:8], axis=AXX),
                  lambda e: e.reciprocal(out=gt_sb[:, 10:11], in_=gt_sb[:, 9:10]),
                  lambda e: e.tensor_scalar(out=gt_sb[:, 0:8], in0=gt_sb[:, 0:8],
                                            scalar1=gt_sb[:, 10:11], scalar2=None, op0=ALU.mult),
                  lambda e: e.max(mx8_sb[:, 0:8], gt_sb[:, 0:8]),
                  lambda e: e.tensor_add(out=gt_sb[:, 11:12], in0=mx8_sb[:, 0:1],
                                         in1=mx8_sb[:, 1:2]),
                  lambda e: e.reciprocal(out=gt_sb[:, 12:13], in_=gt_sb[:, 11:12]),
                  lambda e: e.tensor_single_scalar(out=gt_sb[:, 13:14], in_=mx8_sb[:, 1:2],
                                                   scalar=0.0, op=ALU.bypass),
                  lambda e, tt=tt: e.tensor_single_scalar(out=msk_sb[:, tt, 0:8],
                                                          in_=gt_sb[:, 0:8],
                                                          scalar=mx8_sb[:, 1:2], op=ALU.is_ge),
                  lambda e, tt=tt: e.tensor_mul(out=msk_sb[:, tt, 0:8], in0=msk_sb[:, tt, 0:8],
                                                in1=gt_sb[:, 0:8]),
                  lambda e, tt=tt: e.tensor_scalar(out=msk_sb[:, tt, 0:8],
                                                   in0=msk_sb[:, tt, 0:8],
                                                   scalar1=gt_sb[:, 12:13], scalar2=None,
                                                   op0=ALU.mult))
            P.ssync("vector")
            P.mark("vector", s_ve)
            P.wait("tensor", s_ve)
        w8 = stg_sb[0:8, :, :].rearrange("p a b -> p (a b)")[:, 1024:2048]
        for tt in range(8):
            par = tt % 2
            def ftw(e, tt=tt, par=par):
                e.matmul(ps_c[:, 2 + par, 0:8].rearrange("p a -> p a") if False else
                         ps_c[0:8, 2 + par, 0:128], msk_sb[:, tt, 0:8], ident_f[:, :],
                         start=True, stop=True)
            P.emit("tensor", ftw)
            P.mark("tensor", s_mm)
            P.wait("vector", s_mm)
            def fwe(e, tt=tt, par=par):
                e.tensor_copy(out=w8[0:8, tt * 128:(tt + 1) * 128], in_=ps_c[0:8, 2 + par, 0:128])
            P.emit("vector", fwe)
            P.mark("vector", s_ve)
            P.wait("tensor", s_ve)
        w8_done = s_ve.n
        def hp_v(j):
            return qa_ar.ap()[:, j * 1024:(j + 1) * 1024]
        for ee in range(NE):
            P.ssync("vector")
            P.wait("tensor", s_ve)
            def fwb(e, ee=ee):
                for hf in range(2):
                    e.matmul(ps_c[:, 2 + hf, :], hr8_sb[:, ee, :], w8[:, hf * 512:(hf + 1) * 512],
                             start=True, stop=True)
            P.emit("tensor", fwb)
            P.mark("tensor", s_mm)
            P.wait("vector", s_mm)
            def fwbe(e):
                e.tensor_copy(out=wb1_sb[:, 0:512], in_=ps_c[:, 2, :])
                e.tensor_copy(out=wb1_sb[:, 512:1024], in_=ps_c[:, 3, :])
            P.emit("vector", fwbe)
            P.mark("vector", s_ve)
            for j in range(8):
                for which, col0, pst in ((0, j * 128, ps_a), (1, DI + j * 128, ps_b)):
                    i = wctr[0]; buf = i % 3; wctr[0] += 1
                    if i >= 3:
                        P.wait("gpsimd", s_mm, slab_mm_done[i - 3])
                    h = s_wdma.inc(16)
                    def fw(e, ee=ee, col0=col0, buf=buf, h=h):
                        sl = ee * 16 + col0 // 128
                        e.dma_start(out=wsl[:, buf, :, :],
                                    in_=w_gu[sl, :, :].rearrange("p (k e) -> p k e", k=16)
                                    ).then_inc(h, 16)
                    P.emit("gpsimd", fw)
                    P.wait("tensor", s_wdma)
                    def fmm(e, buf=buf, pst=pst):
                        for hf in range(2):
                            for kc in range(NC):
                                e.matmul(pst[:, hf, :], wsl[:, buf, kc, :],
                                         xn_v[:, kc, hf * 512:(hf + 1) * 512],
                                         start=(kc == 0), stop=(kc == NC - 1))
                    P.emit("tensor", fmm)
                    P.mark("tensor", s_mm)
                    slab_mm_done.append(s_mm.n)
                P.wait("scalar", s_mm)
                for hf in range(2):
                    def fsl(e, hf=hf):
                        e.activation(out=sg1_sb[:, hf, :], in_=ps_a[:, hf, :], func=ACT.Silu)
                    P.emit("scalar", fsl)
                P.mark("scalar", s_ac)
                P.wait("vector", s_ac)
                for hf in range(2):
                    P.seq("vector",
                          lambda e, hf=hf: e.tensor_mul(out=tm1_sb[:, hf, :], in0=ps_b[:, hf, :],
                                                        in1=wb1_sb[:, hf * 512:(hf + 1) * 512]),
                          lambda e, j=j, hf=hf: e.tensor_mul(
                              out=hp_v(j)[:, hf * 512:(hf + 1) * 512],
                              in0=tm1_sb[:, hf, :], in1=sg1_sb[:, hf, :]))
                P.mark("vector", s_ve)
                P.wait("tensor", s_ve)
            for oc in range(NC):
                i = wctr[0]; buf = i % 3; wctr[0] += 1
                if i >= 3:
                    P.wait("gpsimd", s_mm, slab_mm_done[i - 3])
                h = s_wdma.inc(16)
                def fw(e, ee=ee, oc=oc, buf=buf, h=h):
                    sl = ee * 16 + oc
                    e.dma_start(out=wsl[:, buf, 0:8, :],
                                in_=w_dn[sl, :, :].rearrange("p (k e) -> p k e", k=8)
                                ).then_inc(h, 16)
                P.emit("gpsimd", fw)
                P.wait("tensor", s_wdma)
                def fmm(e, buf=buf):
                    for hf in range(2):
                        for kc in range(8):
                            e.matmul(ps_a[:, hf, :], wsl[:, buf, kc, :],
                                     hp_v(kc)[:, hf * 512:(hf + 1) * 512],
                                     start=(kc == 0), stop=(kc == 7))
                P.emit("tensor", fmm)
                P.mark("tensor", s_mm)
                slab_mm_done.append(s_mm.n)
                P.wait("vector", s_mm)
                for hf in range(2):
                    def fev(e, oc=oc, hf=hf):
                        e.scalar_tensor_tensor(out=x_sb[:, oc, hf * 512:(hf + 1) * 512],
                                               in0=ps_a[:, hf, :], scalar=1.0,
                                               in1=x_sb[:, oc, hf * 512:(hf + 1) * 512],
                                               op0=ALU.mult, op1=ALU.add)
                    P.emit("vector", fev)
                P.mark("vector", s_ve)
                P.wait("tensor", s_ve)

    def store_out():
        P.wait("sync", s_ve)
        dma("sync", y_out[:, :, :], x_sb[:, :, :], sem=s_st)
        P.wait("sync", s_st)

    load_consts()
    if N_BLOCKS >= 1:
        layernorm()
        xn_ready = s_ve.n
        cv = l0_qkv(xn_ready)
        l0_attn(cv)
        l0_out()
    if N_BLOCKS >= 2:
        layernorm()
        l0_ffn()
    if N_BLOCKS >= 3:
        layernorm()
        l1_attn()
    if N_BLOCKS >= 4:
        layernorm()
        l1_ffn()
    store_out()

    with nc.Block() as block:
        @block.sync
        def _(e):
            er = EngRec(e)
            for f in P.streams["sync"]:
                f(er)
        @block.gpsimd
        def _(e):
            er = EngRec(e)
            for f in P.streams["gpsimd"]:
                f(er)
        @block.vector
        def _(e):
            er = EngRec(e)
            for f in P.streams["vector"]:
                f(er)
        @block.scalar
        def _(e):
            er = EngRec(e)
            for f in P.streams["scalar"]:
                f(er)
        @block.tensor
        def _(e):
            er = EngRec(e)
            for f in P.streams["tensor"]:
                f(er)
    return nc


def _slabify(wT):
    Dk, E = wT.shape
    nk = Dk // 128
    return np.ascontiguousarray(
        wT.reshape(nk, 128, E // 128, 128).transpose(2, 1, 0, 3).reshape(E // 128, 128, nk * 128))


def _slabify_narrow(wT):
    Dk, E = wT.shape
    nk = Dk // 128
    return np.ascontiguousarray(
        wT.reshape(nk, 128, E).transpose(1, 0, 2).reshape(1, 128, nk * E))


def _host_inputs(inputs):
    bf = ml_dtypes.bfloat16
    x = np.asarray(inputs["L0_attn_x"], dtype=np.float32)
    shared = {
        "w_l0q": _slabify(np.asarray(inputs["L0_attn_w_q"], np.float32).T),
        "w_l0k": _slabify(np.asarray(inputs["L0_attn_w_k"], np.float32).T),
        "w_l0v": _slabify(np.asarray(inputs["L0_attn_w_v"], np.float32).T),
        "w_l0o": _slabify(np.asarray(inputs["L0_attn_w_o"], np.float32).T),
        "w_l0g": _slabify_narrow(np.asarray(inputs["L0_ffn_W_gate"], np.float32).T),
        "w_l0qkv": _slabify(np.asarray(inputs["L0_ffn_W_qkv"], np.float32).T),
        "w_l1q": _slabify(np.asarray(inputs["L1_attn_Wq"], np.float32).T),
        "w_l1k": _slabify(np.asarray(inputs["L1_attn_Wk"], np.float32).T),
        "w_l1v": _slabify(np.asarray(inputs["L1_attn_Wv"], np.float32).T),
        "w_l1g": _slabify(np.asarray(inputs["L1_attn_Wg"], np.float32).T),
        "w_rtr": _slabify_narrow(np.asarray(inputs["L1_ffn_router_gate"], np.float32).T),
    }
    wdn = np.asarray(inputs["L1_ffn_expert_down"], np.float32)
    shared["w_dn"] = np.ascontiguousarray(
        np.concatenate([_slabify(wdn[e]) for e in range(8)], axis=0))
    wgu = np.asarray(inputs["L1_ffn_expert_gate_up"], np.float32)
    wgu = np.concatenate([wgu[:, :, 0::2], wgu[:, :, 1::2]], axis=2)
    shared["w_gu"] = np.ascontiguousarray(
        np.concatenate([_slabify(wgu[e]) for e in range(8)], axis=0))
    bl1 = np.stack([np.asarray(inputs["L1_attn_b" + k], np.float32).reshape(16, 128).T
                    for k in ("q", "k", "v", "g")], axis=1)
    shared["b_l1"] = np.ascontiguousarray(bl1)
    shared["ident_bf"] = np.eye(128, dtype=bf)
    shared["ident_f"] = np.eye(128, dtype=np.float32)
    hc = np.zeros((128, NH, NH), dtype=bf)
    for h in range(NH):
        hc[:, h, h] = 1.0
    shared["hotcols"] = hc
    shared["iota16"] = np.broadcast_to(np.arange(16, dtype=np.float32)[None, :], (128, 16)).copy()
    hr8 = np.zeros((8, 8, 128), dtype=bf)
    for e8 in range(8):
        hr8[e8, e8, :] = 1.0
    shared["hotrows8"] = hr8

    p = np.arange(128, dtype=np.uint32)
    in_maps = []
    for c in range(NCORE):
        g, r = c // 4, c % 4
        xs = np.ascontiguousarray(
            x[g, r * T:(r + 1) * T, :].T.reshape(NC, 128, T).transpose(1, 0, 2))
        gsel = np.zeros((128, 16), dtype=np.uint32)
        for s_ in range(4):
            for b_ in range(4):
                gsel[:, s_ * 4 + b_] = b_ * 2048 + (4 * r + s_) * 128 + p
        ysel = np.zeros((128, 16), dtype=np.uint32)
        for cc in range(16):
            ysel[:, cc] = (cc // 4) * 2048 + r * 512 + (cc % 4) * 128 + p
        pmask = np.zeros((128, 4), dtype=np.float32)
        pmask[:, :r] = 1.0
        m = dict(shared)
        m.update({"x0": xs, "gsel": gsel, "ysel": ysel, "pmask": pmask})
        in_maps.append(m)
    return in_maps


def kernel(**inputs):
    in_maps = _host_inputs(inputs)
    nc = build_program()
    res = run_bass_kernel_spmd(nc, in_maps, core_ids=list(range(NCORE)))
    out = np.empty((2, S, D), dtype=np.float32)
    for c in range(NCORE):
        g, r = c // 4, c % 4
        yc = np.asarray(res.results[c]["y"])
        out[g, r * T:(r + 1) * T, :] = yc.transpose(2, 1, 0).reshape(T, D)
    return out


# version 05
# speedup vs baseline: 1.2404x; 1.0537x over previous
import numpy as np
import ml_dtypes
import concourse.bass as bass
import concourse.mybir as mybir
from concourse.bass_utils import run_bass_kernel_spmd
from concourse.bass_core_rust import IndirectOffsetOnAxis
from contextlib import ExitStack

F32 = mybir.dt.float32
BF16 = mybir.dt.bfloat16
U32 = mybir.dt.uint32
ALU = mybir.AluOpType
ACT = mybir.ActivationFunctionType
AXX = mybir.AxisListType.X

NCORE = 8
T = 1024
D = 2048
NC = 16
NH = 16
S = 4096
SCALE = float(1.0 / np.sqrt(128.0))
NE = 8
DI = 1024

N_BLOCKS = 4

RACE_CHECK = False
STRICT_EVAC = False


class Sem:
    def __init__(self, nc, name):
        self.h = nc.alloc_semaphore(name)
        self.n = 0

    def inc(self, k):
        self.n += k
        return self.h


class Prog:
    ENGINES = ("sync", "gpsimd", "vector", "scalar", "tensor")

    def __init__(self, nc):
        self.nc = nc
        self.streams = {k: [] for k in self.ENGINES}
        self.last_done = {k: None for k in self.ENGINES}

    def emit(self, eng, fn, kind="op"):
        self.streams[eng].append(fn)
        if kind == "op":
            self.last_done[eng] = None

    def wait(self, eng, sem, val=None):
        v = sem.n if val is None else val
        h = sem.h
        if v > 0:
            self.emit(eng, lambda e, h=h, v=v: e.wait_ge(h, v), kind="wait")

    def mark(self, eng, sem, k=1):
        h = sem.inc(k)
        v = k
        if self.last_done[eng] is None:
            self.emit(eng, lambda e, h=h, v=v: e.last.then_inc(h, v), kind="mark")
            self.last_done[eng] = (h, sem.n)
        else:
            h0, v0 = self.last_done[eng]
            def f(e, h=h, v=v, h0=h0, v0=v0):
                e.wait_ge(h0, v0)
                e.nop().then_inc(h, v)
            self.emit(eng, f, kind="mark")
        return sem.n

    def ssync(self, eng):
        if self.last_done[eng] is not None:
            h, v = self.last_done[eng]
            self.emit(eng, lambda e, h=h, v=v: e.wait_ge(h, v), kind="wait")
        else:
            sem = self.chain[eng]
            self.mark(eng, sem)
            self.wait(eng, sem)

    def seq(self, eng, *fns):
        for i, f in enumerate(fns):
            if i:
                self.ssync(eng)
            self.emit(eng, f)


class EngRec:
    def __init__(self, e):
        object.__setattr__(self, "_e", e)
        object.__setattr__(self, "last", None)

    _NO_RECORD = ("wait_ge", "wait_op", "sem_inc", "nop", "maybe_wait_ge")

    def __getattr__(self, name):
        f = getattr(object.__getattribute__(self, "_e"), name)
        if not callable(f):
            return f
        rec = name not in EngRec._NO_RECORD
        def call(*a, **kw):
            r = f(*a, **kw)
            if rec and hasattr(r, "then_inc"):
                object.__setattr__(self, "last", r)
            return r
        return call


def build_program():
    nc = bass.Bass("TRN2", target_bir_lowering=False, debug=False,
                   detect_race_conditions=RACE_CHECK)

    def dram_in(name, shape, dt=F32):
        return nc.dram_tensor(name, list(shape), dt, kind="ExternalInput").ap()

    x0 = dram_in("x0", [128, NC, T])
    w_l0q = dram_in("w_l0q", [16, 128, D]); w_l0k = dram_in("w_l0k", [16, 128, D])
    w_l0v = dram_in("w_l0v", [16, 128, D]); w_l0o = dram_in("w_l0o", [16, 128, D])
    w_l0g = dram_in("w_l0g", [1, 128, 16 * 16]); w_l0qkv = dram_in("w_l0qkv", [48, 128, D])
    w_l1q = dram_in("w_l1q", [16, 128, D]); w_l1k = dram_in("w_l1k", [16, 128, D])
    w_l1v = dram_in("w_l1v", [16, 128, D]); w_l1g = dram_in("w_l1g", [16, 128, D])
    b_l1 = dram_in("b_l1", [128, 4, NC])
    w_rtr = dram_in("w_rtr", [1, 128, 16 * 8])
    w_gu = dram_in("w_gu", [NE * 16, 128, D])
    w_dn = dram_in("w_dn", [NE * 16, 128, 8 * 128])
    ident_bf_in = dram_in("ident_bf", [128, 128], BF16)
    ident_f_in = dram_in("ident_f", [128, 128])
    hotcols_in = dram_in("hotcols", [128, NH, NH], BF16)
    gsel_in = dram_in("gsel", [128, 16], U32)
    ysel_in = dram_in("ysel", [128, 16], U32)
    pmask_in = dram_in("pmask", [128, 4])
    iota16_in = dram_in("iota16", [128, 16])
    hotrows8_in = dram_in("hotrows8", [8, 8, 128], BF16)

    y_out = nc.dram_tensor("y", [128, NC, T], F32, kind="ExternalOutput").ap()

    k_loc = nc.dram_tensor("k_loc", [D, T], BF16, kind="Internal").ap()
    v_loc = nc.dram_tensor("v_loc", [T, D], BF16, kind="Internal").ap()
    k_gth = nc.dram_tensor("k_gth", [4, D, T], BF16, kind="Internal").ap()
    v_gth = nc.dram_tensor("v_gth", [4, T, D], BF16, kind="Internal").ap()
    q_loc = nc.dram_tensor("q_loc", [D, T], BF16, kind="Internal").ap()
    q_gth = nc.dram_tensor("q_gth", [4 * D, T], BF16, kind="Internal").ap()
    kv_loc = nc.dram_tensor("kv_loc", [T, 256], BF16, kind="Internal").ap()
    kv_gth = nc.dram_tensor("kv_gth", [4 * T, 256], BF16, kind="Internal").ap()
    yh_loc = nc.dram_tensor("yh_loc", [4, 512, T], BF16, kind="Internal").ap()
    yh_gth = nc.dram_tensor("yh_gth", [4, 4, 512, T], BF16, kind="Internal").ap()
    pfx_loc = nc.dram_tensor("pfx_loc", [128, NC], F32, kind="Internal").ap()
    pfx_gth = nc.dram_tensor("pfx_gth", [4, 128, NC], F32, kind="Internal").ap()
    RG = [[0, 1, 2, 3], [4, 5, 6, 7]]

    ctx = ExitStack()
    sb = lambda n, s, d: ctx.enter_context(nc.sbuf_tensor(n, s, d))
    psf = lambda n, s: ctx.enter_context(nc.psum_tensor(n, s, F32))

    x_sb = sb("x_sb", [128, NC, T], F32)
    xn_ar = sb("xn_ar", [128, NC * T], BF16)
    qa_ar = sb("qa_ar", [128, NC * T], BF16)
    at_ar = sb("at_ar", [128, NC * T], BF16)
    wsl = sb("wsl", [128, 3, 16, 128], BF16)
    sq_sb = sb("sq_sb", [128, T], F32)
    sqb_sb = sb("sqb_sb", [128, T], BF16)
    vec_sb = sb("vec_sb", [128, 2, T], F32)
    ones_b = sb("ones_b", [128, 128], BF16)
    ones_f = sb("ones_f", [128, 128], F32)
    ident_b = sb("ident_b", [128, 128], BF16)
    ident_f = sb("identf_sb", [128, 128], F32)
    hot_sb = sb("hot_sb", [128, NH, NH], BF16)
    nrm_sb = sb("nrm_sb", [16, 16], F32)
    cvec_sb = sb("cvec_sb", [128, NH], F32)
    gsel_sb = sb("gsel_sb", [128, 16], U32)
    ysel_sb = sb("ysel_sb", [128, 16], U32)
    pmask_sb = sb("pmask_sb", [128, 4], F32)
    bl1_sb = sb("bl1_sb", [128, 4, NC], F32)
    stg_sb = sb("stg_sb", [128, 16, 128], BF16)
    ev_sb = sb("ev_sb", [128, 2, 512], F32)
    gt_sb = sb("gt_sb", [128, 16], F32)
    mx8_sb = sb("mx8_sb", [128, 8], F32)
    mi8_sb = sb("mi8_sb", [128, 8], U32)
    sel_sb = sb("sel_sb", [128, 8, 4], F32)
    msk_sb = sb("msk_sb", [128, 8, 16], F32)
    iota16_sb = sb("iota16_sb", [128, 16], F32)
    cb4_sb = sb("cb4_sb", [128, 4], F32)
    hr8_sb = sb("hr8_sb", [8, 8, 128], BF16)
    l1b_sb = sb("l1b_sb", [128, 16], F32)
    wb1_sb = sb("wb1_sb", [128, T], BF16)
    sg1_sb = sb("sg1_sb", [128, 2, 512], BF16)
    tm1_sb = sb("tm1_sb", [128, 2, 512], BF16)

    ps_a = psf("ps_a", [128, 2, 512])
    ps_b = psf("ps_b", [128, 2, 512])
    ps_c = psf("ps_c", [128, 4, 512])

    def arview(ar, c=NC):
        return ar.ap().rearrange("p (c t) -> p c t", c=c)
    xn_v = arview(xn_ar)
    q_v = arview(qa_ar)
    at_v = arview(at_ar)
    kh_v = xn_ar.ap()[:, 0:4096].rearrange("p (j q) -> p j q", j=32)
    vh_v = xn_ar.ap()[:, 4096:8192].rearrange("p (j h) -> p j h", j=32)
    ptA = at_ar.ap()[:, 0:1024]
    ptB = at_ar.ap()[:, 1024:2048]
    def attn_chunk(c):
        if c < 14:
            return at_ar.ap()[:, 2048 + c * 1024: 2048 + (c + 1) * 1024]
        return xn_ar.ap()[:, 8192 + (c - 14) * 1024: 8192 + (c - 13) * 1024]
    nrm_psv = ps_b[0:16, :, :].rearrange("p a t -> p (a t)")
    ksh_v = at_ar.ap()[:, 0:4096].rearrange("p (j q) -> p j q", j=32)
    vtm_v = at_ar.ap()[:, 4096:8192].rearrange("p (j q) -> p j q", j=32)
    ptfA = at_ar.ap()[:, 8192:9216]
    ptfB = at_ar.ap()[:, 9216:10240]
    ystg_v = at_ar.ap()[:, 10240:10752].rearrange("p (a t) -> p a t", a=2)
    ksel_v = at_ar.ap()[:, 10752:11776].rearrange("p (tt e) -> p tt e", tt=8)
    vsel_v = at_ar.ap()[:, 11776:12800].rearrange("p (tt e) -> p tt e", tt=8)
    ytmp_v = at_ar.ap()[:, 12800:14848].rearrange("p (a t) -> p a t", a=2)
    qmine_v = qa_ar.ap().rearrange("p (s b t) -> p s b t", s=4, b=4)
    m_psv = ps_b[0:1, 0, :]
    m2_psv = ps_b[0:1, 1, :]
    vecA = vec_sb[0:1, 0, :]
    vecB = vec_sb[0:1, 1, :]
    A_bc = ps_c.ap().rearrange("p c t -> p (c t)")[:, 0:1024]
    B_bc = ps_c.ap().rearrange("p c t -> p (c t)")[:, 1024:2048]

    s_wdma = Sem(nc, "s_wdma")
    s_mm = Sem(nc, "s_mm")
    s_ve = Sem(nc, "s_ve")
    s_ac = Sem(nc, "s_ac")
    s_ge = Sem(nc, "s_ge")
    s_io = Sem(nc, "s_io")
    s_cc = Sem(nc, "s_cc")
    s_st = Sem(nc, "s_st")
    s_ind = Sem(nc, "s_ind")

    P = Prog(nc)
    P.chain = {eng: Sem(nc, "s_chain_" + eng) for eng in Prog.ENGINES}

    def dma(eng, out, in_, sem=None, k=16):
        if sem is None:
            sem = s_io
        h = sem.inc(k)
        def f(e, out=out, in_=in_, h=h, k=k):
            e.dma_start(out=out, in_=in_).then_inc(h, k)
        P.emit(eng, f)

    def load_consts():
        dma("sync", ident_b[:], ident_bf_in[:, :])
        dma("sync", ident_f[:], ident_f_in[:, :])
        dma("sync", hot_sb[:], hotcols_in[:, :, :])
        dma("sync", gsel_sb[:], gsel_in[:, :])
        dma("sync", ysel_sb[:], ysel_in[:, :])
        dma("sync", pmask_sb[:], pmask_in[:, :])
        dma("sync", bl1_sb[:], b_l1[:, :, :])
        dma("sync", iota16_sb[:], iota16_in[:, :])
        dma("sync", hr8_sb[:], hotrows8_in[:, :, :])
        dma("sync", x_sb[:], x0[:, :, :])
        def g(e):
            e.memset(ones_b[:], 1.0)
            e.memset(ones_f[:], 1.0)
        P.emit("gpsimd", g)
        P.mark("gpsimd", s_ge)

    def layernorm():
        x_ready = s_io.n
        ve_ready = s_ve.n
        P.wait("scalar", s_io, x_ready)
        P.wait("scalar", s_ve, ve_ready)
        P.wait("tensor", s_io, x_ready)
        P.wait("tensor", s_ve, ve_ready)
        P.wait("tensor", s_ge, 1)
        sqm = []
        for c in range(NC):
            if c >= 1:
                P.wait("scalar", s_mm, sqm[c - 1])
            def fsq(e, c=c):
                e.activation(out=sq_sb[:, :], in_=x_sb[:, c, :], func=ACT.Square)
            P.emit("scalar", fsq)
            P.mark("scalar", s_ac)
            P.wait("tensor", s_ac)
            def fmm(e, c=c):
                for hf in range(2):
                    e.matmul(ps_c[0:1, hf, :], ones_f[:, 0:1],
                             x_sb[:, c, hf * 512:(hf + 1) * 512],
                             start=(c == 0), stop=(c == NC - 1))
                    e.matmul(ps_c[0:1, 2 + hf, :], ones_f[:, 0:1],
                             sq_sb[:, hf * 512:(hf + 1) * 512],
                             start=(c == 0), stop=(c == NC - 1))
            P.emit("tensor", fmm)
            sqm.append(P.mark("tensor", s_mm))
        P.wait("vector", s_mm)
        mfv = ps_c[0:1, :, :].rearrange("p a t -> p (a t)")
        P.seq("vector",
              lambda e: e.tensor_scalar_mul(out=vecA[:], in0=mfv[:, 0:1024], scalar1=1.0 / D),
              lambda e: e.tensor_mul(out=vecB[:], in0=vecA[:], in1=vecA[:]),
              lambda e: e.scalar_tensor_tensor(out=vecB[:], in0=mfv[:, 1024:2048],
                                               scalar=1.0 / float(D), in1=vecB[:],
                                               op0=ALU.mult, op1=ALU.subtract),
              lambda e: e.tensor_scalar_add(out=vecB[:], in0=vecB[:], scalar1=1e-5))
        P.mark("vector", s_ve)
        P.wait("scalar", s_ve)
        def fsr(e):
            e.activation(out=sq_sb[0:1, 0:1024], in_=vecB[:], func=ACT.Sqrt)
        P.emit("scalar", fsr)
        P.mark("scalar", s_ac)
        P.wait("vector", s_ac)
        P.seq("vector",
              lambda e: e.reciprocal(out=vecB[:], in_=sq_sb[0:1, 0:1024]),
              lambda e: e.tensor_mul(out=vecA[:], in0=vecA[:], in1=vecB[:]))
        P.mark("vector", s_ve)
        P.wait("tensor", s_ve)
        def fbc(e):
            for hf in range(2):
                e.matmul(ps_c[:, hf, :], ones_f[0:1, :], vecB[0:1, hf * 512:(hf + 1) * 512],
                         start=True, stop=True)
                e.matmul(ps_c[:, 2 + hf, :], ones_f[0:1, :], vecA[0:1, hf * 512:(hf + 1) * 512],
                         start=True, stop=True)
        P.emit("tensor", fbc)
        P.mark("tensor", s_mm)
        P.wait("vector", s_mm)
        for c in range(NC):
            P.seq("vector",
                  lambda e, c=c: e.tensor_mul(out=sq_sb[:, :], in0=x_sb[:, c, :], in1=A_bc),
                  lambda e, c=c: e.tensor_sub(out=xn_v[:, c, :], in0=sq_sb[:, :], in1=B_bc))
            P.ssync("vector")
        P.mark("vector", s_ve)

    wctr = [0]
    slab_mm_done = []

    def projection(w_dram, col0, n_out, rhs_fn, evac_fn, nkc=NC, xn_ready=None,
                   hook=None):
        if xn_ready is not None:
            P.wait("tensor", s_ve, xn_ready)
        evm = {0: s_ve.n, 1: s_ve.n}
        for c in range(n_out):
            i = wctr[0]
            buf = i % 3
            wctr[0] += 1
            if i >= 3:
                P.wait("gpsimd", s_mm, slab_mm_done[i - 3])
            h = s_wdma.inc(16)
            def fw(e, c=c, buf=buf, h=h, w_dram=w_dram, col0=col0, nkc=nkc):
                sl = col0 // 128 + c
                e.dma_start(out=wsl[:, buf, 0:nkc, :],
                            in_=w_dram[sl, :, :].rearrange("p (k e) -> p k e", k=nkc)
                            ).then_inc(h, 16)
            P.emit("gpsimd", fw)
            P.wait("tensor", s_wdma)
            for hf in range(2):
                P.wait("tensor", s_ve, evm[hf])
                def fmm(e, c=c, buf=buf, hf=hf, rhs_fn=rhs_fn, nkc=nkc):
                    for kc in range(nkc):
                        e.matmul(ps_a[:, hf, :], wsl[:, buf, kc, :], rhs_fn(kc, hf),
                                 start=(kc == 0), stop=(kc == nkc - 1))
                P.emit("tensor", fmm)
                P.mark("tensor", s_mm)
                P.wait("vector", s_mm)
                evac_fn(c, hf)
                evm[hf] = P.mark("vector", s_ve)
                if STRICT_EVAC:
                    P.wait("tensor", s_ve)
            if hook is not None:
                hook(c)
            slab_mm_done.append(s_mm.n)

    def xn_rhs(kc, hf):
        return xn_v[:, kc, hf * 512:(hf + 1) * 512]

    def l0_qkv(xn_ready):
        def evq(c, hf):
            def f(e, c=c, hf=hf):
                e.tensor_copy(out=q_v[:, c, hf * 512:(hf + 1) * 512], in_=ps_a[:, hf, :])
            P.emit("vector", f)
        projection(w_l0q, 0, NC, xn_rhs, evq, xn_ready=xn_ready)
        def evk(c, hf):
            def f(e, c=c, hf=hf):
                e.tensor_copy(out=at_v[:, c, hf * 512:(hf + 1) * 512], in_=ps_a[:, hf, :])
            P.emit("vector", f)
        projection(w_l0k, 0, NC, xn_rhs, evk)
        k_done = s_ve.n
        P.wait("sync", s_ve, k_done)
        dma("sync", k_loc.rearrange("(c p) t -> p c t", p=128), at_v[:, :, :])
        k_dma = s_io.n
        KAG_EARLY = False
        if KAG_EARLY:
            P.wait("gpsimd", s_io, k_dma)
            h1 = s_cc.inc(1)
            def fcc1(e, h1=h1):
                e.collective_compute("AllGather", ALU.bypass, replica_groups=RG,
                                     ins=[k_loc[:, :]], outs=[k_gth[:, :, :]]).then_inc(h1, 1)
            P.emit("gpsimd", fcc1)
        sqm = []
        for c in range(NC):
            P.wait("scalar", s_ve, k_done)
            if c >= 1:
                P.wait("scalar", s_mm, sqm[c - 1])
            def fsq(e, c=c):
                e.activation(out=sqb_sb[:, :], in_=at_v[:, c, :], func=ACT.Square)
            P.emit("scalar", fsq)
            P.mark("scalar", s_ac)
            P.wait("tensor", s_ac)
            def fnm(e, c=c):
                for hf in range(2):
                    e.matmul(nrm_psv[:, hf * 512:(hf + 1) * 512], hot_sb[:, c, :],
                             sqb_sb[:, hf * 512:(hf + 1) * 512],
                             start=(c == 0), stop=(c == NC - 1))
            P.emit("tensor", fnm)
            sqm.append(P.mark("tensor", s_mm))
        P.wait("vector", s_mm)
        def fkn(e):
            e.reduce_max(out=nrm_sb[:, 0:1], in_=nrm_psv[:, :], axis=AXX)
        P.emit("vector", fkn)
        kn_done = P.mark("vector", s_ve)
        P.wait("vector", s_io, k_dma)
        def evv(c, hf):
            def f(e, c=c, hf=hf):
                e.tensor_copy(out=at_v[:, c, hf * 512:(hf + 1) * 512], in_=ps_a[:, hf, :])
            P.emit("vector", f)
        projection(w_l0v, 0, NC, xn_rhs, evv)
        P.wait("tensor", s_ve)
        tp = 0
        for c in range(NC):
            for tt in range(8):
                par = tp % 2
                tp += 1
                def ftr(e, c=c, tt=tt, par=par):
                    e.matmul(ps_c[:, par, 0:128], at_v[:, c, tt * 128:(tt + 1) * 128],
                             ident_b[:], start=True, stop=True)
                P.emit("tensor", ftr)
                P.mark("tensor", s_mm)
                P.wait("vector", s_mm)
                def fev(e, tt=tt, par=par):
                    e.tensor_copy(out=stg_sb[:, tt, :], in_=ps_c[:, par, 0:128])
                P.emit("vector", fev)
                P.mark("vector", s_ve)
                if tp >= 2:
                    P.wait("tensor", s_ve)
            P.wait("sync", s_ve)
            dma("sync", v_loc.rearrange("(tt p) e -> p tt e", p=128)[:, :, c * 128:(c + 1) * 128],
                stg_sb[:, 0:8, :])
            P.wait("vector", s_io)
        v_dma = s_io.n
        sqm2 = []
        for c in range(NC):
            if c == 0:
                P.wait("tensor", s_ve, kn_done)
            if c >= 1:
                P.wait("scalar", s_mm, sqm2[c - 1])
            def fsq(e, c=c):
                e.activation(out=sqb_sb[:, :], in_=q_v[:, c, :], func=ACT.Square)
            P.emit("scalar", fsq)
            P.mark("scalar", s_ac)
            P.wait("tensor", s_ac)
            def fnm(e, c=c):
                for hf in range(2):
                    e.matmul(nrm_psv[:, hf * 512:(hf + 1) * 512], hot_sb[:, c, :],
                             sqb_sb[:, hf * 512:(hf + 1) * 512],
                             start=(c == 0), stop=(c == NC - 1))
            P.emit("tensor", fnm)
            sqm2.append(P.mark("tensor", s_mm))
        P.wait("vector", s_mm)
        P.seq("vector",
              lambda e: e.reduce_max(out=nrm_sb[:, 1:2], in_=nrm_psv[:, :], axis=AXX),
              lambda e: e.tensor_mul(out=nrm_sb[:, 2:3], in0=nrm_sb[:, 0:1], in1=nrm_sb[:, 1:2]))
        P.mark("vector", s_ve)
        P.wait("scalar", s_ve)
        def fsr(e):
            e.activation(out=nrm_sb[:, 3:4], in_=nrm_sb[:, 2:3], func=ACT.Sqrt)
        P.emit("scalar", fsr)
        P.mark("scalar", s_ac)
        P.wait("vector", s_ac)
        P.seq("vector",
              lambda e: e.tensor_scalar_mul(out=nrm_sb[:, 4:5], in0=nrm_sb[:, 3:4], scalar1=-SCALE),
              lambda e: e.tensor_scalar(out=ev_sb[0:16, 0, 0:16], in0=ident_f[0:16, 0:16],
                                        scalar1=nrm_sb[:, 4:5], scalar2=None, op0=ALU.mult))
        P.mark("vector", s_ve)
        P.wait("tensor", s_ve)
        def fcb(e):
            e.matmul(ps_b[:, 1, 0:16], ones_f[0:16, :], ev_sb[0:16, 0, 0:16],
                     start=True, stop=True)
        P.emit("tensor", fcb)
        P.mark("tensor", s_mm)
        P.wait("vector", s_mm)
        def fcv(e):
            e.tensor_copy(out=cvec_sb[:, :], in_=ps_b[:, 1, 0:16])
        P.emit("vector", fcv)
        cvec_done = P.mark("vector", s_ve)
        P.wait("gpsimd", s_io, v_dma)
        if not KAG_EARLY:
            h1 = s_cc.inc(1)
            def fcc1(e, h1=h1):
                e.collective_compute("AllGather", ALU.bypass, replica_groups=RG,
                                     ins=[k_loc[:, :]], outs=[k_gth[:, :, :]]).then_inc(h1, 1)
            P.emit("gpsimd", fcc1)
        h2 = s_cc.inc(1)
        def fcc2(e, h2=h2):
            e.collective_compute("AllGather", ALU.bypass, replica_groups=RG,
                                 ins=[v_loc[:, :]], outs=[v_gth[:, :, :]]).then_inc(h2, 1)
        P.emit("gpsimd", fcc2)
        return cvec_done

    def l0_attn(cvec_done):
        cc_done = s_cc.n
        P.wait("sync", s_cc, cc_done)
        P.wait("scalar", s_ve, cvec_done)
        for h in range(NH):
            for sl in range(4):
                dma("sync", kh_v.rearrange("p j q -> p (j q)")[:, sl * 1024:(sl + 1) * 1024],
                    k_gth[sl, h * 128:(h + 1) * 128, :])
            for sl in range(4):
                dma("sync",
                    vh_v.rearrange("p j h2 -> p (j h2)")[:, sl * 1024:(sl + 1) * 1024]
                    .rearrange("p (tt e) -> p tt e", tt=8),
                    v_gth[sl].rearrange("(tt p) e -> p tt e", p=128)[:, :, h * 128:(h + 1) * 128])
            ld_done = s_io.n
            P.wait("tensor", s_io, ld_done)
            P.wait("scalar", s_io, ld_done)
            for hf in range(2):
                def fst(e, h=h, hf=hf, jc=None, par=None):
                    for jj in range(2):
                        j = jc * 2 + jj
                        e.matmul(ps_c[:, par * 2 + jj, :], kh_v[:, j, :],
                                 q_v[:, h, hf * 512:(hf + 1) * 512],
                                 start=True, stop=True)
                import functools
                P.emit("tensor", functools.partial(fst, jc=0, par=0))
                st_marks = [P.mark("tensor", s_mm)]
                for jc in range(16):
                    par = jc % 2
                    if jc + 1 < 16:
                        P.emit("tensor", functools.partial(fst, jc=jc + 1, par=(jc + 1) % 2))
                        st_marks.append(P.mark("tensor", s_mm))
                    P.wait("scalar", s_mm, st_marks[jc])
                    def fex(e, h=h, par=par):
                        src = ps_c[:, par * 2:par * 2 + 2, :].rearrange("p a t -> p (a t)")
                        dst = ptA if par == 0 else ptB
                        e.activation(out=dst, in_=src, func=ACT.Exp,
                                     bias=cvec_sb[:, h:h + 1], scale=SCALE)
                    P.emit("scalar", fex)
                    P.mark("scalar", s_ac)
                    P.wait("tensor", s_ac)
                    def fpv(e, hf=hf, jc=jc, par=par):
                        for jj in range(2):
                            j = jc * 2 + jj
                            pt = (ptA if par == 0 else ptB)[:, jj * 512:(jj + 1) * 512]
                            e.matmul(ps_a[:, 0, :], vh_v[:, j, :], pt,
                                     start=(j == 0), stop=(j == 31))
                            e.matmul(ps_a[:, 1, :], ones_b[:], pt,
                                     start=(j == 0), stop=(j == 31))
                    P.emit("tensor", fpv)
                    P.mark("tensor", s_mm)
                P.wait("vector", s_mm)
                P.seq("vector",
                      lambda e: e.reciprocal(out=ev_sb[:, 0, :], in_=ps_a[:, 1, :]),
                      lambda e, h=h, hf=hf: e.tensor_mul(
                          out=attn_chunk(h)[:, hf * 512:(hf + 1) * 512],
                          in0=ps_a[:, 0, :], in1=ev_sb[:, 0, :]))
                P.mark("vector", s_ve)
                P.wait("tensor", s_ve)
            P.wait("sync", s_mm)

    def l0_ffn():
        xn_ok = s_ve.n
        gbuf = wctr[0] % 3
        if wctr[0] >= 3:
            P.wait("gpsimd", s_mm, slab_mm_done[wctr[0] - 3])
        h = s_wdma.inc(16)
        def fwg(e, h=h, gbuf=gbuf):
            e.dma_start(out=wsl[:, gbuf, :, 0:16],
                        in_=w_l0g[0, :, :].rearrange("p (k e) -> p k e", k=16)).then_inc(h, 16)
        P.emit("gpsimd", fwg)
        P.wait("tensor", s_wdma)
        P.wait("tensor", s_ve, xn_ok)
        def fgm(e, gbuf=gbuf):
            for hf in range(2):
                for kc in range(NC):
                    e.matmul(nrm_psv[:, hf * 512:(hf + 1) * 512], wsl[:, gbuf, kc, 0:16],
                             xn_v[:, kc, hf * 512:(hf + 1) * 512],
                             start=(kc == 0), stop=(kc == NC - 1))
        P.emit("tensor", fgm)
        P.mark("tensor", s_mm)
        wctr[0] += 3
        slab_mm_done.extend([s_mm.n, s_mm.n, s_mm.n])
        P.wait("vector", s_mm)
        def fge(e):
            e.tensor_copy(out=stg_sb[0:16, :, :].rearrange("p a b -> p (a b)")[:, 0:1024],
                          in_=nrm_psv[:, :])
        P.emit("vector", fge)
        P.mark("vector", s_ve)
        glog = stg_sb[0:16, :, :].rearrange("p a b -> p (a b)")
        P.wait("tensor", s_ve)
        for tt in range(8):
            par = tt % 2
            def ftr(e, tt=tt, par=par):
                e.matmul(ps_c[:, par, 0:16], glog[:, tt * 128:(tt + 1) * 128],
                         ident_b[0:16, 0:16], start=True, stop=True)
            P.emit("tensor", ftr)
            P.mark("tensor", s_mm)
            P.wait("vector", s_mm)
            P.seq("vector",
                  lambda e, par=par: e.tensor_copy(out=gt_sb[:, :], in_=ps_c[:, par, 0:16]),
                  lambda e: e.max(mx8_sb[:, 0:8], gt_sb[:, :]),
                  lambda e: e.max_index(mi8_sb[:, 0:8], mx8_sb[:, 0:8], gt_sb[:, :]))
            P.ssync("vector")
            def frt(e, tt=tt):
                e.tensor_copy(out=sel_sb[:, tt, 0:1], in_=mi8_sb[:, 0:1])
                e.reduce_max(out=sel_sb[:, tt, 1:2], in_=gt_sb[:, :], axis=AXX, negate=True)
            P.emit("vector", frt)
            P.mark("vector", s_ve)
            P.wait("scalar", s_ve)
            def fex(e, tt=tt):
                e.activation(out=gt_sb[:, :], in_=gt_sb[:, :], func=ACT.Exp,
                             bias=sel_sb[:, tt, 1:2], scale=1.0)
            P.emit("scalar", fex)
            P.mark("scalar", s_ac)
            P.wait("vector", s_ac)
            P.seq("vector",
                  lambda e, tt=tt: e.reduce_sum(out=sel_sb[:, tt, 2:3], in_=gt_sb[:, :], axis=AXX),
                  lambda e, tt=tt: e.reciprocal(out=sel_sb[:, tt, 3:4], in_=sel_sb[:, tt, 2:3]),
                  lambda e, tt=tt: e.tensor_single_scalar(out=msk_sb[:, tt, :], in_=iota16_sb[:, :],
                                                          scalar=sel_sb[:, tt, 0:1], op=ALU.is_equal))
            P.mark("vector", s_ve)
            P.wait("tensor", s_ve)
        msk_done = s_ve.n
        qstg = at_ar.ap()[:, 0:2048]
        qdma = []
        def evqf(c, hf):
            slot = c % 2
            def f(e, c=c, hf=hf, slot=slot):
                e.tensor_copy(out=qstg[:, slot * 1024 + hf * 512: slot * 1024 + (hf + 1) * 512],
                              in_=ps_a[:, hf, :])
            P.emit("vector", f)
            if hf == 1:
                P.wait("sync", s_ve, s_ve.n + 1)
                dma("sync", q_loc[c * 128:(c + 1) * 128, :],
                    qstg[:, slot * 1024:(slot + 1) * 1024])
                qdma.append(s_io.n)
        def evqf2(c, hf):
            if c >= 2 and hf == 0:
                P.wait("vector", s_io, qdma[c - 2])
            evqf(c, hf)
        projection(w_l0qkv, 0, NC, xn_rhs, evqf2)
        kvt = at_ar.ap()[:, 0:1024]
        tsel = at_ar.ap()[:, 2048:3072].rearrange("p (tt e) -> p tt e", tt=8)
        for kv_i, (colbase, dst) in enumerate(((2048, ksel_v), (4096, vsel_v))):
            if kv_i == 0:
                P.wait("vector", s_io)
                P.wait("vector", s_ve, msk_done)
            def evkv(c, hf, kvt=kvt):
                def f(e, c=c, hf=hf):
                    e.tensor_copy(out=kvt[:, hf * 512:(hf + 1) * 512], in_=ps_a[:, hf, :])
                P.emit("vector", f)
            def hook(c):
                for tt in range(8):
                    par = tt % 2
                    P.wait("tensor", s_ve)
                    def ftr(e, tt=tt, par=par, kvt=kvt):
                        e.matmul(ps_c[:, 2 + par, 0:128], kvt[:, tt * 128:(tt + 1) * 128],
                                 ident_b[:], start=True, stop=True)
                    P.emit("tensor", ftr)
                    P.mark("tensor", s_mm)
                    P.wait("vector", s_mm)
                    def fev(e, tt=tt, par=par):
                        e.tensor_copy(out=tsel[:, tt, :], in_=ps_c[:, 2 + par, 0:128])
                    P.emit("vector", fev)
                    P.mark("vector", s_ve)
                P.ssync("vector")
                def facc(e, c=c, dst=dst):
                    for tt in range(8):
                        if c == 0:
                            e.tensor_scalar(out=dst[:, tt, :], in0=tsel[:, tt, :],
                                            scalar1=msk_sb[:, tt, c:c + 1], scalar2=None,
                                            op0=ALU.mult)
                        else:
                            e.scalar_tensor_tensor(out=dst[:, tt, :], in0=tsel[:, tt, :],
                                                   scalar=msk_sb[:, tt, c:c + 1],
                                                   in1=dst[:, tt, :],
                                                   op0=ALU.mult, op1=ALU.add)
                P.emit("vector", facc)
                P.mark("vector", s_ve)
            projection(w_l0qkv, colbase, NC, xn_rhs, evkv, hook=hook)
        P.ssync("vector")
        def fvt1(e):
            for tt in range(8):
                e.tensor_scalar(out=vsel_v[:, tt, :], in0=vsel_v[:, tt, :],
                                scalar1=sel_sb[:, tt, 3:4], scalar2=None, op0=ALU.mult)
        P.emit("vector", fvt1)
        P.mark("vector", s_ve)
        P.wait("sync", s_ve)
        dma("sync", kv_loc.rearrange("(tt p) e -> p tt e", p=128)[:, :, 0:128], ksel_v[:, :, :])
        dma("sync", kv_loc.rearrange("(tt p) e -> p tt e", p=128)[:, :, 128:256], vsel_v[:, :, :])
        kv_dma = s_io.n
        P.wait("gpsimd", s_io, kv_dma)
        hq = s_cc.inc(1)
        def fccq(e, hq=hq):
            e.collective_compute("AllGather", ALU.bypass, replica_groups=RG,
                                 ins=[q_loc[:, :]], outs=[q_gth[:, :]]).then_inc(hq, 1)
        P.emit("gpsimd", fccq)
        hkv = s_cc.inc(1)
        def fcckv(e, hkv=hkv):
            e.collective_compute("AllGather", ALU.bypass, replica_groups=RG,
                                 ins=[kv_loc[:, :]], outs=[kv_gth[:, :]]).then_inc(hkv, 1)
        P.emit("gpsimd", fcckv)
        cc_done = s_cc.n
        P.wait("sync", s_cc, cc_done)
        for j in range(32):
            hdt = s_io.inc(16)
            def ftd(e, j=j, hdt=hdt):
                e.dma_start(out=ksh_v[:, j, :], in_=kv_gth[j * 128:(j + 1) * 128, 0:128],
                            transpose=True).then_inc(hdt, 16)
            P.emit("sync", ftd)
        for j in range(32):
            dma("sync", vtm_v[:, j, :], kv_gth[j * 128:(j + 1) * 128, 128:256])
        P.wait("gpsimd", s_cc, cc_done)
        for col in range(16):
            hgi = s_ind.inc(16)
            def fgi(e, col=col, hgi=hgi):
                e.indirect_dma_start(out=qmine_v[:, col // 4, col % 4, :], out_offset=None,
                                     in_=q_gth[:, :],
                                     in_offset=IndirectOffsetOnAxis(gsel_sb[:, col:col + 1], 0)
                                     ).then_inc(hgi, 16)
            P.emit("gpsimd", fgi)
        ld_done = s_io.n
        qm_done = s_ind.n
        P.wait("scalar", s_io, ld_done)
        P.wait("scalar", s_ind, qm_done)
        P.wait("tensor", s_ind, qm_done)
        segs = []
        for seg in range(4):
            if seg >= 1:
                P.wait("scalar", s_mm)
            def fsq(e, seg=seg):
                e.activation(out=sqb_sb[:, :],
                             in_=ksh_v.rearrange("p j q -> p (j q)")[:, seg * 1024:(seg + 1) * 1024],
                             func=ACT.Square)
            P.emit("scalar", fsq)
            P.mark("scalar", s_ac)
            P.wait("tensor", s_ac)
            def fnm(e):
                for hf in range(2):
                    e.matmul(ps_b[0:1, hf, :], ones_b[:, 0:1], sqb_sb[:, hf * 512:(hf + 1) * 512],
                             start=True, stop=True)
            P.emit("tensor", fnm)
            P.mark("tensor", s_mm)
            P.wait("vector", s_mm)
            if seg == 0:
                def frd(e):
                    e.reduce_max(out=nrm_sb[0:1, 0:1],
                                 in_=ps_b[0:1, :, :].rearrange("p a t -> p (a t)"), axis=AXX)
                P.emit("vector", frd)
            else:
                P.seq("vector",
                      lambda e: e.reduce_max(out=nrm_sb[0:1, 1:2],
                                             in_=ps_b[0:1, :, :].rearrange("p a t -> p (a t)"),
                                             axis=AXX),
                      lambda e: e.tensor_max(out=nrm_sb[0:1, 0:1], in0=nrm_sb[0:1, 0:1],
                                             in1=nrm_sb[0:1, 1:2]))
            P.mark("vector", s_ve)
            P.wait("tensor", s_ve)
            segs.append(s_ve.n)
        for sl in range(4):
            for seg in range(4):
                P.wait("scalar", s_mm)
                def fsq(e, sl=sl, seg=seg):
                    e.activation(out=sqb_sb[:, :], in_=qmine_v[:, sl, seg, :], func=ACT.Square)
                P.emit("scalar", fsq)
                P.mark("scalar", s_ac)
                P.wait("tensor", s_ac)
                def fnm(e):
                    for hf in range(2):
                        e.matmul(ps_b[0:1, hf, :], ones_b[:, 0:1],
                                 sqb_sb[:, hf * 512:(hf + 1) * 512], start=True, stop=True)
                P.emit("tensor", fnm)
                P.mark("tensor", s_mm)
                P.wait("vector", s_mm)
                if seg == 0:
                    def frd(e, sl=sl):
                        e.reduce_max(out=nrm_sb[0:1, 2 + sl:3 + sl],
                                     in_=ps_b[0:1, :, :].rearrange("p a t -> p (a t)"), axis=AXX)
                    P.emit("vector", frd)
                else:
                    P.seq("vector",
                          lambda e: e.reduce_max(out=nrm_sb[0:1, 6:7],
                                                 in_=ps_b[0:1, :, :].rearrange("p a t -> p (a t)"),
                                                 axis=AXX),
                          lambda e, sl=sl: e.tensor_max(out=nrm_sb[0:1, 2 + sl:3 + sl],
                                                        in0=nrm_sb[0:1, 2 + sl:3 + sl],
                                                        in1=nrm_sb[0:1, 6:7]))
                P.mark("vector", s_ve)
                P.wait("tensor", s_ve)
        P.ssync("vector")
        def fcs(e):
            for sl in range(4):
                e.tensor_mul(out=nrm_sb[0:1, 8 + sl:9 + sl], in0=nrm_sb[0:1, 0:1],
                             in1=nrm_sb[0:1, 2 + sl:3 + sl])
        P.emit("vector", fcs)
        P.mark("vector", s_ve)
        P.wait("scalar", s_ve)
        def fsr(e):
            e.activation(out=nrm_sb[0:1, 12:16], in_=nrm_sb[0:1, 8:12], func=ACT.Sqrt)
        P.emit("scalar", fsr)
        P.mark("scalar", s_ac)
        P.wait("vector", s_ac)
        def fng(e):
            e.tensor_scalar_mul(out=nrm_sb[0:1, 12:16], in0=nrm_sb[0:1, 12:16], scalar1=-SCALE)
        P.emit("vector", fng)
        P.mark("vector", s_ve)
        P.wait("tensor", s_ve)
        def fbb(e):
            e.matmul(ps_b[:, 1, 0:4], ones_f[0:1, :], nrm_sb[0:1, 12:16], start=True, stop=True)
        P.emit("tensor", fbb)
        P.mark("tensor", s_mm)
        P.wait("vector", s_mm)
        def fcv(e):
            e.tensor_copy(out=cb4_sb[:, :], in_=ps_b[:, 1, 0:4])
        P.emit("vector", fcv)
        cb_done = P.mark("vector", s_ve)
        P.wait("scalar", s_ve, cb_done)
        ydma = []
        for sl in range(4):
            for pi in range(16):
                slab = (2 * pi) // 8
                off = ((2 * pi) % 8) * 128
                njs = 2 * pi + 2
                nch = (njs + 3) // 4
                import functools
                def fstf(e, sl=sl, slab=slab, off=off, par=None, j0=None, jn=None):
                    for jj in range(jn):
                        j = j0 + jj
                        e.matmul(ps_c[:, par * 2 + jj // 2, (jj % 2) * 256:(jj % 2 + 1) * 256],
                                 ksh_v[:, j, :], qmine_v[:, sl, slab, off:off + 256],
                                 start=True, stop=True)
                P.emit("tensor", functools.partial(fstf, par=0, j0=0, jn=min(4, njs)))
                stf_marks = [P.mark("tensor", s_mm)]
                for ch in range(nch):
                    par = ch % 2
                    j0 = ch * 4
                    jn = min(4, njs - j0)
                    if ch + 1 < nch:
                        P.emit("tensor", functools.partial(
                            fstf, par=(ch + 1) % 2, j0=(ch + 1) * 4,
                            jn=min(4, njs - (ch + 1) * 4)))
                        stf_marks.append(P.mark("tensor", s_mm))
                    P.wait("scalar", s_mm, stf_marks[ch])
                    def fex(e, sl=sl, par=par, jn=jn):
                        src = ps_c[:, par * 2:par * 2 + 2, :].rearrange("p a t -> p (a t)")[:, 0:jn * 256]
                        dst = (ptfA if par == 0 else ptfB)[:, 0:jn * 256]
                        e.activation(out=dst, in_=src, func=ACT.Exp,
                                     bias=cb4_sb[:, sl:sl + 1], scale=SCALE)
                    P.emit("scalar", fex)
                    P.mark("scalar", s_ac)
                    has_mask = (j0 + jn == njs)
                    if has_mask:
                        P.wait("gpsimd", s_ac)
                        def fmk(e, par=par, jn=jn):
                            buf = ptfA if par == 0 else ptfB
                            e.affine_select(out=buf[:, (jn - 2) * 256:(jn - 1) * 256],
                                            in_=buf[:, (jn - 2) * 256:(jn - 1) * 256],
                                            pattern=[[1, 256]], compare_op=ALU.is_ge,
                                            fill=0.0, base=0, channel_multiplier=-1)
                            e.affine_select(out=buf[:, (jn - 1) * 256:jn * 256],
                                            in_=buf[:, (jn - 1) * 256:jn * 256],
                                            pattern=[[1, 256]], compare_op=ALU.is_ge,
                                            fill=0.0, base=-128, channel_multiplier=-1)
                        P.emit("gpsimd", fmk)
                        P.mark("gpsimd", s_ge)
                        P.wait("tensor", s_ge)
                    else:
                        P.wait("tensor", s_ac)
                    def fpv(e, par=par, j0=j0, jn=jn, njs=njs):
                        buf = ptfA if par == 0 else ptfB
                        for jj in range(jn):
                            j = j0 + jj
                            pt = buf[:, jj * 256:(jj + 1) * 256]
                            e.matmul(ps_a[:, 0, 0:256], vtm_v[:, j, :], pt,
                                     start=(j == 0), stop=(j == njs - 1))
                            e.matmul(ps_a[:, 1, 0:256], ones_b[:], pt,
                                     start=(j == 0), stop=(j == njs - 1))
                    P.emit("tensor", fpv)
                    P.mark("tensor", s_mm)
                yslot = pi % 2
                if len(ydma) >= 2:
                    P.wait("vector", s_io, ydma[-2])
                P.wait("vector", s_mm)
                P.seq("vector",
                      lambda e: e.reciprocal(out=ev_sb[:, 0, 0:256], in_=ps_a[:, 1, 0:256]),
                      lambda e, yslot=yslot: e.tensor_mul(out=ystg_v[:, yslot, :],
                                                          in0=ps_a[:, 0, 0:256],
                                                          in1=ev_sb[:, 0, 0:256]))
                P.mark("vector", s_ve)
                P.wait("tensor", s_ve)
                P.wait("sync", s_ve)
                dma("sync", yh_loc[slab, sl * 128:(sl + 1) * 128, off:off + 256],
                    ystg_v[:, yslot, :])
                ydma.append(s_io.n)
        yh_done = s_io.n
        P.wait("gpsimd", s_io, yh_done)
        hy = s_cc.inc(1)
        def fccy(e, hy=hy):
            e.collective_compute("AllGather", ALU.bypass, replica_groups=RG,
                                 ins=[yh_loc[:, :, :]], outs=[yh_gth[:, :, :, :]]).then_inc(hy, 1)
        P.emit("gpsimd", fccy)
        ycc = s_cc.n
        P.wait("gpsimd", s_cc, ycc)
        yflat = yh_gth.rearrange("r s h t -> (r s h) t")
        P.ssync("vector")
        add_marks = []
        for c in range(NC):
            if c >= 2:
                P.wait("gpsimd", s_ve, add_marks[c - 2])
            hgi = s_ind.inc(16)
            def fgy(e, c=c, hgi=hgi):
                e.indirect_dma_start(out=ytmp_v[:, c % 2, :], out_offset=None,
                                     in_=yflat,
                                     in_offset=IndirectOffsetOnAxis(ysel_sb[:, c:c + 1], 0)
                                     ).then_inc(hgi, 16)
            P.emit("gpsimd", fgy)
            P.wait("vector", s_ind)
            def fad(e, c=c):
                e.tensor_add(out=x_sb[:, c, :], in0=x_sb[:, c, :], in1=ytmp_v[:, c % 2, :])
            P.emit("vector", fad)
            add_marks.append(P.mark("vector", s_ve))

    def l0_out():
        P.ssync("vector")
        def attn_rhs(kc, hf):
            return attn_chunk(kc)[:, hf * 512:(hf + 1) * 512]
        def evo(c, hf):
            def f(e, c=c, hf=hf):
                e.scalar_tensor_tensor(out=x_sb[:, c, hf * 512:(hf + 1) * 512],
                                       in0=ps_a[:, hf, :], scalar=1.0,
                                       in1=x_sb[:, c, hf * 512:(hf + 1) * 512],
                                       op0=ALU.mult, op1=ALU.add)
            P.emit("vector", f)
        projection(w_l0o, 0, NC, attn_rhs, evo)


    def l1_attn():
        P.ssync("vector")
        def evq1(c, hf):
            def f(e, c=c, hf=hf):
                e.tensor_scalar(out=q_v[:, c, hf * 512:(hf + 1) * 512], in0=ps_a[:, hf, :],
                                scalar1=bl1_sb[:, 0, c:c + 1], scalar2=None, op0=ALU.add)
            P.emit("vector", f)
        projection(w_l1q, 0, NC, xn_rhs, evq1, xn_ready=s_ve.n)
        def evk1(c, hf):
            def f(e, c=c, hf=hf):
                e.tensor_scalar(out=at_v[:, c, hf * 512:(hf + 1) * 512], in0=ps_a[:, hf, :],
                                scalar1=bl1_sb[:, 1, c:c + 1], scalar2=None, op0=ALU.add)
            P.emit("vector", f)
        projection(w_l1k, 0, NC, xn_rhs, evk1)
        def evv1(c, hf):
            def f(e, c=c, hf=hf):
                e.scalar_tensor_tensor(out=at_v[:, c, hf * 512:(hf + 1) * 512],
                                       in0=ps_a[:, hf, :], scalar=bl1_sb[:, 2, c:c + 1],
                                       in1=at_v[:, c, hf * 512:(hf + 1) * 512],
                                       op0=ALU.add, op1=ALU.mult)
            P.emit("vector", f)
        projection(w_l1v, 0, NC, xn_rhs, evv1)
        def evg1(c, hf):
            P.wait("scalar", s_mm)
            def fg(e, c=c, hf=hf):
                e.activation(out=sg1_sb[:, hf, :], in_=ps_a[:, hf, :], func=ACT.Sigmoid,
                             bias=bl1_sb[:, 3, c:c + 1], scale=1.0)
            P.emit("scalar", fg)
            P.mark("scalar", s_ac)
            P.wait("vector", s_ac)
            def fq(e, c=c, hf=hf):
                e.tensor_mul(out=q_v[:, c, hf * 512:(hf + 1) * 512],
                             in0=q_v[:, c, hf * 512:(hf + 1) * 512], in1=sg1_sb[:, hf, :])
            P.emit("vector", fq)
        projection(w_l1g, 0, NC, xn_rhs, evg1)
        P.ssync("vector")
        for c in range(NC):
            def fs(e, c=c):
                e.reduce_sum(out=cvec_sb[:, c:c + 1], in_=at_v[:, c, :], axis=AXX)
            P.emit("vector", fs)
        P.mark("vector", s_ve)
        P.wait("sync", s_ve)
        dma("sync", pfx_loc[:, :], cvec_sb[:, :])
        P.wait("gpsimd", s_io)
        hpf = s_cc.inc(1)
        def fccp(e, hpf=hpf):
            e.collective_compute("AllGather", ALU.bypass, replica_groups=RG,
                                 ins=[pfx_loc[:, :]], outs=[pfx_gth[:, :, :]]).then_inc(hpf, 1)
        P.emit("gpsimd", fccp)
        P.wait("sync", s_cc)
        dma("sync", ev_sb[:, 0, 0:64].rearrange("p (r c) -> p r c", r=4), pfx_gth[:, :, :].rearrange("r p c -> p r c"))
        P.wait("vector", s_io)
        gv = ev_sb[:, 0, 0:64].rearrange("p (r c) -> p r c", r=4)
        P.seq("vector",
              lambda e: e.tensor_scalar(out=l1b_sb[:, :], in0=gv[:, 0, :],
                                        scalar1=pmask_sb[:, 0:1], scalar2=None, op0=ALU.mult),
              lambda e: e.scalar_tensor_tensor(out=l1b_sb[:, :], in0=gv[:, 1, :],
                                               scalar=pmask_sb[:, 1:2], in1=l1b_sb[:, :],
                                               op0=ALU.mult, op1=ALU.add),
              lambda e: e.scalar_tensor_tensor(out=l1b_sb[:, :], in0=gv[:, 2, :],
                                               scalar=pmask_sb[:, 2:3], in1=l1b_sb[:, :],
                                               op0=ALU.mult, op1=ALU.add),
              lambda e: e.scalar_tensor_tensor(out=l1b_sb[:, :], in0=gv[:, 3, :],
                                               scalar=pmask_sb[:, 3:4], in1=l1b_sb[:, :],
                                               op0=ALU.mult, op1=ALU.add))
        P.ssync("vector")
        def fo1(e):
            e.memset(vec_sb[:, 0, :], 1.0)
        P.emit("gpsimd", fo1)
        P.mark("gpsimd", s_ge)
        P.wait("vector", s_ge)
        for c in range(NC):
            P.seq("vector",
                  lambda e, c=c: e.tensor_tensor_scan(out=sq_sb[:, :], data0=vec_sb[:, 0, :],
                                                      data1=at_v[:, c, :],
                                                      initial=l1b_sb[:, c:c + 1],
                                                      op0=ALU.mult, op1=ALU.add),
                  lambda e, c=c: e.tensor_mul(out=vec_sb[:, 1, :], in0=q_v[:, c, :], in1=sq_sb[:, :]),
                  lambda e, c=c: e.tensor_add(out=x_sb[:, c, :], in0=x_sb[:, c, :],
                                              in1=vec_sb[:, 1, :]))
            P.ssync("vector")
        P.mark("vector", s_ve)

    def l1_ffn():
        xn_ok = s_ve.n
        i = wctr[0]; buf = i % 3; wctr[0] += 1
        if i >= 3:
            P.wait("gpsimd", s_mm, slab_mm_done[i - 3])
        h = s_wdma.inc(16)
        def fwr(e, buf=buf, h=h):
            e.dma_start(out=wsl[:, buf, :, 0:8],
                        in_=w_rtr[0, :, :].rearrange("p (k e) -> p k e", k=16)).then_inc(h, 16)
        P.emit("gpsimd", fwr)
        P.wait("tensor", s_wdma)
        P.wait("tensor", s_ve, xn_ok)
        def frm(e, buf=buf):
            for hf in range(2):
                for kc in range(NC):
                    e.matmul(nrm_psv[0:8, hf * 512:(hf + 1) * 512], wsl[:, buf, kc, 0:8],
                             xn_v[:, kc, hf * 512:(hf + 1) * 512],
                             start=(kc == 0), stop=(kc == NC - 1))
        P.emit("tensor", frm)
        P.mark("tensor", s_mm)
        slab_mm_done.append(s_mm.n)
        P.wait("vector", s_mm)
        def fre(e):
            e.tensor_copy(out=stg_sb[0:8, :, :].rearrange("p a b -> p (a b)")[:, 0:1024],
                          in_=nrm_psv[0:8, :])
        P.emit("vector", fre)
        P.mark("vector", s_ve)
        rlog = stg_sb[0:8, :, :].rearrange("p a b -> p (a b)")
        P.wait("tensor", s_ve)
        for tt in range(8):
            par = tt % 2
            def ftr(e, tt=tt, par=par):
                e.matmul(ps_c[:, par, 0:8], rlog[:, tt * 128:(tt + 1) * 128],
                         ident_b[0:8, 0:8], start=True, stop=True)
            P.emit("tensor", ftr)
            P.mark("tensor", s_mm)
            P.wait("vector", s_mm)
            P.seq("vector",
                  lambda e, par=par: e.tensor_copy(out=gt_sb[:, 0:8], in_=ps_c[:, par, 0:8]),
                  lambda e: e.reduce_max(out=gt_sb[:, 8:9], in_=gt_sb[:, 0:8], axis=AXX,
                                         negate=True))
            P.ssync("vector")
            P.wait("scalar", s_ve, s_ve.n + 1)
            P.mark("vector", s_ve)
            def fex(e):
                e.activation(out=gt_sb[:, 0:8], in_=gt_sb[:, 0:8], func=ACT.Exp,
                             bias=gt_sb[:, 8:9], scale=1.0)
            P.emit("scalar", fex)
            P.mark("scalar", s_ac)
            P.wait("vector", s_ac)
            P.seq("vector",
                  lambda e: e.reduce_sum(out=gt_sb[:, 9:10], in_=gt_sb[:, 0:8], axis=AXX),
                  lambda e: e.reciprocal(out=gt_sb[:, 10:11], in_=gt_sb[:, 9:10]),
                  lambda e: e.tensor_scalar(out=gt_sb[:, 0:8], in0=gt_sb[:, 0:8],
                                            scalar1=gt_sb[:, 10:11], scalar2=None, op0=ALU.mult),
                  lambda e: e.max(mx8_sb[:, 0:8], gt_sb[:, 0:8]),
                  lambda e: e.tensor_add(out=gt_sb[:, 11:12], in0=mx8_sb[:, 0:1],
                                         in1=mx8_sb[:, 1:2]),
                  lambda e: e.reciprocal(out=gt_sb[:, 12:13], in_=gt_sb[:, 11:12]),
                  lambda e: e.tensor_single_scalar(out=gt_sb[:, 13:14], in_=mx8_sb[:, 1:2],
                                                   scalar=0.0, op=ALU.bypass),
                  lambda e, tt=tt: e.tensor_single_scalar(out=msk_sb[:, tt, 0:8],
                                                          in_=gt_sb[:, 0:8],
                                                          scalar=mx8_sb[:, 1:2], op=ALU.is_ge),
                  lambda e, tt=tt: e.tensor_mul(out=msk_sb[:, tt, 0:8], in0=msk_sb[:, tt, 0:8],
                                                in1=gt_sb[:, 0:8]),
                  lambda e, tt=tt: e.tensor_scalar(out=msk_sb[:, tt, 0:8],
                                                   in0=msk_sb[:, tt, 0:8],
                                                   scalar1=gt_sb[:, 12:13], scalar2=None,
                                                   op0=ALU.mult))
            P.ssync("vector")
            P.mark("vector", s_ve)
            P.wait("tensor", s_ve)
        w8 = stg_sb[0:8, :, :].rearrange("p a b -> p (a b)")[:, 1024:2048]
        for tt in range(8):
            par = tt % 2
            def ftw(e, tt=tt, par=par):
                e.matmul(ps_c[:, 2 + par, 0:8].rearrange("p a -> p a") if False else
                         ps_c[0:8, 2 + par, 0:128], msk_sb[:, tt, 0:8], ident_f[:, :],
                         start=True, stop=True)
            P.emit("tensor", ftw)
            P.mark("tensor", s_mm)
            P.wait("vector", s_mm)
            def fwe(e, tt=tt, par=par):
                e.tensor_copy(out=w8[0:8, tt * 128:(tt + 1) * 128], in_=ps_c[0:8, 2 + par, 0:128])
            P.emit("vector", fwe)
            P.mark("vector", s_ve)
            P.wait("tensor", s_ve)
        w8_done = s_ve.n
        def hp_v(j):
            return qa_ar.ap()[:, j * 1024:(j + 1) * 1024]
        for ee in range(NE):
            P.ssync("vector")
            P.wait("tensor", s_ve)
            def fwb(e, ee=ee):
                for hf in range(2):
                    e.matmul(ps_c[:, 2 + hf, :], hr8_sb[:, ee, :], w8[:, hf * 512:(hf + 1) * 512],
                             start=True, stop=True)
            P.emit("tensor", fwb)
            P.mark("tensor", s_mm)
            P.wait("vector", s_mm)
            def fwbe(e):
                e.tensor_copy(out=wb1_sb[:, 0:512], in_=ps_c[:, 2, :])
                e.tensor_copy(out=wb1_sb[:, 512:1024], in_=ps_c[:, 3, :])
            P.emit("vector", fwbe)
            P.mark("vector", s_ve)
            ga_done = s_ac.n
            gu_done = s_ve.n
            for j in range(8):
                for which, col0, pst in ((0, j * 128, ps_a), (1, DI + j * 128, ps_b)):
                    i = wctr[0]; buf = i % 3; wctr[0] += 1
                    if i >= 3:
                        P.wait("gpsimd", s_mm, slab_mm_done[i - 3])
                    h = s_wdma.inc(16)
                    def fw(e, ee=ee, col0=col0, buf=buf, h=h):
                        sl = ee * 16 + col0 // 128
                        e.dma_start(out=wsl[:, buf, :, :],
                                    in_=w_gu[sl, :, :].rearrange("p (k e) -> p k e", k=16)
                                    ).then_inc(h, 16)
                    P.emit("gpsimd", fw)
                    P.wait("tensor", s_wdma)
                    if which == 0:
                        P.wait("tensor", s_ac, ga_done)
                    else:
                        P.wait("tensor", s_ve, gu_done)
                    def fmm(e, buf=buf, pst=pst):
                        for hf in range(2):
                            for kc in range(NC):
                                e.matmul(pst[:, hf, :], wsl[:, buf, kc, :],
                                         xn_v[:, kc, hf * 512:(hf + 1) * 512],
                                         start=(kc == 0), stop=(kc == NC - 1))
                    P.emit("tensor", fmm)
                    P.mark("tensor", s_mm)
                    slab_mm_done.append(s_mm.n)
                P.wait("scalar", s_mm)
                for hf in range(2):
                    def fsl(e, hf=hf):
                        e.activation(out=sg1_sb[:, hf, :], in_=ps_a[:, hf, :], func=ACT.Silu)
                    P.emit("scalar", fsl)
                ga_done = P.mark("scalar", s_ac)
                P.wait("vector", s_ac)
                for hf in range(2):
                    P.seq("vector",
                          lambda e, hf=hf: e.tensor_mul(out=tm1_sb[:, hf, :], in0=ps_b[:, hf, :],
                                                        in1=wb1_sb[:, hf * 512:(hf + 1) * 512]),
                          lambda e, j=j, hf=hf: e.tensor_mul(
                              out=hp_v(j)[:, hf * 512:(hf + 1) * 512],
                              in0=tm1_sb[:, hf, :], in1=sg1_sb[:, hf, :]))
                gu_done = P.mark("vector", s_ve)
            for oc in range(NC):
                i = wctr[0]; buf = i % 3; wctr[0] += 1
                if i >= 3:
                    P.wait("gpsimd", s_mm, slab_mm_done[i - 3])
                h = s_wdma.inc(16)
                def fw(e, ee=ee, oc=oc, buf=buf, h=h):
                    sl = ee * 16 + oc
                    e.dma_start(out=wsl[:, buf, 0:8, :],
                                in_=w_dn[sl, :, :].rearrange("p (k e) -> p k e", k=8)
                                ).then_inc(h, 16)
                P.emit("gpsimd", fw)
                P.wait("tensor", s_wdma)
                if oc == 0:
                    dn_done = s_ve.n if ee == 0 else dn_done
                    P.wait("tensor", s_ac, ga_done)
                P.wait("tensor", s_ve, dn_done)
                def fmm(e, buf=buf):
                    for hf in range(2):
                        for kc in range(8):
                            e.matmul(ps_a[:, hf, :], wsl[:, buf, kc, :],
                                     hp_v(kc)[:, hf * 512:(hf + 1) * 512],
                                     start=(kc == 0), stop=(kc == 7))
                P.emit("tensor", fmm)
                P.mark("tensor", s_mm)
                slab_mm_done.append(s_mm.n)
                P.wait("vector", s_mm)
                for hf in range(2):
                    def fev(e, oc=oc, hf=hf):
                        e.scalar_tensor_tensor(out=x_sb[:, oc, hf * 512:(hf + 1) * 512],
                                               in0=ps_a[:, hf, :], scalar=1.0,
                                               in1=x_sb[:, oc, hf * 512:(hf + 1) * 512],
                                               op0=ALU.mult, op1=ALU.add)
                    P.emit("vector", fev)
                dn_done = P.mark("vector", s_ve)

    def store_out():
        P.wait("sync", s_ve)
        dma("sync", y_out[:, :, :], x_sb[:, :, :], sem=s_st)
        P.wait("sync", s_st)

    load_consts()
    if N_BLOCKS >= 1:
        layernorm()
        xn_ready = s_ve.n
        cv = l0_qkv(xn_ready)
        l0_attn(cv)
        l0_out()
    if N_BLOCKS >= 2:
        layernorm()
        l0_ffn()
    if N_BLOCKS >= 3:
        layernorm()
        l1_attn()
    if N_BLOCKS >= 4:
        layernorm()
        l1_ffn()
    store_out()

    with nc.Block() as block:
        @block.sync
        def _(e):
            er = EngRec(e)
            for f in P.streams["sync"]:
                f(er)
        @block.gpsimd
        def _(e):
            er = EngRec(e)
            for f in P.streams["gpsimd"]:
                f(er)
        @block.vector
        def _(e):
            er = EngRec(e)
            for f in P.streams["vector"]:
                f(er)
        @block.scalar
        def _(e):
            er = EngRec(e)
            for f in P.streams["scalar"]:
                f(er)
        @block.tensor
        def _(e):
            er = EngRec(e)
            for f in P.streams["tensor"]:
                f(er)
    return nc


def _slabify(wT):
    Dk, E = wT.shape
    nk = Dk // 128
    return np.ascontiguousarray(
        wT.reshape(nk, 128, E // 128, 128).transpose(2, 1, 0, 3).reshape(E // 128, 128, nk * 128))


def _slabify_narrow(wT):
    Dk, E = wT.shape
    nk = Dk // 128
    return np.ascontiguousarray(
        wT.reshape(nk, 128, E).transpose(1, 0, 2).reshape(1, 128, nk * E))


def _host_inputs(inputs):
    bf = ml_dtypes.bfloat16
    x = np.asarray(inputs["L0_attn_x"], dtype=np.float32)
    shared = {
        "w_l0q": _slabify(np.asarray(inputs["L0_attn_w_q"], np.float32).T),
        "w_l0k": _slabify(np.asarray(inputs["L0_attn_w_k"], np.float32).T),
        "w_l0v": _slabify(np.asarray(inputs["L0_attn_w_v"], np.float32).T),
        "w_l0o": _slabify(np.asarray(inputs["L0_attn_w_o"], np.float32).T),
        "w_l0g": _slabify_narrow(np.asarray(inputs["L0_ffn_W_gate"], np.float32).T),
        "w_l0qkv": _slabify(np.asarray(inputs["L0_ffn_W_qkv"], np.float32).T),
        "w_l1q": _slabify(np.asarray(inputs["L1_attn_Wq"], np.float32).T),
        "w_l1k": _slabify(np.asarray(inputs["L1_attn_Wk"], np.float32).T),
        "w_l1v": _slabify(np.asarray(inputs["L1_attn_Wv"], np.float32).T),
        "w_l1g": _slabify(np.asarray(inputs["L1_attn_Wg"], np.float32).T),
        "w_rtr": _slabify_narrow(np.asarray(inputs["L1_ffn_router_gate"], np.float32).T),
    }
    wdn = np.asarray(inputs["L1_ffn_expert_down"], np.float32)
    shared["w_dn"] = np.ascontiguousarray(
        np.concatenate([_slabify(wdn[e]) for e in range(8)], axis=0))
    wgu = np.asarray(inputs["L1_ffn_expert_gate_up"], np.float32)
    wgu = np.concatenate([wgu[:, :, 0::2], wgu[:, :, 1::2]], axis=2)
    shared["w_gu"] = np.ascontiguousarray(
        np.concatenate([_slabify(wgu[e]) for e in range(8)], axis=0))
    bl1 = np.stack([np.asarray(inputs["L1_attn_b" + k], np.float32).reshape(16, 128).T
                    for k in ("q", "k", "v", "g")], axis=1)
    shared["b_l1"] = np.ascontiguousarray(bl1)
    shared["ident_bf"] = np.eye(128, dtype=bf)
    shared["ident_f"] = np.eye(128, dtype=np.float32)
    hc = np.zeros((128, NH, NH), dtype=bf)
    for h in range(NH):
        hc[:, h, h] = 1.0
    shared["hotcols"] = hc
    shared["iota16"] = np.broadcast_to(np.arange(16, dtype=np.float32)[None, :], (128, 16)).copy()
    hr8 = np.zeros((8, 8, 128), dtype=bf)
    for e8 in range(8):
        hr8[e8, e8, :] = 1.0
    shared["hotrows8"] = hr8

    p = np.arange(128, dtype=np.uint32)
    in_maps = []
    for c in range(NCORE):
        g, r = c // 4, c % 4
        xs = np.ascontiguousarray(
            x[g, r * T:(r + 1) * T, :].T.reshape(NC, 128, T).transpose(1, 0, 2))
        gsel = np.zeros((128, 16), dtype=np.uint32)
        for s_ in range(4):
            for b_ in range(4):
                gsel[:, s_ * 4 + b_] = b_ * 2048 + (4 * r + s_) * 128 + p
        ysel = np.zeros((128, 16), dtype=np.uint32)
        for cc in range(16):
            ysel[:, cc] = (cc // 4) * 2048 + r * 512 + (cc % 4) * 128 + p
        pmask = np.zeros((128, 4), dtype=np.float32)
        pmask[:, :r] = 1.0
        m = dict(shared)
        m.update({"x0": xs, "gsel": gsel, "ysel": ysel, "pmask": pmask})
        in_maps.append(m)
    return in_maps


def kernel(**inputs):
    in_maps = _host_inputs(inputs)
    nc = build_program()
    res = run_bass_kernel_spmd(nc, in_maps, core_ids=list(range(NCORE)))
    out = np.empty((2, S, D), dtype=np.float32)
    for c in range(NCORE):
        g, r = c // 4, c % 4
        yc = np.asarray(res.results[c]["y"])
        out[g, r * T:(r + 1) * T, :] = yc.transpose(2, 1, 0).reshape(T, D)
    return out


# version 06
# speedup vs baseline: 1.2600x; 1.0158x over previous
import numpy as np
import ml_dtypes
import concourse.bass as bass
import concourse.mybir as mybir
from concourse.bass_utils import run_bass_kernel_spmd
from concourse.bass_core_rust import IndirectOffsetOnAxis
from contextlib import ExitStack

F32 = mybir.dt.float32
BF16 = mybir.dt.bfloat16
U32 = mybir.dt.uint32
ALU = mybir.AluOpType
ACT = mybir.ActivationFunctionType
AXX = mybir.AxisListType.X

NCORE = 8
T = 1024
D = 2048
NC = 16
NH = 16
S = 4096
SCALE = float(1.0 / np.sqrt(128.0))
NE = 8
DI = 1024

N_BLOCKS = 4

RACE_CHECK = False
STRICT_EVAC = False


class Sem:
    def __init__(self, nc, name):
        self.h = nc.alloc_semaphore(name)
        self.n = 0

    def inc(self, k):
        self.n += k
        return self.h


class Prog:
    ENGINES = ("sync", "gpsimd", "vector", "scalar", "tensor")

    def __init__(self, nc):
        self.nc = nc
        self.streams = {k: [] for k in self.ENGINES}
        self.last_done = {k: None for k in self.ENGINES}

    def emit(self, eng, fn, kind="op"):
        self.streams[eng].append(fn)
        if kind == "op":
            self.last_done[eng] = None

    def wait(self, eng, sem, val=None):
        v = sem.n if val is None else val
        h = sem.h
        if v > 0:
            self.emit(eng, lambda e, h=h, v=v: e.wait_ge(h, v), kind="wait")

    def mark(self, eng, sem, k=1):
        h = sem.inc(k)
        v = k
        if self.last_done[eng] is None:
            self.emit(eng, lambda e, h=h, v=v: e.last.then_inc(h, v), kind="mark")
            self.last_done[eng] = (h, sem.n)
        else:
            h0, v0 = self.last_done[eng]
            def f(e, h=h, v=v, h0=h0, v0=v0):
                e.wait_ge(h0, v0)
                e.nop().then_inc(h, v)
            self.emit(eng, f, kind="mark")
        return sem.n

    def ssync(self, eng):
        if self.last_done[eng] is not None:
            h, v = self.last_done[eng]
            self.emit(eng, lambda e, h=h, v=v: e.wait_ge(h, v), kind="wait")
        else:
            sem = self.chain[eng]
            self.mark(eng, sem)
            self.wait(eng, sem)

    def seq(self, eng, *fns):
        for i, f in enumerate(fns):
            if i:
                self.ssync(eng)
            self.emit(eng, f)


class EngRec:
    def __init__(self, e):
        object.__setattr__(self, "_e", e)
        object.__setattr__(self, "last", None)

    _NO_RECORD = ("wait_ge", "wait_op", "sem_inc", "nop", "maybe_wait_ge")

    def __getattr__(self, name):
        f = getattr(object.__getattribute__(self, "_e"), name)
        if not callable(f):
            return f
        rec = name not in EngRec._NO_RECORD
        def call(*a, **kw):
            r = f(*a, **kw)
            if rec and hasattr(r, "then_inc"):
                object.__setattr__(self, "last", r)
            return r
        return call


def build_program():
    nc = bass.Bass("TRN2", target_bir_lowering=False, debug=False,
                   detect_race_conditions=RACE_CHECK)

    def dram_in(name, shape, dt=F32):
        return nc.dram_tensor(name, list(shape), dt, kind="ExternalInput").ap()

    x0 = dram_in("x0", [128, NC, T])
    w_l0q = dram_in("w_l0q", [16, 128, D]); w_l0k = dram_in("w_l0k", [16, 128, D])
    w_l0v = dram_in("w_l0v", [16, 128, D]); w_l0o = dram_in("w_l0o", [16, 128, D])
    w_l0g = dram_in("w_l0g", [1, 128, 16 * 16]); w_l0qkv = dram_in("w_l0qkv", [48, 128, D])
    w_l1q = dram_in("w_l1q", [16, 128, D]); w_l1k = dram_in("w_l1k", [16, 128, D])
    w_l1v = dram_in("w_l1v", [16, 128, D]); w_l1g = dram_in("w_l1g", [16, 128, D])
    b_l1 = dram_in("b_l1", [128, 4, NC])
    w_rtr = dram_in("w_rtr", [1, 128, 16 * 8])
    w_gu = dram_in("w_gu", [NE * 16, 128, D])
    w_dn = dram_in("w_dn", [NE * 16, 128, 8 * 128])
    ident_bf_in = dram_in("ident_bf", [128, 128], BF16)
    ident_f_in = dram_in("ident_f", [128, 128])
    hotcols_in = dram_in("hotcols", [128, NH, NH], BF16)
    gsel_in = dram_in("gsel", [128, 16], U32)
    ysel_in = dram_in("ysel", [128, 16], U32)
    pmask_in = dram_in("pmask", [128, 4])
    iota16_in = dram_in("iota16", [128, 16])
    hotrows8_in = dram_in("hotrows8", [8, 8, 128], BF16)

    y_out = nc.dram_tensor("y", [128, NC, T], F32, kind="ExternalOutput").ap()

    k_loc = nc.dram_tensor("k_loc", [D, T], BF16, kind="Internal").ap()
    v_loc = nc.dram_tensor("v_loc", [T, D], BF16, kind="Internal").ap()
    k_gth = nc.dram_tensor("k_gth", [4, D, T], BF16, kind="Internal").ap()
    v_gth = nc.dram_tensor("v_gth", [4, T, D], BF16, kind="Internal").ap()
    q_loc = nc.dram_tensor("q_loc", [D, T], BF16, kind="Internal").ap()
    q_gth = nc.dram_tensor("q_gth", [4 * D, T], BF16, kind="Internal").ap()
    kv_loc = nc.dram_tensor("kv_loc", [T, 256], BF16, kind="Internal").ap()
    kv_gth = nc.dram_tensor("kv_gth", [4 * T, 256], BF16, kind="Internal").ap()
    yh_loc = nc.dram_tensor("yh_loc", [4, 512, T], BF16, kind="Internal").ap()
    yh_gth = nc.dram_tensor("yh_gth", [4, 4, 512, T], BF16, kind="Internal").ap()
    pfx_loc = nc.dram_tensor("pfx_loc", [128, NC], F32, kind="Internal").ap()
    pfx_gth = nc.dram_tensor("pfx_gth", [4, 128, NC], F32, kind="Internal").ap()
    RG = [[0, 1, 2, 3], [4, 5, 6, 7]]

    ctx = ExitStack()
    sb = lambda n, s, d: ctx.enter_context(nc.sbuf_tensor(n, s, d))
    psf = lambda n, s: ctx.enter_context(nc.psum_tensor(n, s, F32))

    x_sb = sb("x_sb", [128, NC, T], F32)
    xn_ar = sb("xn_ar", [128, NC * T], BF16)
    qa_ar = sb("qa_ar", [128, NC * T], BF16)
    at_ar = sb("at_ar", [128, NC * T], BF16)
    wsl = sb("wsl", [128, 3, 16, 128], BF16)
    sq_sb = sb("sq_sb", [128, T], F32)
    sqb_sb = sb("sqb_sb", [128, T], BF16)
    vec_sb = sb("vec_sb", [128, 2, T], F32)
    ones_b = sb("ones_b", [128, 128], BF16)
    ones_f = sb("ones_f", [128, 128], F32)
    ident_b = sb("ident_b", [128, 128], BF16)
    ident_f = sb("identf_sb", [128, 128], F32)
    hot_sb = sb("hot_sb", [128, NH, NH], BF16)
    nrm_sb = sb("nrm_sb", [16, 16], F32)
    cvec_sb = sb("cvec_sb", [128, NH], F32)
    gsel_sb = sb("gsel_sb", [128, 16], U32)
    ysel_sb = sb("ysel_sb", [128, 16], U32)
    pmask_sb = sb("pmask_sb", [128, 4], F32)
    bl1_sb = sb("bl1_sb", [128, 4, NC], F32)
    stg_sb = sb("stg_sb", [128, 16, 128], BF16)
    ev_sb = sb("ev_sb", [128, 2, 512], F32)
    gt_sb = sb("gt_sb", [128, 16], F32)
    mx8_sb = sb("mx8_sb", [128, 8], F32)
    mi8_sb = sb("mi8_sb", [128, 8], U32)
    sel_sb = sb("sel_sb", [128, 8, 4], F32)
    msk_sb = sb("msk_sb", [128, 8, 16], F32)
    iota16_sb = sb("iota16_sb", [128, 16], F32)
    cb4_sb = sb("cb4_sb", [128, 4], F32)
    hr8_sb = sb("hr8_sb", [8, 8, 128], BF16)
    l1b_sb = sb("l1b_sb", [128, 16], F32)
    wb1_sb = sb("wb1_sb", [128, T], BF16)
    sg1_sb = sb("sg1_sb", [128, 2, 512], BF16)
    tm1_sb = sb("tm1_sb", [128, 2, 512], BF16)

    ps_a = psf("ps_a", [128, 2, 512])
    ps_b = psf("ps_b", [128, 2, 512])
    ps_c = psf("ps_c", [128, 4, 512])

    def arview(ar, c=NC):
        return ar.ap().rearrange("p (c t) -> p c t", c=c)
    xn_v = arview(xn_ar)
    q_v = arview(qa_ar)
    at_v = arview(at_ar)
    kh_v = xn_ar.ap()[:, 0:4096].rearrange("p (j q) -> p j q", j=32)
    vh_v = xn_ar.ap()[:, 4096:8192].rearrange("p (j h) -> p j h", j=32)
    ptA = at_ar.ap()[:, 0:1024]
    ptB = at_ar.ap()[:, 1024:2048]
    def attn_chunk(c):
        if c < 14:
            return at_ar.ap()[:, 2048 + c * 1024: 2048 + (c + 1) * 1024]
        return xn_ar.ap()[:, 8192 + (c - 14) * 1024: 8192 + (c - 13) * 1024]
    nrm_psv = ps_b[0:16, :, :].rearrange("p a t -> p (a t)")
    ksh_v = at_ar.ap()[:, 0:4096].rearrange("p (j q) -> p j q", j=32)
    vtm_v = at_ar.ap()[:, 4096:8192].rearrange("p (j q) -> p j q", j=32)
    ptfA = at_ar.ap()[:, 8192:9216]
    ptfB = at_ar.ap()[:, 9216:10240]
    ystg_v = at_ar.ap()[:, 10240:10752].rearrange("p (a t) -> p a t", a=2)
    ksel_v = at_ar.ap()[:, 10752:11776].rearrange("p (tt e) -> p tt e", tt=8)
    vsel_v = at_ar.ap()[:, 11776:12800].rearrange("p (tt e) -> p tt e", tt=8)
    ytmp_v = at_ar.ap()[:, 12800:14848].rearrange("p (a t) -> p a t", a=2)
    qmine_v = qa_ar.ap().rearrange("p (s b t) -> p s b t", s=4, b=4)
    m_psv = ps_b[0:1, 0, :]
    m2_psv = ps_b[0:1, 1, :]
    vecA = vec_sb[0:1, 0, :]
    vecB = vec_sb[0:1, 1, :]
    A_bc = ps_c.ap().rearrange("p c t -> p (c t)")[:, 0:1024]
    B_bc = ps_c.ap().rearrange("p c t -> p (c t)")[:, 1024:2048]

    s_wdma = Sem(nc, "s_wdma")
    s_mm = Sem(nc, "s_mm")
    s_ve = Sem(nc, "s_ve")
    s_ac = Sem(nc, "s_ac")
    s_ge = Sem(nc, "s_ge")
    s_io = Sem(nc, "s_io")
    s_cc = Sem(nc, "s_cc")
    s_st = Sem(nc, "s_st")
    s_ind = Sem(nc, "s_ind")

    P = Prog(nc)
    P.chain = {eng: Sem(nc, "s_chain_" + eng) for eng in Prog.ENGINES}

    def dma(eng, out, in_, sem=None, k=16):
        if sem is None:
            sem = s_io
        h = sem.inc(k)
        def f(e, out=out, in_=in_, h=h, k=k):
            e.dma_start(out=out, in_=in_).then_inc(h, k)
        P.emit(eng, f)

    def load_consts():
        dma("sync", ident_b[:], ident_bf_in[:, :])
        dma("sync", ident_f[:], ident_f_in[:, :])
        dma("sync", hot_sb[:], hotcols_in[:, :, :])
        dma("sync", gsel_sb[:], gsel_in[:, :])
        dma("sync", ysel_sb[:], ysel_in[:, :])
        dma("sync", pmask_sb[:], pmask_in[:, :])
        dma("sync", bl1_sb[:], b_l1[:, :, :])
        dma("sync", iota16_sb[:], iota16_in[:, :])
        dma("sync", hr8_sb[:], hotrows8_in[:, :, :])
        dma("sync", x_sb[:], x0[:, :, :])
        def g(e):
            e.memset(ones_b[:], 1.0)
            e.memset(ones_f[:], 1.0)
        P.emit("gpsimd", g)
        P.mark("gpsimd", s_ge)

    def layernorm(chunk_marks=None):
        x_ready = s_io.n
        P.wait("scalar", s_io, x_ready)
        P.wait("tensor", s_io, x_ready)
        if chunk_marks is None:
            ve_ready = s_ve.n
            P.wait("scalar", s_ve, ve_ready)
            P.wait("tensor", s_ve, ve_ready)
        P.wait("tensor", s_ge, 1)
        sqm = []
        for c in range(NC):
            if chunk_marks is not None:
                P.wait("scalar", s_ve, chunk_marks[c])
            if c >= 1:
                P.wait("scalar", s_mm, sqm[c - 1])
            def fsq(e, c=c):
                e.activation(out=sq_sb[:, :], in_=x_sb[:, c, :], func=ACT.Square)
            P.emit("scalar", fsq)
            P.mark("scalar", s_ac)
            P.wait("tensor", s_ac)
            def fmm(e, c=c):
                for hf in range(2):
                    e.matmul(ps_c[0:1, hf, :], ones_f[:, 0:1],
                             x_sb[:, c, hf * 512:(hf + 1) * 512],
                             start=(c == 0), stop=(c == NC - 1))
                    e.matmul(ps_c[0:1, 2 + hf, :], ones_f[:, 0:1],
                             sq_sb[:, hf * 512:(hf + 1) * 512],
                             start=(c == 0), stop=(c == NC - 1))
            P.emit("tensor", fmm)
            sqm.append(P.mark("tensor", s_mm))
        P.wait("vector", s_mm)
        mfv = ps_c[0:1, :, :].rearrange("p a t -> p (a t)")
        P.seq("vector",
              lambda e: e.tensor_scalar_mul(out=vecA[:], in0=mfv[:, 0:1024], scalar1=1.0 / D),
              lambda e: e.tensor_mul(out=vecB[:], in0=vecA[:], in1=vecA[:]),
              lambda e: e.scalar_tensor_tensor(out=vecB[:], in0=mfv[:, 1024:2048],
                                               scalar=1.0 / float(D), in1=vecB[:],
                                               op0=ALU.mult, op1=ALU.subtract),
              lambda e: e.tensor_scalar_add(out=vecB[:], in0=vecB[:], scalar1=1e-5))
        P.mark("vector", s_ve)
        P.wait("scalar", s_ve)
        def fsr(e):
            e.activation(out=sq_sb[0:1, 0:1024], in_=vecB[:], func=ACT.Sqrt)
        P.emit("scalar", fsr)
        P.mark("scalar", s_ac)
        P.wait("vector", s_ac)
        P.seq("vector",
              lambda e: e.reciprocal(out=vecB[:], in_=sq_sb[0:1, 0:1024]),
              lambda e: e.tensor_mul(out=vecA[:], in0=vecA[:], in1=vecB[:]))
        P.mark("vector", s_ve)
        P.wait("tensor", s_ve)
        def fbc(e):
            for hf in range(2):
                e.matmul(ps_c[:, hf, :], ones_f[0:1, :], vecB[0:1, hf * 512:(hf + 1) * 512],
                         start=True, stop=True)
                e.matmul(ps_c[:, 2 + hf, :], ones_f[0:1, :], vecA[0:1, hf * 512:(hf + 1) * 512],
                         start=True, stop=True)
        P.emit("tensor", fbc)
        P.mark("tensor", s_mm)
        P.wait("vector", s_mm)
        for c in range(NC):
            P.seq("vector",
                  lambda e, c=c: e.tensor_mul(out=sq_sb[:, :], in0=x_sb[:, c, :], in1=A_bc),
                  lambda e, c=c: e.tensor_sub(out=xn_v[:, c, :], in0=sq_sb[:, :], in1=B_bc))
            P.ssync("vector")
        P.mark("vector", s_ve)

    wctr = [0]
    slab_mm_done = []

    def projection(w_dram, col0, n_out, rhs_fn, evac_fn, nkc=NC, xn_ready=None,
                   hook=None):
        if xn_ready is not None:
            P.wait("tensor", s_ve, xn_ready)
        evm = {0: s_ve.n, 1: s_ve.n}
        cmarks = []
        for c in range(n_out):
            i = wctr[0]
            buf = i % 3
            wctr[0] += 1
            if i >= 3:
                P.wait("gpsimd", s_mm, slab_mm_done[i - 3])
            h = s_wdma.inc(16)
            def fw(e, c=c, buf=buf, h=h, w_dram=w_dram, col0=col0, nkc=nkc):
                sl = col0 // 128 + c
                e.dma_start(out=wsl[:, buf, 0:nkc, :],
                            in_=w_dram[sl, :, :].rearrange("p (k e) -> p k e", k=nkc)
                            ).then_inc(h, 16)
            P.emit("gpsimd", fw)
            P.wait("tensor", s_wdma)
            for hf in range(2):
                P.wait("tensor", s_ve, evm[hf])
                def fmm(e, c=c, buf=buf, hf=hf, rhs_fn=rhs_fn, nkc=nkc):
                    for kc in range(nkc):
                        e.matmul(ps_a[:, hf, :], wsl[:, buf, kc, :], rhs_fn(kc, hf),
                                 start=(kc == 0), stop=(kc == nkc - 1))
                P.emit("tensor", fmm)
                P.mark("tensor", s_mm)
                P.wait("vector", s_mm)
                evac_fn(c, hf)
                evm[hf] = P.mark("vector", s_ve)
                if STRICT_EVAC:
                    P.wait("tensor", s_ve)
            cmarks.append(s_ve.n)
            if hook is not None:
                hook(c)
            slab_mm_done.append(s_mm.n)
        return cmarks

    def xn_rhs(kc, hf):
        return xn_v[:, kc, hf * 512:(hf + 1) * 512]

    def l0_qkv(xn_ready):
        def evq(c, hf):
            def f(e, c=c, hf=hf):
                e.tensor_copy(out=q_v[:, c, hf * 512:(hf + 1) * 512], in_=ps_a[:, hf, :])
            P.emit("vector", f)
        projection(w_l0q, 0, NC, xn_rhs, evq, xn_ready=xn_ready)
        def evk(c, hf):
            def f(e, c=c, hf=hf):
                e.tensor_copy(out=at_v[:, c, hf * 512:(hf + 1) * 512], in_=ps_a[:, hf, :])
            P.emit("vector", f)
        projection(w_l0k, 0, NC, xn_rhs, evk)
        k_done = s_ve.n
        P.wait("sync", s_ve, k_done)
        dma("sync", k_loc.rearrange("(c p) t -> p c t", p=128), at_v[:, :, :])
        k_dma = s_io.n
        KAG_EARLY = False
        if KAG_EARLY:
            P.wait("gpsimd", s_io, k_dma)
            h1 = s_cc.inc(1)
            def fcc1(e, h1=h1):
                e.collective_compute("AllGather", ALU.bypass, replica_groups=RG,
                                     ins=[k_loc[:, :]], outs=[k_gth[:, :, :]]).then_inc(h1, 1)
            P.emit("gpsimd", fcc1)
        sqm = []
        for c in range(NC):
            P.wait("scalar", s_ve, k_done)
            if c >= 1:
                P.wait("scalar", s_mm, sqm[c - 1])
            def fsq(e, c=c):
                e.activation(out=sqb_sb[:, :], in_=at_v[:, c, :], func=ACT.Square)
            P.emit("scalar", fsq)
            P.mark("scalar", s_ac)
            P.wait("tensor", s_ac)
            def fnm(e, c=c):
                for hf in range(2):
                    e.matmul(nrm_psv[:, hf * 512:(hf + 1) * 512], hot_sb[:, c, :],
                             sqb_sb[:, hf * 512:(hf + 1) * 512],
                             start=(c == 0), stop=(c == NC - 1))
            P.emit("tensor", fnm)
            sqm.append(P.mark("tensor", s_mm))
        P.wait("vector", s_mm)
        def fkn(e):
            e.reduce_max(out=nrm_sb[:, 0:1], in_=nrm_psv[:, :], axis=AXX)
        P.emit("vector", fkn)
        kn_done = P.mark("vector", s_ve)
        P.wait("vector", s_io, k_dma)
        def evv(c, hf):
            def f(e, c=c, hf=hf):
                e.tensor_copy(out=at_v[:, c, hf * 512:(hf + 1) * 512], in_=ps_a[:, hf, :])
            P.emit("vector", f)
        projection(w_l0v, 0, NC, xn_rhs, evv)
        P.wait("tensor", s_ve)
        tp = 0
        for c in range(NC):
            for tt in range(8):
                par = tp % 2
                tp += 1
                def ftr(e, c=c, tt=tt, par=par):
                    e.matmul(ps_c[:, par, 0:128], at_v[:, c, tt * 128:(tt + 1) * 128],
                             ident_b[:], start=True, stop=True)
                P.emit("tensor", ftr)
                P.mark("tensor", s_mm)
                P.wait("vector", s_mm)
                def fev(e, tt=tt, par=par):
                    e.tensor_copy(out=stg_sb[:, tt, :], in_=ps_c[:, par, 0:128])
                P.emit("vector", fev)
                P.mark("vector", s_ve)
                if tp >= 2:
                    P.wait("tensor", s_ve)
            P.wait("sync", s_ve)
            dma("sync", v_loc.rearrange("(tt p) e -> p tt e", p=128)[:, :, c * 128:(c + 1) * 128],
                stg_sb[:, 0:8, :])
            P.wait("vector", s_io)
        v_dma = s_io.n
        sqm2 = []
        for c in range(NC):
            if c == 0:
                P.wait("tensor", s_ve, kn_done)
            if c >= 1:
                P.wait("scalar", s_mm, sqm2[c - 1])
            def fsq(e, c=c):
                e.activation(out=sqb_sb[:, :], in_=q_v[:, c, :], func=ACT.Square)
            P.emit("scalar", fsq)
            P.mark("scalar", s_ac)
            P.wait("tensor", s_ac)
            def fnm(e, c=c):
                for hf in range(2):
                    e.matmul(nrm_psv[:, hf * 512:(hf + 1) * 512], hot_sb[:, c, :],
                             sqb_sb[:, hf * 512:(hf + 1) * 512],
                             start=(c == 0), stop=(c == NC - 1))
            P.emit("tensor", fnm)
            sqm2.append(P.mark("tensor", s_mm))
        P.wait("vector", s_mm)
        P.seq("vector",
              lambda e: e.reduce_max(out=nrm_sb[:, 1:2], in_=nrm_psv[:, :], axis=AXX),
              lambda e: e.tensor_mul(out=nrm_sb[:, 2:3], in0=nrm_sb[:, 0:1], in1=nrm_sb[:, 1:2]))
        P.mark("vector", s_ve)
        P.wait("scalar", s_ve)
        def fsr(e):
            e.activation(out=nrm_sb[:, 3:4], in_=nrm_sb[:, 2:3], func=ACT.Sqrt)
        P.emit("scalar", fsr)
        P.mark("scalar", s_ac)
        P.wait("vector", s_ac)
        P.seq("vector",
              lambda e: e.tensor_scalar_mul(out=nrm_sb[:, 4:5], in0=nrm_sb[:, 3:4], scalar1=-SCALE),
              lambda e: e.tensor_scalar(out=ev_sb[0:16, 0, 0:16], in0=ident_f[0:16, 0:16],
                                        scalar1=nrm_sb[:, 4:5], scalar2=None, op0=ALU.mult))
        P.mark("vector", s_ve)
        P.wait("tensor", s_ve)
        def fcb(e):
            e.matmul(ps_b[:, 1, 0:16], ones_f[0:16, :], ev_sb[0:16, 0, 0:16],
                     start=True, stop=True)
        P.emit("tensor", fcb)
        P.mark("tensor", s_mm)
        P.wait("vector", s_mm)
        def fcv(e):
            e.tensor_copy(out=cvec_sb[:, :], in_=ps_b[:, 1, 0:16])
        P.emit("vector", fcv)
        cvec_done = P.mark("vector", s_ve)
        P.wait("gpsimd", s_io, v_dma)
        if not KAG_EARLY:
            h1 = s_cc.inc(1)
            def fcc1(e, h1=h1):
                e.collective_compute("AllGather", ALU.bypass, replica_groups=RG,
                                     ins=[k_loc[:, :]], outs=[k_gth[:, :, :]]).then_inc(h1, 1)
            P.emit("gpsimd", fcc1)
        h2 = s_cc.inc(1)
        def fcc2(e, h2=h2):
            e.collective_compute("AllGather", ALU.bypass, replica_groups=RG,
                                 ins=[v_loc[:, :]], outs=[v_gth[:, :, :]]).then_inc(h2, 1)
        P.emit("gpsimd", fcc2)
        return cvec_done

    def l0_attn(cvec_done):
        cc_done = s_cc.n
        P.wait("sync", s_cc, cc_done)
        P.wait("scalar", s_ve, cvec_done)
        for h in range(NH):
            for sl in range(4):
                dma("sync", kh_v.rearrange("p j q -> p (j q)")[:, sl * 1024:(sl + 1) * 1024],
                    k_gth[sl, h * 128:(h + 1) * 128, :])
            for sl in range(4):
                dma("sync",
                    vh_v.rearrange("p j h2 -> p (j h2)")[:, sl * 1024:(sl + 1) * 1024]
                    .rearrange("p (tt e) -> p tt e", tt=8),
                    v_gth[sl].rearrange("(tt p) e -> p tt e", p=128)[:, :, h * 128:(h + 1) * 128])
            ld_done = s_io.n
            P.wait("tensor", s_io, ld_done)
            P.wait("scalar", s_io, ld_done)
            for hf in range(2):
                def fst(e, h=h, hf=hf, jc=None, par=None):
                    for jj in range(2):
                        j = jc * 2 + jj
                        e.matmul(ps_c[:, par * 2 + jj, :], kh_v[:, j, :],
                                 q_v[:, h, hf * 512:(hf + 1) * 512],
                                 start=True, stop=True)
                import functools
                P.emit("tensor", functools.partial(fst, jc=0, par=0))
                st_marks = [P.mark("tensor", s_mm)]
                for jc in range(16):
                    par = jc % 2
                    if jc + 1 < 16:
                        P.emit("tensor", functools.partial(fst, jc=jc + 1, par=(jc + 1) % 2))
                        st_marks.append(P.mark("tensor", s_mm))
                    P.wait("scalar", s_mm, st_marks[jc])
                    def fex(e, h=h, par=par):
                        src = ps_c[:, par * 2:par * 2 + 2, :].rearrange("p a t -> p (a t)")
                        dst = ptA if par == 0 else ptB
                        e.activation(out=dst, in_=src, func=ACT.Exp,
                                     bias=cvec_sb[:, h:h + 1], scale=SCALE)
                    P.emit("scalar", fex)
                    P.mark("scalar", s_ac)
                    P.wait("tensor", s_ac)
                    def fpv(e, hf=hf, jc=jc, par=par):
                        for jj in range(2):
                            j = jc * 2 + jj
                            pt = (ptA if par == 0 else ptB)[:, jj * 512:(jj + 1) * 512]
                            e.matmul(ps_a[:, 0, :], vh_v[:, j, :], pt,
                                     start=(j == 0), stop=(j == 31))
                            e.matmul(ps_a[:, 1, :], ones_b[:], pt,
                                     start=(j == 0), stop=(j == 31))
                    P.emit("tensor", fpv)
                    P.mark("tensor", s_mm)
                P.wait("vector", s_mm)
                P.seq("vector",
                      lambda e: e.reciprocal(out=ev_sb[:, 0, :], in_=ps_a[:, 1, :]),
                      lambda e, h=h, hf=hf: e.tensor_mul(
                          out=attn_chunk(h)[:, hf * 512:(hf + 1) * 512],
                          in0=ps_a[:, 0, :], in1=ev_sb[:, 0, :]))
                P.mark("vector", s_ve)
                P.wait("tensor", s_ve)
            P.wait("sync", s_mm)

    def l0_ffn():
        xn_ok = s_ve.n
        gbuf = wctr[0] % 3
        if wctr[0] >= 3:
            P.wait("gpsimd", s_mm, slab_mm_done[wctr[0] - 3])
        h = s_wdma.inc(16)
        def fwg(e, h=h, gbuf=gbuf):
            e.dma_start(out=wsl[:, gbuf, :, 0:16],
                        in_=w_l0g[0, :, :].rearrange("p (k e) -> p k e", k=16)).then_inc(h, 16)
        P.emit("gpsimd", fwg)
        P.wait("tensor", s_wdma)
        P.wait("tensor", s_ve, xn_ok)
        def fgm(e, gbuf=gbuf):
            for hf in range(2):
                for kc in range(NC):
                    e.matmul(nrm_psv[:, hf * 512:(hf + 1) * 512], wsl[:, gbuf, kc, 0:16],
                             xn_v[:, kc, hf * 512:(hf + 1) * 512],
                             start=(kc == 0), stop=(kc == NC - 1))
        P.emit("tensor", fgm)
        P.mark("tensor", s_mm)
        wctr[0] += 3
        slab_mm_done.extend([s_mm.n, s_mm.n, s_mm.n])
        P.wait("vector", s_mm)
        def fge(e):
            e.tensor_copy(out=stg_sb[0:16, :, :].rearrange("p a b -> p (a b)")[:, 0:1024],
                          in_=nrm_psv[:, :])
        P.emit("vector", fge)
        P.mark("vector", s_ve)
        glog = stg_sb[0:16, :, :].rearrange("p a b -> p (a b)")
        P.wait("tensor", s_ve)
        for tt in range(8):
            par = tt % 2
            def ftr(e, tt=tt, par=par):
                e.matmul(ps_c[:, par, 0:16], glog[:, tt * 128:(tt + 1) * 128],
                         ident_b[0:16, 0:16], start=True, stop=True)
            P.emit("tensor", ftr)
            P.mark("tensor", s_mm)
            P.wait("vector", s_mm)
            P.seq("vector",
                  lambda e, par=par: e.tensor_copy(out=gt_sb[:, :], in_=ps_c[:, par, 0:16]),
                  lambda e: e.max(mx8_sb[:, 0:8], gt_sb[:, :]),
                  lambda e: e.max_index(mi8_sb[:, 0:8], mx8_sb[:, 0:8], gt_sb[:, :]))
            P.ssync("vector")
            def frt(e, tt=tt):
                e.tensor_copy(out=sel_sb[:, tt, 0:1], in_=mi8_sb[:, 0:1])
                e.reduce_max(out=sel_sb[:, tt, 1:2], in_=gt_sb[:, :], axis=AXX, negate=True)
            P.emit("vector", frt)
            P.mark("vector", s_ve)
            P.wait("scalar", s_ve)
            def fex(e, tt=tt):
                e.activation(out=gt_sb[:, :], in_=gt_sb[:, :], func=ACT.Exp,
                             bias=sel_sb[:, tt, 1:2], scale=1.0)
            P.emit("scalar", fex)
            P.mark("scalar", s_ac)
            P.wait("vector", s_ac)
            P.seq("vector",
                  lambda e, tt=tt: e.reduce_sum(out=sel_sb[:, tt, 2:3], in_=gt_sb[:, :], axis=AXX),
                  lambda e, tt=tt: e.reciprocal(out=sel_sb[:, tt, 3:4], in_=sel_sb[:, tt, 2:3]),
                  lambda e, tt=tt: e.tensor_single_scalar(out=msk_sb[:, tt, :], in_=iota16_sb[:, :],
                                                          scalar=sel_sb[:, tt, 0:1], op=ALU.is_equal))
            P.mark("vector", s_ve)
            P.wait("tensor", s_ve)
        msk_done = s_ve.n
        qstg = at_ar.ap()[:, 0:2048]
        qdma = []
        def evqf(c, hf):
            slot = c % 2
            def f(e, c=c, hf=hf, slot=slot):
                e.tensor_copy(out=qstg[:, slot * 1024 + hf * 512: slot * 1024 + (hf + 1) * 512],
                              in_=ps_a[:, hf, :])
            P.emit("vector", f)
            if hf == 1:
                P.wait("sync", s_ve, s_ve.n + 1)
                dma("sync", q_loc[c * 128:(c + 1) * 128, :],
                    qstg[:, slot * 1024:(slot + 1) * 1024])
                qdma.append(s_io.n)
        def evqf2(c, hf):
            if c >= 2 and hf == 0:
                P.wait("vector", s_io, qdma[c - 2])
            evqf(c, hf)
        projection(w_l0qkv, 0, NC, xn_rhs, evqf2)
        kvt = at_ar.ap()[:, 0:1024]
        tsel = at_ar.ap()[:, 2048:3072].rearrange("p (tt e) -> p tt e", tt=8)
        for kv_i, (colbase, dst) in enumerate(((2048, ksel_v), (4096, vsel_v))):
            if kv_i == 0:
                P.wait("vector", s_io)
                P.wait("vector", s_ve, msk_done)
            def evkv(c, hf, kvt=kvt):
                def f(e, c=c, hf=hf):
                    e.tensor_copy(out=kvt[:, hf * 512:(hf + 1) * 512], in_=ps_a[:, hf, :])
                P.emit("vector", f)
            def hook(c):
                for tt in range(8):
                    par = tt % 2
                    P.wait("tensor", s_ve)
                    def ftr(e, tt=tt, par=par, kvt=kvt):
                        e.matmul(ps_c[:, 2 + par, 0:128], kvt[:, tt * 128:(tt + 1) * 128],
                                 ident_b[:], start=True, stop=True)
                    P.emit("tensor", ftr)
                    P.mark("tensor", s_mm)
                    P.wait("vector", s_mm)
                    def fev(e, tt=tt, par=par):
                        e.tensor_copy(out=tsel[:, tt, :], in_=ps_c[:, 2 + par, 0:128])
                    P.emit("vector", fev)
                    P.mark("vector", s_ve)
                P.ssync("vector")
                def facc(e, c=c, dst=dst):
                    for tt in range(8):
                        if c == 0:
                            e.tensor_scalar(out=dst[:, tt, :], in0=tsel[:, tt, :],
                                            scalar1=msk_sb[:, tt, c:c + 1], scalar2=None,
                                            op0=ALU.mult)
                        else:
                            e.scalar_tensor_tensor(out=dst[:, tt, :], in0=tsel[:, tt, :],
                                                   scalar=msk_sb[:, tt, c:c + 1],
                                                   in1=dst[:, tt, :],
                                                   op0=ALU.mult, op1=ALU.add)
                P.emit("vector", facc)
                P.mark("vector", s_ve)
            projection(w_l0qkv, colbase, NC, xn_rhs, evkv, hook=hook)
        P.ssync("vector")
        def fvt1(e):
            for tt in range(8):
                e.tensor_scalar(out=vsel_v[:, tt, :], in0=vsel_v[:, tt, :],
                                scalar1=sel_sb[:, tt, 3:4], scalar2=None, op0=ALU.mult)
        P.emit("vector", fvt1)
        P.mark("vector", s_ve)
        P.wait("sync", s_ve)
        dma("sync", kv_loc.rearrange("(tt p) e -> p tt e", p=128)[:, :, 0:128], ksel_v[:, :, :])
        dma("sync", kv_loc.rearrange("(tt p) e -> p tt e", p=128)[:, :, 128:256], vsel_v[:, :, :])
        kv_dma = s_io.n
        P.wait("gpsimd", s_io, kv_dma)
        hq = s_cc.inc(1)
        def fccq(e, hq=hq):
            e.collective_compute("AllGather", ALU.bypass, replica_groups=RG,
                                 ins=[q_loc[:, :]], outs=[q_gth[:, :]]).then_inc(hq, 1)
        P.emit("gpsimd", fccq)
        hkv = s_cc.inc(1)
        def fcckv(e, hkv=hkv):
            e.collective_compute("AllGather", ALU.bypass, replica_groups=RG,
                                 ins=[kv_loc[:, :]], outs=[kv_gth[:, :]]).then_inc(hkv, 1)
        P.emit("gpsimd", fcckv)
        cc_done = s_cc.n
        P.wait("sync", s_cc, cc_done)
        for j in range(32):
            hdt = s_io.inc(16)
            def ftd(e, j=j, hdt=hdt):
                e.dma_start(out=ksh_v[:, j, :], in_=kv_gth[j * 128:(j + 1) * 128, 0:128],
                            transpose=True).then_inc(hdt, 16)
            P.emit("sync", ftd)
        for j in range(32):
            dma("sync", vtm_v[:, j, :], kv_gth[j * 128:(j + 1) * 128, 128:256])
        P.wait("gpsimd", s_cc, cc_done)
        for col in range(16):
            hgi = s_ind.inc(16)
            def fgi(e, col=col, hgi=hgi):
                e.indirect_dma_start(out=qmine_v[:, col // 4, col % 4, :], out_offset=None,
                                     in_=q_gth[:, :],
                                     in_offset=IndirectOffsetOnAxis(gsel_sb[:, col:col + 1], 0)
                                     ).then_inc(hgi, 16)
            P.emit("gpsimd", fgi)
        ld_done = s_io.n
        qm_done = s_ind.n
        P.wait("scalar", s_io, ld_done)
        P.wait("scalar", s_ind, qm_done)
        P.wait("tensor", s_ind, qm_done)
        segs = []
        for seg in range(4):
            if seg >= 1:
                P.wait("scalar", s_mm)
            def fsq(e, seg=seg):
                e.activation(out=sqb_sb[:, :],
                             in_=ksh_v.rearrange("p j q -> p (j q)")[:, seg * 1024:(seg + 1) * 1024],
                             func=ACT.Square)
            P.emit("scalar", fsq)
            P.mark("scalar", s_ac)
            P.wait("tensor", s_ac)
            def fnm(e):
                for hf in range(2):
                    e.matmul(ps_b[0:1, hf, :], ones_b[:, 0:1], sqb_sb[:, hf * 512:(hf + 1) * 512],
                             start=True, stop=True)
            P.emit("tensor", fnm)
            P.mark("tensor", s_mm)
            P.wait("vector", s_mm)
            if seg == 0:
                def frd(e):
                    e.reduce_max(out=nrm_sb[0:1, 0:1],
                                 in_=ps_b[0:1, :, :].rearrange("p a t -> p (a t)"), axis=AXX)
                P.emit("vector", frd)
            else:
                P.seq("vector",
                      lambda e: e.reduce_max(out=nrm_sb[0:1, 1:2],
                                             in_=ps_b[0:1, :, :].rearrange("p a t -> p (a t)"),
                                             axis=AXX),
                      lambda e: e.tensor_max(out=nrm_sb[0:1, 0:1], in0=nrm_sb[0:1, 0:1],
                                             in1=nrm_sb[0:1, 1:2]))
            P.mark("vector", s_ve)
            P.wait("tensor", s_ve)
            segs.append(s_ve.n)
        for sl in range(4):
            for seg in range(4):
                P.wait("scalar", s_mm)
                def fsq(e, sl=sl, seg=seg):
                    e.activation(out=sqb_sb[:, :], in_=qmine_v[:, sl, seg, :], func=ACT.Square)
                P.emit("scalar", fsq)
                P.mark("scalar", s_ac)
                P.wait("tensor", s_ac)
                def fnm(e):
                    for hf in range(2):
                        e.matmul(ps_b[0:1, hf, :], ones_b[:, 0:1],
                                 sqb_sb[:, hf * 512:(hf + 1) * 512], start=True, stop=True)
                P.emit("tensor", fnm)
                P.mark("tensor", s_mm)
                P.wait("vector", s_mm)
                if seg == 0:
                    def frd(e, sl=sl):
                        e.reduce_max(out=nrm_sb[0:1, 2 + sl:3 + sl],
                                     in_=ps_b[0:1, :, :].rearrange("p a t -> p (a t)"), axis=AXX)
                    P.emit("vector", frd)
                else:
                    P.seq("vector",
                          lambda e: e.reduce_max(out=nrm_sb[0:1, 6:7],
                                                 in_=ps_b[0:1, :, :].rearrange("p a t -> p (a t)"),
                                                 axis=AXX),
                          lambda e, sl=sl: e.tensor_max(out=nrm_sb[0:1, 2 + sl:3 + sl],
                                                        in0=nrm_sb[0:1, 2 + sl:3 + sl],
                                                        in1=nrm_sb[0:1, 6:7]))
                P.mark("vector", s_ve)
                P.wait("tensor", s_ve)
        P.ssync("vector")
        def fcs(e):
            for sl in range(4):
                e.tensor_mul(out=nrm_sb[0:1, 8 + sl:9 + sl], in0=nrm_sb[0:1, 0:1],
                             in1=nrm_sb[0:1, 2 + sl:3 + sl])
        P.emit("vector", fcs)
        P.mark("vector", s_ve)
        P.wait("scalar", s_ve)
        def fsr(e):
            e.activation(out=nrm_sb[0:1, 12:16], in_=nrm_sb[0:1, 8:12], func=ACT.Sqrt)
        P.emit("scalar", fsr)
        P.mark("scalar", s_ac)
        P.wait("vector", s_ac)
        def fng(e):
            e.tensor_scalar_mul(out=nrm_sb[0:1, 12:16], in0=nrm_sb[0:1, 12:16], scalar1=-SCALE)
        P.emit("vector", fng)
        P.mark("vector", s_ve)
        P.wait("tensor", s_ve)
        def fbb(e):
            e.matmul(ps_b[:, 1, 0:4], ones_f[0:1, :], nrm_sb[0:1, 12:16], start=True, stop=True)
        P.emit("tensor", fbb)
        P.mark("tensor", s_mm)
        P.wait("vector", s_mm)
        def fcv(e):
            e.tensor_copy(out=cb4_sb[:, :], in_=ps_b[:, 1, 0:4])
        P.emit("vector", fcv)
        cb_done = P.mark("vector", s_ve)
        P.wait("scalar", s_ve, cb_done)
        ydma = []
        for sl in range(4):
            for pi in range(16):
                slab = (2 * pi) // 8
                off = ((2 * pi) % 8) * 128
                njs = 2 * pi + 2
                nch = (njs + 3) // 4
                import functools
                def fstf(e, sl=sl, slab=slab, off=off, par=None, j0=None, jn=None):
                    for jj in range(jn):
                        j = j0 + jj
                        e.matmul(ps_c[:, par * 2 + jj // 2, (jj % 2) * 256:(jj % 2 + 1) * 256],
                                 ksh_v[:, j, :], qmine_v[:, sl, slab, off:off + 256],
                                 start=True, stop=True)
                P.emit("tensor", functools.partial(fstf, par=0, j0=0, jn=min(4, njs)))
                stf_marks = [P.mark("tensor", s_mm)]
                for ch in range(nch):
                    par = ch % 2
                    j0 = ch * 4
                    jn = min(4, njs - j0)
                    if ch + 1 < nch:
                        P.emit("tensor", functools.partial(
                            fstf, par=(ch + 1) % 2, j0=(ch + 1) * 4,
                            jn=min(4, njs - (ch + 1) * 4)))
                        stf_marks.append(P.mark("tensor", s_mm))
                    P.wait("scalar", s_mm, stf_marks[ch])
                    def fex(e, sl=sl, par=par, jn=jn):
                        src = ps_c[:, par * 2:par * 2 + 2, :].rearrange("p a t -> p (a t)")[:, 0:jn * 256]
                        dst = (ptfA if par == 0 else ptfB)[:, 0:jn * 256]
                        e.activation(out=dst, in_=src, func=ACT.Exp,
                                     bias=cb4_sb[:, sl:sl + 1], scale=SCALE)
                    P.emit("scalar", fex)
                    P.mark("scalar", s_ac)
                    has_mask = (j0 + jn == njs)
                    if has_mask:
                        P.wait("gpsimd", s_ac)
                        def fmk(e, par=par, jn=jn):
                            buf = ptfA if par == 0 else ptfB
                            e.affine_select(out=buf[:, (jn - 2) * 256:(jn - 1) * 256],
                                            in_=buf[:, (jn - 2) * 256:(jn - 1) * 256],
                                            pattern=[[1, 256]], compare_op=ALU.is_ge,
                                            fill=0.0, base=0, channel_multiplier=-1)
                            e.affine_select(out=buf[:, (jn - 1) * 256:jn * 256],
                                            in_=buf[:, (jn - 1) * 256:jn * 256],
                                            pattern=[[1, 256]], compare_op=ALU.is_ge,
                                            fill=0.0, base=-128, channel_multiplier=-1)
                        P.emit("gpsimd", fmk)
                        P.mark("gpsimd", s_ge)
                        P.wait("tensor", s_ge)
                    else:
                        P.wait("tensor", s_ac)
                    def fpv(e, par=par, j0=j0, jn=jn, njs=njs):
                        buf = ptfA if par == 0 else ptfB
                        for jj in range(jn):
                            j = j0 + jj
                            pt = buf[:, jj * 256:(jj + 1) * 256]
                            e.matmul(ps_a[:, 0, 0:256], vtm_v[:, j, :], pt,
                                     start=(j == 0), stop=(j == njs - 1))
                            e.matmul(ps_a[:, 1, 0:256], ones_b[:], pt,
                                     start=(j == 0), stop=(j == njs - 1))
                    P.emit("tensor", fpv)
                    P.mark("tensor", s_mm)
                yslot = pi % 2
                if len(ydma) >= 2:
                    P.wait("vector", s_io, ydma[-2])
                P.wait("vector", s_mm)
                P.seq("vector",
                      lambda e: e.reciprocal(out=ev_sb[:, 0, 0:256], in_=ps_a[:, 1, 0:256]),
                      lambda e, yslot=yslot: e.tensor_mul(out=ystg_v[:, yslot, :],
                                                          in0=ps_a[:, 0, 0:256],
                                                          in1=ev_sb[:, 0, 0:256]))
                P.mark("vector", s_ve)
                P.wait("tensor", s_ve)
                P.wait("sync", s_ve)
                dma("sync", yh_loc[slab, sl * 128:(sl + 1) * 128, off:off + 256],
                    ystg_v[:, yslot, :])
                ydma.append(s_io.n)
        yh_done = s_io.n
        P.wait("gpsimd", s_io, yh_done)
        hy = s_cc.inc(1)
        def fccy(e, hy=hy):
            e.collective_compute("AllGather", ALU.bypass, replica_groups=RG,
                                 ins=[yh_loc[:, :, :]], outs=[yh_gth[:, :, :, :]]).then_inc(hy, 1)
        P.emit("gpsimd", fccy)
        ycc = s_cc.n
        P.wait("gpsimd", s_cc, ycc)
        yflat = yh_gth.rearrange("r s h t -> (r s h) t")
        P.ssync("vector")
        add_marks = []
        for c in range(NC):
            if c >= 2:
                P.wait("gpsimd", s_ve, add_marks[c - 2])
            hgi = s_ind.inc(16)
            def fgy(e, c=c, hgi=hgi):
                e.indirect_dma_start(out=ytmp_v[:, c % 2, :], out_offset=None,
                                     in_=yflat,
                                     in_offset=IndirectOffsetOnAxis(ysel_sb[:, c:c + 1], 0)
                                     ).then_inc(hgi, 16)
            P.emit("gpsimd", fgy)
            P.wait("vector", s_ind)
            def fad(e, c=c):
                e.tensor_add(out=x_sb[:, c, :], in0=x_sb[:, c, :], in1=ytmp_v[:, c % 2, :])
            P.emit("vector", fad)
            add_marks.append(P.mark("vector", s_ve))
        return add_marks

    def l0_out():
        P.ssync("vector")
        def attn_rhs(kc, hf):
            return attn_chunk(kc)[:, hf * 512:(hf + 1) * 512]
        def evo(c, hf):
            def f(e, c=c, hf=hf):
                e.scalar_tensor_tensor(out=x_sb[:, c, hf * 512:(hf + 1) * 512],
                                       in0=ps_a[:, hf, :], scalar=1.0,
                                       in1=x_sb[:, c, hf * 512:(hf + 1) * 512],
                                       op0=ALU.mult, op1=ALU.add)
            P.emit("vector", f)
        return projection(w_l0o, 0, NC, attn_rhs, evo)


    def l1_attn():
        P.ssync("vector")
        def evq1(c, hf):
            def f(e, c=c, hf=hf):
                e.tensor_scalar(out=q_v[:, c, hf * 512:(hf + 1) * 512], in0=ps_a[:, hf, :],
                                scalar1=bl1_sb[:, 0, c:c + 1], scalar2=None, op0=ALU.add)
            P.emit("vector", f)
        projection(w_l1q, 0, NC, xn_rhs, evq1, xn_ready=s_ve.n)
        def evk1(c, hf):
            def f(e, c=c, hf=hf):
                e.tensor_scalar(out=at_v[:, c, hf * 512:(hf + 1) * 512], in0=ps_a[:, hf, :],
                                scalar1=bl1_sb[:, 1, c:c + 1], scalar2=None, op0=ALU.add)
            P.emit("vector", f)
        projection(w_l1k, 0, NC, xn_rhs, evk1)
        def evv1(c, hf):
            def f(e, c=c, hf=hf):
                e.scalar_tensor_tensor(out=at_v[:, c, hf * 512:(hf + 1) * 512],
                                       in0=ps_a[:, hf, :], scalar=bl1_sb[:, 2, c:c + 1],
                                       in1=at_v[:, c, hf * 512:(hf + 1) * 512],
                                       op0=ALU.add, op1=ALU.mult)
            P.emit("vector", f)
        projection(w_l1v, 0, NC, xn_rhs, evv1)
        def evg1(c, hf):
            P.wait("scalar", s_mm)
            def fg(e, c=c, hf=hf):
                e.activation(out=sg1_sb[:, hf, :], in_=ps_a[:, hf, :], func=ACT.Sigmoid,
                             bias=bl1_sb[:, 3, c:c + 1], scale=1.0)
            P.emit("scalar", fg)
            P.mark("scalar", s_ac)
            P.wait("vector", s_ac)
            def fq(e, c=c, hf=hf):
                e.tensor_mul(out=q_v[:, c, hf * 512:(hf + 1) * 512],
                             in0=q_v[:, c, hf * 512:(hf + 1) * 512], in1=sg1_sb[:, hf, :])
            P.emit("vector", fq)
        projection(w_l1g, 0, NC, xn_rhs, evg1)
        P.ssync("vector")
        for c in range(NC):
            def fs(e, c=c):
                e.reduce_sum(out=cvec_sb[:, c:c + 1], in_=at_v[:, c, :], axis=AXX)
            P.emit("vector", fs)
        P.mark("vector", s_ve)
        P.wait("sync", s_ve)
        dma("sync", pfx_loc[:, :], cvec_sb[:, :])
        P.wait("gpsimd", s_io)
        hpf = s_cc.inc(1)
        def fccp(e, hpf=hpf):
            e.collective_compute("AllGather", ALU.bypass, replica_groups=RG,
                                 ins=[pfx_loc[:, :]], outs=[pfx_gth[:, :, :]]).then_inc(hpf, 1)
        P.emit("gpsimd", fccp)
        P.wait("sync", s_cc)
        dma("sync", ev_sb[:, 0, 0:64].rearrange("p (r c) -> p r c", r=4), pfx_gth[:, :, :].rearrange("r p c -> p r c"))
        P.wait("vector", s_io)
        gv = ev_sb[:, 0, 0:64].rearrange("p (r c) -> p r c", r=4)
        P.seq("vector",
              lambda e: e.tensor_scalar(out=l1b_sb[:, :], in0=gv[:, 0, :],
                                        scalar1=pmask_sb[:, 0:1], scalar2=None, op0=ALU.mult),
              lambda e: e.scalar_tensor_tensor(out=l1b_sb[:, :], in0=gv[:, 1, :],
                                               scalar=pmask_sb[:, 1:2], in1=l1b_sb[:, :],
                                               op0=ALU.mult, op1=ALU.add),
              lambda e: e.scalar_tensor_tensor(out=l1b_sb[:, :], in0=gv[:, 2, :],
                                               scalar=pmask_sb[:, 2:3], in1=l1b_sb[:, :],
                                               op0=ALU.mult, op1=ALU.add),
              lambda e: e.scalar_tensor_tensor(out=l1b_sb[:, :], in0=gv[:, 3, :],
                                               scalar=pmask_sb[:, 3:4], in1=l1b_sb[:, :],
                                               op0=ALU.mult, op1=ALU.add))
        P.ssync("vector")
        def fo1(e):
            e.memset(vec_sb[:, 0, :], 1.0)
        P.emit("gpsimd", fo1)
        P.mark("gpsimd", s_ge)
        P.wait("vector", s_ge)
        g_marks = []
        ev_flat = ev_sb.ap().rearrange("p a t -> p (a t)")
        for c in range(NC):
            P.seq("vector",
                  lambda e, c=c: e.tensor_tensor_scan(out=ev_flat, data0=vec_sb[:, 0, :],
                                                      data1=at_v[:, c, :],
                                                      initial=l1b_sb[:, c:c + 1],
                                                      op0=ALU.mult, op1=ALU.add),
                  lambda e, c=c: e.tensor_mul(out=vec_sb[:, 1, :], in0=q_v[:, c, :], in1=ev_flat),
                  lambda e, c=c: e.tensor_add(out=x_sb[:, c, :], in0=x_sb[:, c, :],
                                              in1=vec_sb[:, 1, :]))
            g_marks.append(P.mark("vector", s_ve))
            P.wait("vector", s_ve)
        return g_marks

    def l1_ffn():
        xn_ok = s_ve.n
        i = wctr[0]; buf = i % 3; wctr[0] += 1
        if i >= 3:
            P.wait("gpsimd", s_mm, slab_mm_done[i - 3])
        h = s_wdma.inc(16)
        def fwr(e, buf=buf, h=h):
            e.dma_start(out=wsl[:, buf, :, 0:8],
                        in_=w_rtr[0, :, :].rearrange("p (k e) -> p k e", k=16)).then_inc(h, 16)
        P.emit("gpsimd", fwr)
        P.wait("tensor", s_wdma)
        P.wait("tensor", s_ve, xn_ok)
        def frm(e, buf=buf):
            for hf in range(2):
                for kc in range(NC):
                    e.matmul(nrm_psv[0:8, hf * 512:(hf + 1) * 512], wsl[:, buf, kc, 0:8],
                             xn_v[:, kc, hf * 512:(hf + 1) * 512],
                             start=(kc == 0), stop=(kc == NC - 1))
        P.emit("tensor", frm)
        P.mark("tensor", s_mm)
        slab_mm_done.append(s_mm.n)
        P.wait("vector", s_mm)
        def fre(e):
            e.tensor_copy(out=stg_sb[0:8, :, :].rearrange("p a b -> p (a b)")[:, 0:1024],
                          in_=nrm_psv[0:8, :])
        P.emit("vector", fre)
        P.mark("vector", s_ve)
        rlog = stg_sb[0:8, :, :].rearrange("p a b -> p (a b)")
        P.wait("tensor", s_ve)
        for tt in range(8):
            par = tt % 2
            def ftr(e, tt=tt, par=par):
                e.matmul(ps_c[:, par, 0:8], rlog[:, tt * 128:(tt + 1) * 128],
                         ident_b[0:8, 0:8], start=True, stop=True)
            P.emit("tensor", ftr)
            P.mark("tensor", s_mm)
            P.wait("vector", s_mm)
            P.seq("vector",
                  lambda e, par=par: e.tensor_copy(out=gt_sb[:, 0:8], in_=ps_c[:, par, 0:8]),
                  lambda e: e.reduce_max(out=gt_sb[:, 8:9], in_=gt_sb[:, 0:8], axis=AXX,
                                         negate=True))
            P.ssync("vector")
            P.wait("scalar", s_ve, s_ve.n + 1)
            P.mark("vector", s_ve)
            def fex(e):
                e.activation(out=gt_sb[:, 0:8], in_=gt_sb[:, 0:8], func=ACT.Exp,
                             bias=gt_sb[:, 8:9], scale=1.0)
            P.emit("scalar", fex)
            P.mark("scalar", s_ac)
            P.wait("vector", s_ac)
            P.seq("vector",
                  lambda e: e.reduce_sum(out=gt_sb[:, 9:10], in_=gt_sb[:, 0:8], axis=AXX),
                  lambda e: e.reciprocal(out=gt_sb[:, 10:11], in_=gt_sb[:, 9:10]),
                  lambda e: e.tensor_scalar(out=gt_sb[:, 0:8], in0=gt_sb[:, 0:8],
                                            scalar1=gt_sb[:, 10:11], scalar2=None, op0=ALU.mult),
                  lambda e: e.max(mx8_sb[:, 0:8], gt_sb[:, 0:8]),
                  lambda e: e.tensor_add(out=gt_sb[:, 11:12], in0=mx8_sb[:, 0:1],
                                         in1=mx8_sb[:, 1:2]),
                  lambda e: e.reciprocal(out=gt_sb[:, 12:13], in_=gt_sb[:, 11:12]),
                  lambda e: e.tensor_single_scalar(out=gt_sb[:, 13:14], in_=mx8_sb[:, 1:2],
                                                   scalar=0.0, op=ALU.bypass),
                  lambda e, tt=tt: e.tensor_single_scalar(out=msk_sb[:, tt, 0:8],
                                                          in_=gt_sb[:, 0:8],
                                                          scalar=mx8_sb[:, 1:2], op=ALU.is_ge),
                  lambda e, tt=tt: e.tensor_mul(out=msk_sb[:, tt, 0:8], in0=msk_sb[:, tt, 0:8],
                                                in1=gt_sb[:, 0:8]),
                  lambda e, tt=tt: e.tensor_scalar(out=msk_sb[:, tt, 0:8],
                                                   in0=msk_sb[:, tt, 0:8],
                                                   scalar1=gt_sb[:, 12:13], scalar2=None,
                                                   op0=ALU.mult))
            P.ssync("vector")
            P.mark("vector", s_ve)
            P.wait("tensor", s_ve)
        w8 = stg_sb[0:8, :, :].rearrange("p a b -> p (a b)")[:, 1024:2048]
        for tt in range(8):
            par = tt % 2
            def ftw(e, tt=tt, par=par):
                e.matmul(ps_c[:, 2 + par, 0:8].rearrange("p a -> p a") if False else
                         ps_c[0:8, 2 + par, 0:128], msk_sb[:, tt, 0:8], ident_f[:, :],
                         start=True, stop=True)
            P.emit("tensor", ftw)
            P.mark("tensor", s_mm)
            P.wait("vector", s_mm)
            def fwe(e, tt=tt, par=par):
                e.tensor_copy(out=w8[0:8, tt * 128:(tt + 1) * 128], in_=ps_c[0:8, 2 + par, 0:128])
            P.emit("vector", fwe)
            P.mark("vector", s_ve)
            P.wait("tensor", s_ve)
        w8_done = s_ve.n
        h_marks = []
        def hp_v(j):
            return qa_ar.ap()[:, j * 1024:(j + 1) * 1024]
        for ee in range(NE):
            P.ssync("vector")
            P.wait("tensor", s_ve)
            def fwb(e, ee=ee):
                for hf in range(2):
                    e.matmul(ps_c[:, 2 + hf, :], hr8_sb[:, ee, :], w8[:, hf * 512:(hf + 1) * 512],
                             start=True, stop=True)
            P.emit("tensor", fwb)
            P.mark("tensor", s_mm)
            P.wait("vector", s_mm)
            def fwbe(e):
                e.tensor_copy(out=wb1_sb[:, 0:512], in_=ps_c[:, 2, :])
                e.tensor_copy(out=wb1_sb[:, 512:1024], in_=ps_c[:, 3, :])
            P.emit("vector", fwbe)
            P.mark("vector", s_ve)
            ga_done = s_ac.n
            gu_done = s_ve.n
            for j in range(8):
                for which, col0, pst in ((0, j * 128, ps_a), (1, DI + j * 128, ps_b)):
                    i = wctr[0]; buf = i % 3; wctr[0] += 1
                    if i >= 3:
                        P.wait("gpsimd", s_mm, slab_mm_done[i - 3])
                    h = s_wdma.inc(16)
                    def fw(e, ee=ee, col0=col0, buf=buf, h=h):
                        sl = ee * 16 + col0 // 128
                        e.dma_start(out=wsl[:, buf, :, :],
                                    in_=w_gu[sl, :, :].rearrange("p (k e) -> p k e", k=16)
                                    ).then_inc(h, 16)
                    P.emit("gpsimd", fw)
                    P.wait("tensor", s_wdma)
                    if which == 0:
                        P.wait("tensor", s_ac, ga_done)
                    else:
                        P.wait("tensor", s_ve, gu_done)
                    def fmm(e, buf=buf, pst=pst):
                        for hf in range(2):
                            for kc in range(NC):
                                e.matmul(pst[:, hf, :], wsl[:, buf, kc, :],
                                         xn_v[:, kc, hf * 512:(hf + 1) * 512],
                                         start=(kc == 0), stop=(kc == NC - 1))
                    P.emit("tensor", fmm)
                    P.mark("tensor", s_mm)
                    slab_mm_done.append(s_mm.n)
                P.wait("scalar", s_mm)
                for hf in range(2):
                    def fsl(e, hf=hf):
                        e.activation(out=sg1_sb[:, hf, :], in_=ps_a[:, hf, :], func=ACT.Silu)
                    P.emit("scalar", fsl)
                ga_done = P.mark("scalar", s_ac)
                P.wait("vector", s_ac)
                for hf in range(2):
                    P.seq("vector",
                          lambda e, hf=hf: e.tensor_mul(out=tm1_sb[:, hf, :], in0=ps_b[:, hf, :],
                                                        in1=wb1_sb[:, hf * 512:(hf + 1) * 512]),
                          lambda e, j=j, hf=hf: e.tensor_mul(
                              out=hp_v(j)[:, hf * 512:(hf + 1) * 512],
                              in0=tm1_sb[:, hf, :], in1=sg1_sb[:, hf, :]))
                gu_done = P.mark("vector", s_ve)
            for oc in range(NC):
                i = wctr[0]; buf = i % 3; wctr[0] += 1
                if i >= 3:
                    P.wait("gpsimd", s_mm, slab_mm_done[i - 3])
                h = s_wdma.inc(16)
                def fw(e, ee=ee, oc=oc, buf=buf, h=h):
                    sl = ee * 16 + oc
                    e.dma_start(out=wsl[:, buf, 0:8, :],
                                in_=w_dn[sl, :, :].rearrange("p (k e) -> p k e", k=8)
                                ).then_inc(h, 16)
                P.emit("gpsimd", fw)
                P.wait("tensor", s_wdma)
                if oc == 0:
                    dn_done = s_ve.n if ee == 0 else dn_done
                    P.wait("tensor", s_ac, ga_done)
                P.wait("tensor", s_ve, dn_done)
                def fmm(e, buf=buf):
                    for hf in range(2):
                        for kc in range(8):
                            e.matmul(ps_a[:, hf, :], wsl[:, buf, kc, :],
                                     hp_v(kc)[:, hf * 512:(hf + 1) * 512],
                                     start=(kc == 0), stop=(kc == 7))
                P.emit("tensor", fmm)
                P.mark("tensor", s_mm)
                slab_mm_done.append(s_mm.n)
                P.wait("vector", s_mm)
                for hf in range(2):
                    def fev(e, oc=oc, hf=hf):
                        e.scalar_tensor_tensor(out=x_sb[:, oc, hf * 512:(hf + 1) * 512],
                                               in0=ps_a[:, hf, :], scalar=1.0,
                                               in1=x_sb[:, oc, hf * 512:(hf + 1) * 512],
                                               op0=ALU.mult, op1=ALU.add)
                    P.emit("vector", fev)
                dn_done = P.mark("vector", s_ve)
                if ee == NE - 1:
                    h_marks.append(dn_done)

        return h_marks

    def store_out():
        P.wait("sync", s_ve)
        dma("sync", y_out[:, :, :], x_sb[:, :, :], sem=s_st)
        P.wait("sync", s_st)

    load_consts()
    marks = None
    if N_BLOCKS >= 1:
        layernorm()
        xn_ready = s_ve.n
        cv = l0_qkv(xn_ready)
        l0_attn(cv)
        marks = l0_out()
    if N_BLOCKS >= 2:
        layernorm(marks)
        marks = l0_ffn()
    if N_BLOCKS >= 3:
        layernorm(marks)
        marks = l1_attn()
    if N_BLOCKS >= 4:
        layernorm(marks)
        marks = l1_ffn()
    store_out()

    with nc.Block() as block:
        @block.sync
        def _(e):
            er = EngRec(e)
            for f in P.streams["sync"]:
                f(er)
        @block.gpsimd
        def _(e):
            er = EngRec(e)
            for f in P.streams["gpsimd"]:
                f(er)
        @block.vector
        def _(e):
            er = EngRec(e)
            for f in P.streams["vector"]:
                f(er)
        @block.scalar
        def _(e):
            er = EngRec(e)
            for f in P.streams["scalar"]:
                f(er)
        @block.tensor
        def _(e):
            er = EngRec(e)
            for f in P.streams["tensor"]:
                f(er)
    return nc


def _slabify(wT):
    Dk, E = wT.shape
    nk = Dk // 128
    return np.ascontiguousarray(
        wT.reshape(nk, 128, E // 128, 128).transpose(2, 1, 0, 3).reshape(E // 128, 128, nk * 128))


def _slabify_narrow(wT):
    Dk, E = wT.shape
    nk = Dk // 128
    return np.ascontiguousarray(
        wT.reshape(nk, 128, E).transpose(1, 0, 2).reshape(1, 128, nk * E))


def _host_inputs(inputs):
    bf = ml_dtypes.bfloat16
    x = np.asarray(inputs["L0_attn_x"], dtype=np.float32)
    shared = {
        "w_l0q": _slabify(np.asarray(inputs["L0_attn_w_q"], np.float32).T),
        "w_l0k": _slabify(np.asarray(inputs["L0_attn_w_k"], np.float32).T),
        "w_l0v": _slabify(np.asarray(inputs["L0_attn_w_v"], np.float32).T),
        "w_l0o": _slabify(np.asarray(inputs["L0_attn_w_o"], np.float32).T),
        "w_l0g": _slabify_narrow(np.asarray(inputs["L0_ffn_W_gate"], np.float32).T),
        "w_l0qkv": _slabify(np.asarray(inputs["L0_ffn_W_qkv"], np.float32).T),
        "w_l1q": _slabify(np.asarray(inputs["L1_attn_Wq"], np.float32).T),
        "w_l1k": _slabify(np.asarray(inputs["L1_attn_Wk"], np.float32).T),
        "w_l1v": _slabify(np.asarray(inputs["L1_attn_Wv"], np.float32).T),
        "w_l1g": _slabify(np.asarray(inputs["L1_attn_Wg"], np.float32).T),
        "w_rtr": _slabify_narrow(np.asarray(inputs["L1_ffn_router_gate"], np.float32).T),
    }
    wdn = np.asarray(inputs["L1_ffn_expert_down"], np.float32)
    shared["w_dn"] = np.ascontiguousarray(
        np.concatenate([_slabify(wdn[e]) for e in range(8)], axis=0))
    wgu = np.asarray(inputs["L1_ffn_expert_gate_up"], np.float32)
    wgu = np.concatenate([wgu[:, :, 0::2], wgu[:, :, 1::2]], axis=2)
    shared["w_gu"] = np.ascontiguousarray(
        np.concatenate([_slabify(wgu[e]) for e in range(8)], axis=0))
    bl1 = np.stack([np.asarray(inputs["L1_attn_b" + k], np.float32).reshape(16, 128).T
                    for k in ("q", "k", "v", "g")], axis=1)
    shared["b_l1"] = np.ascontiguousarray(bl1)
    shared["ident_bf"] = np.eye(128, dtype=bf)
    shared["ident_f"] = np.eye(128, dtype=np.float32)
    hc = np.zeros((128, NH, NH), dtype=bf)
    for h in range(NH):
        hc[:, h, h] = 1.0
    shared["hotcols"] = hc
    shared["iota16"] = np.broadcast_to(np.arange(16, dtype=np.float32)[None, :], (128, 16)).copy()
    hr8 = np.zeros((8, 8, 128), dtype=bf)
    for e8 in range(8):
        hr8[e8, e8, :] = 1.0
    shared["hotrows8"] = hr8

    p = np.arange(128, dtype=np.uint32)
    in_maps = []
    for c in range(NCORE):
        g, r = c // 4, c % 4
        xs = np.ascontiguousarray(
            x[g, r * T:(r + 1) * T, :].T.reshape(NC, 128, T).transpose(1, 0, 2))
        gsel = np.zeros((128, 16), dtype=np.uint32)
        for s_ in range(4):
            for b_ in range(4):
                gsel[:, s_ * 4 + b_] = b_ * 2048 + (4 * r + s_) * 128 + p
        ysel = np.zeros((128, 16), dtype=np.uint32)
        for cc in range(16):
            ysel[:, cc] = (cc // 4) * 2048 + r * 512 + (cc % 4) * 128 + p
        pmask = np.zeros((128, 4), dtype=np.float32)
        pmask[:, :r] = 1.0
        m = dict(shared)
        m.update({"x0": xs, "gsel": gsel, "ysel": ysel, "pmask": pmask})
        in_maps.append(m)
    return in_maps


def kernel(**inputs):
    in_maps = _host_inputs(inputs)
    nc = build_program()
    res = run_bass_kernel_spmd(nc, in_maps, core_ids=list(range(NCORE)))
    out = np.empty((2, S, D), dtype=np.float32)
    for c in range(NCORE):
        g, r = c // 4, c % 4
        yc = np.asarray(res.results[c]["y"])
        out[g, r * T:(r + 1) * T, :] = yc.transpose(2, 1, 0).reshape(T, D)
    return out
